# version 03
# baseline (speedup 1.0000x reference)
import contextlib

import numpy as np
import ml_dtypes
import concourse.bass as bass
import concourse.mybir as mybir
from concourse.bass import IndirectOffsetOnAxis
from concourse.bass_utils import run_bass_kernel_spmd

F32 = mybir.dt.float32
BF16 = mybir.dt.bfloat16
I32 = mybir.dt.int32
AF = mybir.ActivationFunctionType
ALU = mybir.AluOpType
AX = mybir.AxisListType

TOK = 4096
D = 1024
H = 16
DH = 64
NT = TOK // 128
E = 32
TOPK = 4
NGRP = 8
LG = 4
CAP = 5120
EPS_LN = 1e-5
EPS_AT = 1e-6
BASE_C = 640
OV_GRAN = 256
OV_N = 28
NSLOT = E * BASE_C + OV_N * OV_GRAN
SINK = NSLOT
NROW = NSLOT + 128
OV_BASE = E * BASE_C
BIG = 1e30
RLN = 1.0 / D



def ring_sems(ctx, nc, name, n):
    return [ctx.enter_context(nc.semaphore(f"{name}{i}")) for i in range(n)]


def ring_inc(sems, idx, per=16):
    return sems[idx % len(sems)], per


def ring_wait(eng, sems, idx, per=16, mult=1):
    if idx < 0:
        return
    eng.wait_ge(sems[idx % len(sems)], per * mult * (idx // len(sems) + 1))


class SelfSync:

    def __init__(self, ctx, nc, name):
        self.sem = ctx.enter_context(nc.semaphore(name))
        self.n = 0

    def __call__(self, eng):
        self.n += 1
        nopf = getattr(eng, "engine_nop", None) or eng.nop
        nopf().then_inc(self.sem, 1)
        eng.wait_ge(self.sem, self.n)

    def after(self, eng, ins):
        self.n += 1
        ins.then_inc(self.sem, 1)
        eng.wait_ge(self.sem, self.n)



def build_l1():
    nc = bass.Bass("TRN2", target_bir_lowering=False,
                   detect_race_conditions=False)
    x0 = nc.dram_tensor("x0", [TOK, D], F32, kind="ExternalInput").ap()
    w_in = [nc.dram_tensor(nm, [D, D], F32, kind="ExternalInput").ap()
            for nm in ("wq", "wk", "wv")]
    ones_bf = nc.dram_tensor("c_ones_bf", [128, 1], BF16, kind="ExternalInput").ap()

    qf_o = nc.dram_tensor("qf", [TOK, D], BF16, kind="ExternalOutput").ap()
    kvp_o = nc.dram_tensor("kvp", [64, D], F32, kind="ExternalOutput").ap()
    ksp_o = nc.dram_tensor("ksp", [1, D], F32, kind="ExternalOutput").ap()

    xn_d = nc.dram_tensor("xn_d", [TOK, D], BF16, kind="Internal").ap()
    w_bf_d = [nc.dram_tensor(f"wbf_d{j}", [D, D], BF16, kind="Internal").ap()
              for j in range(3)]

    ctx = contextlib.ExitStack()
    with ctx:
        xin = ctx.enter_context(nc.sbuf_tensor("xin", [128, 2 * D], F32))
        xm = ctx.enter_context(nc.sbuf_tensor("xm", [128, D], F32))
        sq = ctx.enter_context(nc.sbuf_tensor("sq", [128, D], F32))
        stat = ctx.enter_context(nc.sbuf_tensor("stat", [128, 8], F32))
        xnbf = ctx.enter_context(nc.sbuf_tensor("xnbf", [128, 2 * D], BF16))
        wtile = ctx.enter_context(nc.sbuf_tensor("wtile", [128, 2 * D], BF16))
        wqt = ctx.enter_context(nc.sbuf_tensor("wqt", [128, 8192], BF16))
        wkt = ctx.enter_context(nc.sbuf_tensor("wkt", [128, 8192], BF16))
        wvt = ctx.enter_context(nc.sbuf_tensor("wvt", [128, 8192], BF16))
        xnfm = ctx.enter_context(nc.sbuf_tensor("xnfm", [128, 2 * 4096], BF16))
        qt = ctx.enter_context(nc.sbuf_tensor("qt", [128, 2 * D], BF16))
        kt = ctx.enter_context(nc.sbuf_tensor("kt", [128, 2 * D], BF16))
        vt = ctx.enter_context(nc.sbuf_tensor("vt", [128, 2 * D], BF16))
        ev1 = ctx.enter_context(nc.sbuf_tensor("ev1", [128, 512], F32))
        ev2 = ctx.enter_context(nc.sbuf_tensor("ev2", [128, 512], F32))
        ev3 = ctx.enter_context(nc.sbuf_tensor("ev3", [128, 512], F32))
        onesb = ctx.enter_context(nc.sbuf_tensor("onesb", [128, 1], BF16))
        kvs = ctx.enter_context(nc.sbuf_tensor("kvs", [64, D], F32))
        kss = ctx.enter_context(nc.sbuf_tensor("kss", [1, D], F32))
        pp0 = ctx.enter_context(nc.psum_tensor("pp0", [128, 512], F32))
        pp1 = ctx.enter_context(nc.psum_tensor("pp1", [128, 512], F32))
        kvp0 = ctx.enter_context(nc.psum_tensor("kvps0", [64, D], F32))
        kvp1 = ctx.enter_context(nc.psum_tensor("kvps1", [64, D], F32))
        ksp = ctx.enter_context(nc.psum_tensor("ksps", [1, D], F32))
        s_ld = ring_sems(ctx, nc, "s_ld", 2)
        s_cst = ring_sems(ctx, nc, "s_cst", 2)
        s_wst = ring_sems(ctx, nc, "s_wst", 2)
        s_one = ctx.enter_context(nc.semaphore("s_one"))
        s_wt = ctx.enter_context(nc.semaphore("s_wt"))
        s_xn = ring_sems(ctx, nc, "s_xn", 2)
        s_fm = ring_sems(ctx, nc, "s_fm", 2)
        s_ln = ctx.enter_context(nc.semaphore("s_ln"))
        s_sc = ctx.enter_context(nc.semaphore("s_sc"))
        s_pe = ctx.enter_context(nc.semaphore("s_pe"))
        s_ev = ctx.enter_context(nc.semaphore("s_ev"))
        s_ex = ctx.enter_context(nc.semaphore("s_ex"))
        s_qk = ctx.enter_context(nc.semaphore("s_qk"))
        s_kv = ctx.enter_context(nc.semaphore("s_kv"))
        s_kve = ctx.enter_context(nc.semaphore("s_kve"))
        s_kse = ctx.enter_context(nc.semaphore("s_kse"))
        s_qf = ring_sems(ctx, nc, "s_qf", 2)
        s_fin = ctx.enter_context(nc.semaphore("s_fin"))
        s_out = ctx.enter_context(nc.semaphore("s_out"))
        vsync = SelfSync(ctx, nc, "vsync")
        asns = SelfSync(ctx, nc, "asns")
        block = ctx.enter_context(nc.Block())
        wts = [wqt, wkt, wvt]
        NG = NT // 4

        @block.gpsimd
        def _(g):
            g.dma_start(out=onesb[:, :], in_=ones_bf).then_inc(s_one, 16)
            n = 0
            for j in range(3):
                for r in range(8):
                    ring_wait(g, s_wst, n - 2)
                    sm, per = ring_inc(s_cst, n)
                    g.dma_start(
                        out=wtile[:, (n % 2) * D:(n % 2 + 1) * D],
                        in_=w_in[j][r * 128:(r + 1) * 128, :],
                    ).then_inc(sm, per)
                    n += 1
            for i in range(NT):
                g.wait_ge(s_qk, i * 6 + 2)
                sm, per = ring_inc(s_qf, i)
                g.dma_start(
                    out=qf_o[i * 128:(i + 1) * 128, :],
                    in_=qt[:, (i % 2) * D:(i % 2 + 1) * D],
                ).then_inc(sm, per)
            g.wait_ge(s_kve, NT)
            g.wait_ge(s_kse, NT)
            g.dma_start(out=kvp_o, in_=kvs[:, :]).then_inc(s_out, 16)
            g.dma_start(out=ksp_o, in_=kss[:, :]).then_inc(s_out, 16)
            g.wait_ge(s_out, 32)
            ring_wait(g, s_qf, NT - 1)
            ring_wait(g, s_qf, NT - 2)

        @block.sync
        def _(s):
            n = 0
            for j in range(3):
                for r in range(8):
                    ring_wait(s, s_cst, n)
                    sm, per = ring_inc(s_wst, n)
                    s.dma_start(
                        out=w_bf_d[j][r * 128:(r + 1) * 128, :],
                        in_=wtile[:, (n % 2) * D:(n % 2 + 1) * D],
                    ).then_inc(sm, per)
                    n += 1
            for j in range(3):
                ring_wait(s, s_wst, 8 * (j + 1) - 1)
                ring_wait(s, s_wst, 8 * (j + 1) - 2)
                for k in range(8):
                    for h in range(2):
                        s.dma_start(
                            out=wts[j][:, k * 1024 + h * 512:k * 1024 + h * 512 + 512],
                            in_=w_bf_d[j][h * 512:(h + 1) * 512, k * 128:(k + 1) * 128],
                            transpose=True,
                        ).then_inc(s_wt, 16)
            for i in range(NT):
                if i >= 2:
                    s.wait_ge(s_ln, 3 * (i - 2) + 1)
                sm, per = ring_inc(s_ld, i)
                s.dma_start(
                    out=xin[:, (i % 2) * D:(i % 2 + 1) * D],
                    in_=x0[i * 128:(i + 1) * 128, :],
                ).then_inc(sm, per)
                s.wait_ge(s_ln, 3 * (i + 1))
                sm, per = ring_inc(s_xn, i)
                s.dma_start(
                    out=xn_d[i * 128:(i + 1) * 128, :],
                    in_=xnbf[:, (i % 2) * D:(i % 2 + 1) * D],
                ).then_inc(sm, per)
                if i % 4 == 3:
                    gi = i // 4
                    ring_wait(s, s_xn, i)
                    ring_wait(s, s_xn, i - 1)
                    if gi >= 2:
                        s.wait_ge(s_pe, 6 * 4 * (gi - 1))
                    sm, per = ring_inc(s_fm, gi)
                    for c in range(8):
                        s.dma_start(
                            out=xnfm[:, (gi % 2) * 4096 + c * 512:
                                     (gi % 2) * 4096 + c * 512 + 512],
                            in_=xn_d[gi * 512:(gi + 1) * 512, c * 128:(c + 1) * 128],
                            transpose=True,
                        ).then_inc(sm, per)

        @block.vector
        def _(v):
            v.memset(stat[:, 5:6], EPS_LN)
            v.memset(kvs[:, :], 0.0)
            v.memset(kss[:, :], 0.0)
            for gi in range(NT // 4):
                for ti in range(4):
                    i = gi * 4 + ti
                    xi = xin[:, (i % 2) * D:(i % 2 + 1) * D]
                    ring_wait(v, s_ld, i)
                    ins = v.reduce_sum(stat[:, 0:1], xi, axis=AX.X)
                    vsync.after(v, ins)
                    ins = v.tensor_scalar(stat[:, 1:2], stat[:, 0:1], -RLN,
                                          None, ALU.mult)
                    vsync.after(v, ins)
                    v.tensor_scalar(xm[:, :], xi, stat[:, 1:2], None,
                                    ALU.add).then_inc(s_ln, 1)
                    v.wait_ge(s_sc, 2 * i + 1)
                    v.reduce_sum(stat[:, 2:3], sq[:, :],
                                 axis=AX.X).then_inc(s_ln, 1)
                    v.wait_ge(s_sc, 2 * i + 2)
                    ring_wait(v, s_xn, i - 2)
                    v.tensor_scalar(xnbf[:, (i % 2) * D:(i % 2 + 1) * D],
                                    xm[:, :], stat[:, 4:5], None,
                                    ALU.mult).then_inc(s_ln, 1)
                for ti in range(4):
                    i = gi * 4 + ti
                    for pg in range(6):
                        npg = i * 6 + pg
                        ps = (pp0, pp1)[npg % 2]
                        v.wait_ge(s_pe, npg + 1)
                        hh = pg % 2
                        dofs = (i % 2) * D + hh * 512
                        if pg < 4:
                            if pg < 2:
                                ring_wait(v, s_qf, i - 2)
                            elif i >= 2:
                                v.wait_ge(s_kv, i - 1)
                            dst = qt if pg < 2 else kt
                            v.tensor_scalar_min(ev1[:, :], ps[:, :], 0.0)
                            v.tensor_scalar_max(ev2[:, :], ps[:, :],
                                                0.0).then_inc(s_ev, 1)
                            v.wait_ge(s_ex, i * 4 + pg + 1)
                            v.tensor_tensor(dst[:, dofs:dofs + 512],
                                            ev3[:, :], ev2[:, :],
                                            ALU.add).then_inc(s_qk, 1)
                        else:
                            if i >= 2:
                                v.wait_ge(s_kv, i - 1)
                            v.tensor_copy(vt[:, dofs:dofs + 512],
                                          ps[:, :]).then_inc(s_qk, 1)
                    v.wait_ge(s_kv, i + 1)
                    kvpi = (kvp0, kvp1)[i % 2]
                    v.tensor_tensor(kvs[:, :], kvs[:, :], kvpi[:, :],
                                    ALU.add).then_inc(s_kve, 1)
                    v.tensor_tensor(kss[:, :], kss[:, :], ksp[:, :],
                                    ALU.add).then_inc(s_kse, 1)

        @block.scalar
        def _(a):
            for gi in range(NT // 4):
                for ti in range(4):
                    i = gi * 4 + ti
                    a.wait_ge(s_ln, 3 * i + 1)
                    a.activation(sq[:, :], xm[:, :], AF.Square).then_inc(s_sc, 1)
                    a.wait_ge(s_ln, 3 * i + 2)
                    ins = a.activation(stat[:, 3:4], stat[:, 2:3], AF.Ln,
                                       bias=stat[:, 5:6], scale=RLN)
                    asns.after(a, ins)
                    a.activation(stat[:, 4:5], stat[:, 3:4], AF.Exp,
                                 scale=-0.5).then_inc(s_sc, 1)
                for ti in range(4):
                    i = gi * 4 + ti
                    for eg in range(4):
                        a.wait_ge(s_ev, i * 4 + eg + 1)
                        a.activation(ev3[:, :], ev1[:, :],
                                     AF.Exp).then_inc(s_ex, 1)

        @block.tensor
        def _(t):
            t.wait_ge(s_wt, 16 * 48)
            for gi in range(NG):
                ring_wait(t, s_fm, gi, mult=8)
                for ti in range(4):
                    i = gi * 4 + ti
                    mofs = (gi % 2) * 4096
                    for pg in range(6):
                        npg = i * 6 + pg
                        ps = (pp0, pp1)[npg % 2]
                        j = pg // 2
                        hh = pg % 2
                        if npg >= 2:
                            t.wait_ge(s_qk, npg - 1)
                        for c in range(8):
                            ins = t.matmul(
                                ps[:, :],
                                xnfm[:, mofs + c * 512 + ti * 128:
                                     mofs + c * 512 + ti * 128 + 128],
                                wts[j][:, c * 1024 + hh * 512:
                                       c * 1024 + hh * 512 + 512],
                                start=(c == 0), stop=(c == 7),
                            )
                        ins.then_inc(s_pe, 1)
                    t.wait_ge(s_qk, (i + 1) * 6)
                    if i >= 2:
                        t.wait_ge(s_kve, i - 1)
                    kvpi = (kvp0, kvp1)[i % 2]
                    for h in range(H):
                        t.matmul(
                            kvpi[:, h * 64:(h + 1) * 64],
                            kt[:, (i % 2) * D + h * 64:(i % 2) * D + h * 64 + 64],
                            vt[:, (i % 2) * D + h * 64:(i % 2) * D + h * 64 + 64],
                            start=True, stop=True,
                        )
                    if i >= 1:
                        t.wait_ge(s_kse, i)
                    for hh in range(2):
                        ins = t.matmul(
                            ksp[:, hh * 512:(hh + 1) * 512],
                            onesb[:, :],
                            kt[:, (i % 2) * D + hh * 512:(i % 2) * D + hh * 512 + 512],
                            start=True, stop=True,
                        )
                    ins.then_inc(s_kv, 1)
    return nc



def build_l2():
    nc = bass.Bass("TRN2", target_bir_lowering=False,
                   detect_race_conditions=False)
    x0 = nc.dram_tensor("x0", [TOK, D], F32, kind="ExternalInput").ap()
    qf_i = nc.dram_tensor("qf", [TOK, D], BF16, kind="ExternalInput").ap()
    kv4 = nc.dram_tensor("kv4", [64, 4 * D], F32, kind="ExternalInput").ap()
    ks4 = nc.dram_tensor("ks4", [4, D], F32, kind="ExternalInput").ap()
    wo = nc.dram_tensor("wo", [D, D], F32, kind="ExternalInput").ap()
    bo_bc_i = nc.dram_tensor("bo_bc", [128, D], F32, kind="ExternalInput").ap()
    wg = nc.dram_tensor("wg", [E, D], F32, kind="ExternalInput").ap()
    wcat = nc.dram_tensor("wcat", [3 * E * D // 8, D], F32,
                          kind="ExternalInput").ap()
    c_id = nc.dram_tensor("c_id", [128, 128], BF16, kind="ExternalInput").ap()
    c_us = nc.dram_tensor("c_us", [128, 128], F32, kind="ExternalInput").ap()
    c_io32 = nc.dram_tensor("c_io32", [128, E], F32, kind="ExternalInput").ap()
    c_on128 = nc.dram_tensor("c_on128", [128, 1], F32, kind="ExternalInput").ap()
    c_onr = nc.dram_tensor("c_onr", [1, 128], F32, kind="ExternalInput").ap()
    c_on4 = nc.dram_tensor("c_on4", [4, 128], F32, kind="ExternalInput").ap()

    x1_o = nc.dram_tensor("x1", [TOK, D], F32, kind="ExternalOutput").ap()
    xf_o = nc.dram_tensor("xf", [TOK, D], BF16, kind="ExternalOutput").ap()
    ro_o = nc.dram_tensor("rout", [TOK, 12], F32, kind="ExternalOutput").ap()
    ct_o = nc.dram_tensor("counts", [1, E], F32, kind="ExternalOutput").ap()
    wbfp_o = nc.dram_tensor("wbfp", [3 * E * D // 8, D], BF16,
                            kind="ExternalOutput").ap()

    wo_bf_d = nc.dram_tensor("wo_bf_d", [D, D], BF16, kind="Internal").ap()
    wg_bf_d = nc.dram_tensor("wg_bf_d", [E, D], BF16, kind="Internal").ap()

    ctx = contextlib.ExitStack()
    with ctx:
        sb = lambda n, s, d: ctx.enter_context(nc.sbuf_tensor(n, s, d))
        pt = lambda n, s, d: ctx.enter_context(nc.psum_tensor(n, s, d))
        xin = sb("xin", [128, 2 * D], F32)
        qin = sb("qin", [128, 2 * D], BF16)
        prodb = sb("prodb", [128, D], F32)
        den = sb("den", [128, 2 * H], F32)
        qfs = sb("qfs", [128, D], BF16)
        qfsfme = sb("qfsfme", [64, D], BF16)
        qfsfmo = sb("qfsfmo", [64, D], BF16)
        mrg = sb("mrg", [128, D], BF16)
        mrgfm = sb("mrgfm", [128, D], BF16)
        x1t = sb("x1t", [128, 2 * D], F32)
        xm2 = sb("xm2", [128, D], F32)
        sq2 = sb("sq2", [128, D], F32)
        stat = sb("stat", [128, 8], F32)
        xfb = sb("xfb", [128, 2 * D], BF16)
        xffm = sb("xffm", [128, D], BF16)
        lg = sb("lg", [128, E], F32)
        sc_t = sb("sc_t", [128, E], F32)
        gs = sb("gs", [128, NGRP], F32)
        kg = sb("kg", [128, NGRP], F32)
        kg32 = sb("kg32", [128, E], F32)
        msk = sb("msk", [128, E], F32)
        selJ = sb("selJ", [128, 4 * E], F32)
        tmp32 = sb("tmp32", [128, E], F32)
        wcol = sb("wcol", [128, 4], F32)
        rsm = sb("rsm", [128, 4], F32)
        g8 = sb("g8", [128, 8], F32)
        m8 = sb("m8", [128, 8], F32)
        eidx = sb("eidx", [128, 8], mybir.dt.uint32)
        eidf = sb("eidf", [128, 8], F32)
        post = sb("post", [128, E], F32)
        routt = sb("routt", [128, 2 * 12], F32)
        runr = sb("runr", [1, E], F32)
        wot = sb("wot", [128, 8192], BF16)
        wgt = sb("wgt", [128, 8 * E], BF16)
        wtile = sb("wtile", [128, 2 * D], BF16)
        idsb = sb("idsb", [128, 128], BF16)
        ussb = sb("ussb", [128, 128], F32)
        iosb = sb("iosb", [128, E], F32)
        on1 = sb("on1", [128, 1], F32)
        onr = sb("onr", [1, 128], F32)
        on4 = sb("on4", [4, 128], F32)
        bobc = sb("bobc", [128, D], F32)
        ksbc = sb("ksbc", [128, D], F32)
        ks4s = sb("ks4s", [4, D], F32)
        kv4s = sb("kv4s", [64, 4 * D], F32)
        kvsum = sb("kvsum", [64, D], F32)
        kvte = sb("kvte", [64, 512], BF16)
        kvto = sb("kvto", [64, 512], BF16)
        tp0 = pt("tp0", [128, 128], BF16)
        tp1 = pt("tp1", [128, 128], BF16)
        mg = pt("mg", [128, D], F32)
        op0 = pt("op0", [128, 512], F32)
        op1 = pt("op1", [128, 512], F32)
        smallp = pt("smallp", [128, 128], F32)
        s_ld = ring_sems(ctx, nc, "s_ld", 2)
        s_qi = ring_sems(ctx, nc, "s_qi", 2)
        s_cst = ring_sems(ctx, nc, "s_cst", 2)
        s_wst = ring_sems(ctx, nc, "s_wst", 2)
        s_sx1 = ring_sems(ctx, nc, "s_sx1", 2)
        s_sxf = ring_sems(ctx, nc, "s_sxf", 2)
        s_sro = ring_sems(ctx, nc, "s_sro", 2)
        s_wt = ctx.enter_context(nc.semaphore("s_wt"))
        s_cs2 = ctx.enter_context(nc.semaphore("s_cs2"))
        s_su = ctx.enter_context(nc.semaphore("s_su"))
        s_sup = ctx.enter_context(nc.semaphore("s_sup"))
        s_den = ctx.enter_context(nc.semaphore("s_den"))
        s_tp = ctx.enter_context(nc.semaphore("s_tp"))
        s_tpe = ctx.enter_context(nc.semaphore("s_tpe"))
        s_qkv = ctx.enter_context(nc.semaphore("s_qkv"))
        s_mev = ctx.enter_context(nc.semaphore("s_mev"))
        s_op = ctx.enter_context(nc.semaphore("s_op"))
        s_x1 = ctx.enter_context(nc.semaphore("s_x1"))
        s_ln = ctx.enter_context(nc.semaphore("s_ln"))
        s_sc = ctx.enter_context(nc.semaphore("s_sc"))
        s_xfb = ctx.enter_context(nc.semaphore("s_xfb"))
        s_gt = ctx.enter_context(nc.semaphore("s_gt"))
        s_rt = ctx.enter_context(nc.semaphore("s_rt"))
        s_ps = ctx.enter_context(nc.semaphore("s_ps"))
        s_rr = ctx.enter_context(nc.semaphore("s_rr"))
        s_ro = ctx.enter_context(nc.semaphore("s_ro"))
        s_out = ctx.enter_context(nc.semaphore("s_out"))
        vsync = SelfSync(ctx, nc, "vsync")
        asns = SelfSync(ctx, nc, "asns")
        block = ctx.enter_context(nc.Block())

        @block.gpsimd
        def _(g):
            n = 0
            for r in range(8):
                ring_wait(g, s_wst, n - 2)
                sm, per = ring_inc(s_cst, n)
                g.dma_start(out=wtile[:, (n % 2) * D:(n % 2 + 1) * D],
                            in_=wo[r * 128:(r + 1) * 128, :]).then_inc(sm, per)
                n += 1
            ring_wait(g, s_wst, n - 2)
            sm, per = ring_inc(s_cst, n)
            g.dma_start(out=wtile[0:E, (n % 2) * D:(n % 2 + 1) * D],
                        in_=wg).then_inc(sm, per)
            n += 1
            for r in range(96):
                ring_wait(g, s_wst, n - 2)
                sm, per = ring_inc(s_cst, n)
                g.dma_start(out=wtile[:, (n % 2) * D:(n % 2 + 1) * D],
                            in_=wcat[r * 128:(r + 1) * 128, :]
                            ).then_inc(sm, per)
                n += 1

        @block.sync
        def _(s):
            s.dma_start(out=idsb[:, :], in_=c_id).then_inc(s_cs2, 16)
            s.dma_start(out=ussb[:, :], in_=c_us).then_inc(s_cs2, 16)
            s.dma_start(out=iosb[:, :], in_=c_io32).then_inc(s_cs2, 16)
            s.dma_start(out=on1[:, :], in_=c_on128).then_inc(s_cs2, 16)
            s.dma_start(out=onr[:, :], in_=c_onr).then_inc(s_cs2, 16)
            s.dma_start(out=on4[:, :], in_=c_on4).then_inc(s_cs2, 16)
            s.dma_start(out=bobc[:, :], in_=bo_bc_i).then_inc(s_cs2, 16)
            s.dma_start(out=ks4s[:, :], in_=ks4).then_inc(s_cs2, 16)
            s.dma_start(out=kv4s[:, :], in_=kv4).then_inc(s_cs2, 16)
            n = 0
            for r in range(8):
                ring_wait(s, s_cst, n)
                sm, per = ring_inc(s_wst, n)
                s.dma_start(out=wo_bf_d[r * 128:(r + 1) * 128, :],
                            in_=wtile[:, (n % 2) * D:(n % 2 + 1) * D]
                            ).then_inc(sm, per)
                n += 1
            ring_wait(s, s_cst, n)
            sm, per = ring_inc(s_wst, n)
            s.dma_start(out=wg_bf_d,
                        in_=wtile[0:E, (n % 2) * D:(n % 2 + 1) * D]
                        ).then_inc(sm, per)
            n += 1
            for r in range(96):
                ring_wait(s, s_cst, n)
                sm, per = ring_inc(s_wst, n)
                s.dma_start(out=wbfp_o[r * 128:(r + 1) * 128, :],
                            in_=wtile[:, (n % 2) * D:(n % 2 + 1) * D]
                            ).then_inc(sm, per)
                n += 1
            ring_wait(s, s_wst, n - 1)
            ring_wait(s, s_wst, n - 2)
            for k in range(8):
                for h in range(2):
                    s.dma_start(
                        out=wot[:, k * 1024 + h * 512:k * 1024 + h * 512 + 512],
                        in_=wo_bf_d[h * 512:(h + 1) * 512, k * 128:(k + 1) * 128],
                        transpose=True).then_inc(s_wt, 16)
            for k in range(8):
                s.dma_start(out=wgt[:, k * E:(k + 1) * E],
                            in_=wg_bf_d[0:E, k * 128:(k + 1) * 128],
                            transpose=True).then_inc(s_wt, 16)
            for i in range(NT):
                if i >= 2:
                    s.wait_ge(s_x1, i - 1)
                    s.wait_ge(s_den, i - 1)
                sm, per = ring_inc(s_ld, i)
                s.dma_start(out=xin[:, (i % 2) * D:(i % 2 + 1) * D],
                            in_=x0[i * 128:(i + 1) * 128, :]).then_inc(sm, per)
                sm, per = ring_inc(s_qi, i)
                s.dma_start(out=qin[:, (i % 2) * D:(i % 2 + 1) * D],
                            in_=qf_i[i * 128:(i + 1) * 128, :]).then_inc(sm, per)
                s.wait_ge(s_x1, i + 1)
                sm, per = ring_inc(s_sx1, i)
                s.dma_start(out=x1_o[i * 128:(i + 1) * 128, :],
                            in_=x1t[:, (i % 2) * D:(i % 2 + 1) * D]
                            ).then_inc(sm, per)
                s.wait_ge(s_xfb, i + 1)
                sm, per = ring_inc(s_sxf, i)
                s.dma_start(out=xf_o[i * 128:(i + 1) * 128, :],
                            in_=xfb[:, (i % 2) * D:(i % 2 + 1) * D]
                            ).then_inc(sm, per)
                s.wait_ge(s_ro, i + 1)
                sm, per = ring_inc(s_sro, i)
                s.dma_start(out=ro_o[i * 128:(i + 1) * 128, :],
                            in_=routt[:, (i % 2) * 12:(i % 2 + 1) * 12]
                            ).then_inc(sm, per)
            s.wait_ge(s_rr, NT)
            s.dma_start(out=ct_o, in_=runr[:, :]).then_inc(s_out, 16)
            s.wait_ge(s_out, 16)
            ring_wait(s, s_sx1, NT - 1)
            ring_wait(s, s_sx1, NT - 2)
            ring_wait(s, s_sxf, NT - 1)
            ring_wait(s, s_sxf, NT - 2)
            ring_wait(s, s_sro, NT - 1)
            ring_wait(s, s_sro, NT - 2)
            ring_wait(s, s_wst, 96 + 9 - 1)
            ring_wait(s, s_wst, 96 + 9 - 2)

        @block.tensor
        def _(t):
            t.wait_ge(s_cs2, 16 * 9)
            for hh in range(2):
                t.matmul((op0, op1)[hh][:, :], on4[:, :],
                         ks4s[:, hh * 512:(hh + 1) * 512],
                         start=True, stop=True).then_inc(s_sup, 1)
            t.wait_ge(s_wt, 16 * 24)
            ntp = 0
            for i in range(NT):
                t.wait_ge(s_den, i + 1)
                for c in range(8):
                    if ntp >= 2:
                        t.wait_ge(s_tpe, ntp - 1)
                    ins = t.transpose((tp0, tp1)[ntp % 2][:, :],
                                      qfs[:, c * 128:(c + 1) * 128], idsb[:, :])
                    ins.then_inc(s_tp, 1)
                    ntp += 1
                t.wait_ge(s_tpe, ntp)
                if i >= 1:
                    t.wait_ge(s_mev, i)
                for h in range(H):
                    qsrc = qfsfme if h % 2 == 0 else qfsfmo
                    ksrc = kvte if h % 2 == 0 else kvto
                    ins = t.matmul(
                        mg[:, h * 64:(h + 1) * 64],
                        qsrc[:, (h // 2) * 128:(h // 2) * 128 + 128],
                        ksrc[:, (h // 2) * 64:(h // 2) * 64 + 64],
                        start=True, stop=True)
                ins.then_inc(s_qkv, 1)
                t.wait_ge(s_mev, i + 1)
                for c in range(8):
                    if ntp >= 2:
                        t.wait_ge(s_tpe, ntp - 1)
                    ins = t.transpose((tp0, tp1)[ntp % 2][:, :],
                                      mrg[:, c * 128:(c + 1) * 128], idsb[:, :])
                    ins.then_inc(s_tp, 1)
                    ntp += 1
                t.wait_ge(s_tpe, ntp)
                if i >= 1:
                    t.wait_ge(s_x1, i)
                else:
                    t.wait_ge(s_su, 1)
                for hh in range(2):
                    for c in range(8):
                        ins = t.matmul(
                            (op0, op1)[hh][:, :],
                            mrgfm[:, c * 128:(c + 1) * 128],
                            wot[:, c * 1024 + hh * 512:c * 1024 + hh * 512 + 512],
                            start=(c == 0), stop=(c == 7))
                ins.then_inc(s_op, 1)
                t.wait_ge(s_xfb, i + 1)
                for c in range(8):
                    if ntp >= 2:
                        t.wait_ge(s_tpe, ntp - 1)
                    ins = t.transpose((tp0, tp1)[ntp % 2][:, :],
                                      xfb[:, (i % 2) * D + c * 128:
                                          (i % 2) * D + c * 128 + 128],
                                      idsb[:, :])
                    ins.then_inc(s_tp, 1)
                    ntp += 1
                t.wait_ge(s_tpe, ntp)
                if i >= 1:
                    t.wait_ge(s_rr, i)
                for c in range(8):
                    ins = t.matmul(smallp[:, 0:E],
                                   xffm[:, c * 128:(c + 1) * 128],
                                   wgt[:, c * E:(c + 1) * E],
                                   start=(c == 0), stop=(c == 7))
                ins.then_inc(s_gt, 1)
                t.wait_ge(s_rt, i + 1)
                t.wait_ge(s_rr, i)
                t.matmul(smallp[:, 32:32 + E], ussb[:, :], kg32[:, :],
                         start=True, stop=False)
                t.matmul(smallp[:, 32:32 + E], onr[:, :], runr[:, :],
                         start=False, stop=True)
                ins2 = t.matmul(smallp[0:1, 64:64 + E], on1[:, :],
                                kg32[:, :], start=True, stop=True)
                ins2.then_inc(s_ps, 1)

        @block.vector
        def _(v):
            v.memset(stat[:, 5:6], EPS_LN)
            v.memset(runr[:, :], 0.0)
            v.wait_ge(s_cs2, 16 * 9)
            v.tensor_tensor(kvsum[:, :], kv4s[:, 0:D], kv4s[:, D:2 * D], ALU.add)
            v.tensor_tensor(kvsum[:, :], kvsum[:, :], kv4s[:, 2 * D:3 * D], ALU.add)
            ins = v.tensor_tensor(kvsum[:, :], kvsum[:, :],
                                  kv4s[:, 3 * D:4 * D], ALU.add)
            vsync.after(v, ins)
            v.tensor_copy(
                kvte[:, :].rearrange("p (g d) -> p g d", d=64),
                kvsum[:, :].rearrange("p (g two d) -> p g two d", two=2, d=64)[:, :, 0, :])
            v.tensor_copy(
                kvto[:, :].rearrange("p (g d) -> p g d", d=64),
                kvsum[:, :].rearrange("p (g two d) -> p g two d", two=2, d=64)[:, :, 1, :])
            v.wait_ge(s_sup, 2)
            v.tensor_copy(ksbc[:, 0:512], op0[:, :])
            v.tensor_copy(ksbc[:, 512:1024], op1[:, :]).then_inc(s_su, 1)
            ntp = 0
            for i in range(NT):
                qi = qin[:, (i % 2) * D:(i % 2 + 1) * D]
                xi = xin[:, (i % 2) * D:(i % 2 + 1) * D]
                ring_wait(v, s_qi, i)
                ins = v.tensor_tensor(prodb[:, :], ksbc[:, :], qi, ALU.mult)
                vsync.after(v, ins)
                ins = v.reduce_sum(den[:, 0:H],
                                   prodb[:, :].rearrange("p (h d) -> p h d",
                                                         d=DH),
                                   axis=AX.X)
                vsync.after(v, ins)
                ins = v.tensor_scalar(den[:, 0:H], den[:, 0:H], EPS_AT, None,
                                      ALU.add)
                vsync.after(v, ins)
                ins = v.reciprocal(den[:, H:2 * H], den[:, 0:H])
                vsync.after(v, ins)
                for h in range(H):
                    ins = v.tensor_scalar(qfs[:, h * 64:(h + 1) * 64],
                                          qi[:, h * 64:(h + 1) * 64],
                                          den[:, H + h:H + h + 1], None,
                                          ALU.mult)
                ins.then_inc(s_den, 1)
                for c in range(8):
                    v.wait_ge(s_tp, ntp + 1)
                    tpp = (tp0, tp1)[ntp % 2]
                    v.tensor_copy(qfsfme[:, c * 128:(c + 1) * 128],
                                  tpp[0:64, :])
                    v.tensor_copy(qfsfmo[:, c * 128:(c + 1) * 128],
                                  tpp[64:128, :]).then_inc(s_tpe, 1)
                    ntp += 1
                v.wait_ge(s_qkv, i + 1)
                v.tensor_copy(mrg[:, :], mg[:, :]).then_inc(s_mev, 1)
                for c in range(8):
                    v.wait_ge(s_tp, ntp + 1)
                    v.tensor_copy(mrgfm[:, c * 128:(c + 1) * 128],
                                  (tp0, tp1)[ntp % 2][:, :]).then_inc(s_tpe, 1)
                    ntp += 1
                v.wait_ge(s_op, i + 1)
                ring_wait(v, s_ld, i)
                ring_wait(v, s_sx1, i - 2)
                xo = x1t[:, (i % 2) * D:(i % 2 + 1) * D]
                for hh in range(2):
                    ps = (op0, op1)[hh]
                    sl = slice(hh * 512, (hh + 1) * 512)
                    v.tensor_tensor(xo[:, sl], ps[:, :], bobc[:, sl], ALU.add)
                    ins = v.tensor_tensor(xo[:, sl], xo[:, sl], xi[:, sl],
                                          ALU.add)
                ins.then_inc(s_x1, 1)
                v.wait_ge(s_x1, i + 1)
                ins = v.reduce_sum(stat[:, 0:1], xo, axis=AX.X)
                vsync.after(v, ins)
                ins = v.tensor_scalar(stat[:, 1:2], stat[:, 0:1], -RLN, None,
                                      ALU.mult)
                vsync.after(v, ins)
                v.tensor_scalar(xm2[:, :], xo, stat[:, 1:2], None,
                                ALU.add).then_inc(s_ln, 1)
                v.wait_ge(s_sc, 3 * i + 1)
                v.reduce_sum(stat[:, 2:3], sq2[:, :], axis=AX.X).then_inc(s_ln, 1)
                v.wait_ge(s_sc, 3 * i + 2)
                ring_wait(v, s_sxf, i - 2)
                v.tensor_scalar(xfb[:, (i % 2) * D:(i % 2 + 1) * D],
                                xm2[:, :], stat[:, 4:5], None,
                                ALU.mult).then_inc(s_xfb, 1)
                for c in range(8):
                    v.wait_ge(s_tp, ntp + 1)
                    v.tensor_copy(xffm[:, c * 128:(c + 1) * 128],
                                  (tp0, tp1)[ntp % 2][:, :]).then_inc(s_tpe, 1)
                    ntp += 1
                v.wait_ge(s_gt, i + 1)
                ins = v.tensor_copy(lg[:, :], smallp[:, 0:E])
                vsync.after(v, ins)
                v.reduce_max(stat[:, 6:7], lg[:, :], axis=AX.X,
                             negate=True).then_inc(s_ln, 1)
                v.wait_ge(s_sc, 3 * i + 3)
                ins = v.reduce_sum(stat[:, 7:8], sc_t[:, :], axis=AX.X)
                vsync.after(v, ins)
                ins = v.reciprocal(stat[:, 6:7], stat[:, 7:8])
                vsync.after(v, ins)
                ins = v.tensor_scalar(sc_t[:, :], sc_t[:, :], stat[:, 6:7],
                                      None, ALU.mult)
                vsync.after(v, ins)
                ins = v.reduce_max(gs[:, :],
                                   sc_t[:, :].rearrange("p (g k) -> p g k",
                                                        k=4),
                                   axis=AX.X)
                vsync.after(v, ins)
                ins = v.max(g8[:, :], gs[:, :])
                vsync.after(v, ins)
                ins = v.tensor_scalar(kg[:, :], gs[:, :], g8[:, LG - 1:LG],
                                      None, ALU.is_ge)
                vsync.after(v, ins)
                for jj in range(4):
                    ins = v.tensor_copy(
                        kg32[:, :].rearrange("p (g k) -> p g k", k=4)[:, :, jj],
                        kg[:, :])
                vsync.after(v, ins)
                ins = v.tensor_scalar(msk[:, :], kg32[:, :], -1.0, None,
                                      ALU.add)
                vsync.after(v, ins)
                ins = v.scalar_tensor_tensor(msk[:, :], msk[:, :], 1e8,
                                             sc_t[:, :], ALU.mult, ALU.add)
                vsync.after(v, ins)
                ins = v.max(m8[:, :], msk[:, :])
                vsync.after(v, ins)
                ins = v.max_index(eidx[:, :], m8[:, :], msk[:, :])
                vsync.after(v, ins)
                ins = v.tensor_copy(eidf[:, :], eidx[:, :])
                vsync.after(v, ins)
                v.tensor_scalar(kg32[:, :], msk[:, :], m8[:, TOPK - 1:TOPK],
                                None, ALU.is_ge).then_inc(s_rt, 1)
                ins = v.reduce_sum(rsm[:, 2:3], m8[:, 0:TOPK], axis=AX.X)
                vsync.after(v, ins)
                ins = v.tensor_scalar(rsm[:, 2:3], rsm[:, 2:3], 1e-9, None,
                                      ALU.add)
                vsync.after(v, ins)
                ins = v.reciprocal(rsm[:, 3:4], rsm[:, 2:3])
                vsync.after(v, ins)
                ring_wait(v, s_sro, i - 2)
                v.tensor_scalar(routt[:, (i % 2) * 12 + 8:(i % 2) * 12 + 12],
                                m8[:, 0:TOPK], rsm[:, 3:4], None, ALU.mult)
                v.tensor_copy(routt[:, (i % 2) * 12:(i % 2) * 12 + TOPK],
                              eidf[:, 0:TOPK])
                for j in range(TOPK):
                    v.tensor_scalar(selJ[:, j * E:(j + 1) * E], iosb[:, :],
                                    eidf[:, j:j + 1], None, ALU.is_equal)
                v.wait_ge(s_ps, i + 1)
                ins = v.tensor_copy(post[:, :], smallp[:, 32:32 + E])
                vsync.after(v, ins)
                for j in range(TOPK):
                    ins = v.tensor_tensor(tmp32[:, :],
                                          selJ[:, j * E:(j + 1) * E],
                                          post[:, :], ALU.mult)
                    vsync.after(v, ins)
                    ins = v.reduce_sum(routt[:, (i % 2) * 12 + 4 + j:
                                             (i % 2) * 12 + 4 + j + 1],
                                       tmp32[:, :], axis=AX.X)
                    vsync.after(v, ins)
                v.engine_nop().then_inc(s_ro, 1)
                v.tensor_tensor(runr[:, :], runr[:, :], smallp[0:1, 64:64 + E],
                                ALU.add).then_inc(s_rr, 1)

        @block.scalar
        def _(a):
            for i in range(NT):
                a.wait_ge(s_ln, 3 * i + 1)
                a.activation(sq2[:, :], xm2[:, :], AF.Square).then_inc(s_sc, 1)
                a.wait_ge(s_ln, 3 * i + 2)
                a.activation(stat[:, 3:4], stat[:, 2:3], AF.Ln,
                             bias=stat[:, 5:6], scale=RLN)
                asns(a)
                a.activation(stat[:, 4:5], stat[:, 3:4], AF.Exp,
                             scale=-0.5).then_inc(s_sc, 1)
                a.wait_ge(s_ln, 3 * i + 3)
                a.activation(sc_t[:, :], lg[:, :], AF.Exp,
                             bias=stat[:, 6:7]).then_inc(s_sc, 1)
    return nc




def l3_sched():
    gl = []
    for sgrp in range(8):
        gl.append((0, sgrp * 512, 512, -1))
    for e in range(E):
        gl.append((1, e * BASE_C, 512, e))
        gl.append((1, e * BASE_C + 512, 128, e))
    for og in range(OV_N):
        gl.append((2, OV_BASE + og * OV_GRAN, OV_GRAN, E + og))
    return gl


def build_l3(second=False):
    nc = bass.Bass("TRN2", target_bir_lowering=False,
                   detect_race_conditions=False)
    x1 = nc.dram_tensor("x1", [TOK, D], F32, kind="ExternalInput").ap()
    xf_i = nc.dram_tensor("xf", [TOK, D], BF16, kind="ExternalInput").ap()
    ro_i = nc.dram_tensor("rout", [TOK, 12], F32, kind="ExternalInput").ap()
    cts = nc.dram_tensor("counts_all", [8, E], F32, kind="ExternalInput").ap()
    myct = nc.dram_tensor("mycounts", [1, E], F32, kind="ExternalInput").ap()
    pmask = nc.dram_tensor("prefmask", [8, 1], F32, kind="ExternalInput").ap()
    sw1 = nc.dram_tensor("sw1", [D, D], F32, kind="ExternalInput").ap()
    sw2 = nc.dram_tensor("sw2", [D, D], F32, kind="ExternalInput").ap()
    sw3 = nc.dram_tensor("sw3", [D, D], F32, kind="ExternalInput").ap()
    if not second:
        wat_i = nc.dram_tensor("w_attn", [D, 3 * D], F32,
                               kind="ExternalInput").ap()
    c_io32 = nc.dram_tensor("c_io32", [128, E], F32, kind="ExternalInput").ap()
    c_io28 = nc.dram_tensor("c_io28", [1, OV_N], F32, kind="ExternalInput").ap()
    c_onr = nc.dram_tensor("c_onr", [1, 128], F32, kind="ExternalInput").ap()
    c_cb = nc.dram_tensor("c_colbase", [128, 24], F32, kind="ExternalInput").ap()
    c_onb = nc.dram_tensor("c_ones_bf", [128, 1], BF16, kind="ExternalInput").ap()

    x2_o = nc.dram_tensor("x2", [TOK, D], F32, kind="ExternalOutput").ap()
    wbf_o = nc.dram_tensor("wbf", [3 * E * D, D], BF16,
                           kind="ExternalInput").ap()
    if not second:
        qb_o = nc.dram_tensor("qb", [TOK, D], BF16,
                              kind="ExternalOutput").ap()
        kb_o = nc.dram_tensor("kb", [TOK, D], BF16,
                              kind="ExternalOutput").ap()
        vb_o = nc.dram_tensor("vb", [TOK, D], BF16,
                              kind="ExternalOutput").ap()
        sp_o = nc.dram_tensor("Sp", [64, D], F32, kind="ExternalOutput").ap()
        zp_o = nc.dram_tensor("zp", [1, D], F32, kind="ExternalOutput").ap()

    _l3kind = "ExternalOutput" if globals().get("L3_DEBUG") else "Internal"
    xe_d = nc.dram_tensor("xe_d", [NROW, D], BF16, kind=_l3kind).ap()
    so_d = nc.dram_tensor("so_d", [NROW, D], F32, kind=_l3kind).ap()
    z_d = nc.dram_tensor("z_d", [TOK, D], F32, kind=_l3kind).ap()
    if not second:
        xb_d = nc.dram_tensor("xb_d", [TOK, D], BF16, kind="Internal").ap()
    tab_d = nc.dram_tensor(
        "tab_d", [1, 4 * E], F32,
        kind="ExternalOutput" if globals().get("L3_DEBUG") else "Internal"
    ).ap()
    if globals().get("L3_DEBUG"):
        dst_dbg = nc.dram_tensor("dst_dbg", [TOK, 4], I32,
                                 kind="ExternalOutput").ap()
        tbt_dbg = nc.dram_tensor("tbt_dbg", [128, 16 * NT], F32,
                                 kind="ExternalOutput").ap()

    GL = l3_sched()
    NGR = len(GL)
    HB, SB_, nh, ns = [], [], 0, 0
    for (kind, base, n, u) in GL:
        HB.append(nh)
        SB_.append(ns)
        nh += 8
        ns += n // 128
    NHC, NST = nh, ns
    unit_last_group = {}
    for gi, (kind, base, n, u) in enumerate(GL):
        if u >= 0:
            unit_last_group[u] = gi
    NUNIT = E + OV_N

    ctx = contextlib.ExitStack()
    with ctx:
        sb = lambda n_, s, d: ctx.enter_context(nc.sbuf_tensor(n_, s, d))
        pt = lambda n_, s, d: ctx.enter_context(nc.psum_tensor(n_, s, d))
        wslab = sb("wslab", [128, 49152], BF16)
        xefm = sb("xefm", [128, 2 * 4096], BF16)
        hfm = sb("hfm", [128, 4096], BF16)
        hs_t = sb("hs_t", [128, 1024], F32)
        outb = sb("outb", [128, 2 * D], F32)
        rot = sb("rot", [128, 2 * 12], F32)
        xft = sb("xft", [128, 2 * D], BF16)
        tbt = sb("tbt", [128, 16], F32)
        ei32 = sb("ei32", [128, 4], I32)
        di32 = sb("di32", [128, 8 * NT], I32)
        dsc = sb("dsc", [128, 160], F32)
        row32 = sb("row32", [1, 256], F32)
        ov28 = sb("ov28", [1, 96], F32)
        ebc = sb("ebc", [128, OV_N], F32)
        bc96 = sb("bc96", [128, 96], F32)
        widxf = sb("widxf", [128, 24], F32)
        widx = sb("widx", [128, 2 * 24], I32)
        gt = sb("gt", [128, 2 * D], F32)
        x1t = sb("x1t", [128, D], F32)
        x2t = sb("x2t", [128, 2 * D], F32)
        xm3 = sb("xm3", [128, D], F32)
        sq3 = sb("sq3", [128, D], F32)
        stat = sb("stat", [128, 8], F32)
        xbt = sb("xbt", [128, 2 * D], BF16)
        xbfm = sb("xbfm", [128, 4096], BF16)
        qbt = sb("qbt", [128, 2 * D], BF16)
        kbt = sb("kbt", [128, 2 * D], BF16)
        vbt = sb("vbt", [128, 2 * D], BF16)
        ev1 = sb("ev1", [128, 512], F32)
        ev2 = sb("ev2", [128, 512], F32)
        ev3 = sb("ev3", [128, 512], F32)
        onesb = sb("onesb", [128, 1], BF16)
        io32s = sb("io32s", [128, E], F32)
        cbs = sb("cbs", [128, 24], F32)
        onrs = sb("onrs", [1, 128], F32)
        pms = sb("pms", [8, 128], F32)
        ctss = sb("ctss", [8, E], F32)
        mycs = sb("mycs", [1, E], F32)
        io28s = sb("io28s", [1, OV_N], F32)
        spac = sb("spac", [64, D], F32)
        zpac = sb("zpac", [1, D], F32)
        h1a = pt("h1a", [128, 512], F32)
        h1b = pt("h1b", [128, 512], F32)
        h3a = pt("h3a", [128, 512], F32)
        h3b = pt("h3b", [128, 512], F32)
        outp = pt("outp", [128, D], F32)
        smallp = pt("smallp", [128, 128], F32)
        s_cs = ctx.enter_context(nc.semaphore("s_cs"))
        s_onb = ctx.enter_context(nc.semaphore("s_onb"))
        s_zro = ctx.enter_context(nc.semaphore("s_zro"))
        s_xez = ctx.enter_context(nc.semaphore("s_xez"))
        s_swl = ctx.enter_context(nc.semaphore("s_swl"))
        s_sup = ctx.enter_context(nc.semaphore("s_sup"))
        s_su = ctx.enter_context(nc.semaphore("s_su"))
        s_tab = ctx.enter_context(nc.semaphore("s_tab"))
        if globals().get("L3_DEBUG"):
            s_tbd = ctx.enter_context(nc.semaphore("s_tbd"))
        s_rol = ring_sems(ctx, nc, "s_rol", 2)
        s_xfl = ring_sems(ctx, nc, "s_xfl", 2)
        s_ei = ctx.enter_context(nc.semaphore("s_ei"))
        s_tbg = ctx.enter_context(nc.semaphore("s_tbg"))
        s_dst = ctx.enter_context(nc.semaphore("s_dst"))
        s_sct = ring_sems(ctx, nc, "s_sct", 2)
        s_hm = ctx.enter_context(nc.semaphore("s_hm"))
        s_sil = ctx.enter_context(nc.semaphore("s_sil"))
        s_hev = ctx.enter_context(nc.semaphore("s_hev"))
        s_hfb = ctx.enter_context(nc.semaphore("s_hfb"))
        s_og = ctx.enter_context(nc.semaphore("s_og"))
        s_oev = ctx.enter_context(nc.semaphore("s_oev"))
        s_sod = ring_sems(ctx, nc, "s_sod", 2)
        s_wld = ring_sems(ctx, nc, "s_wld", 2)
        s_wldh = ring_sems(ctx, nc, "s_wldh", 2)
        s_wlda = ring_sems(ctx, nc, "s_wlda", 2)
        s_wldo = ring_sems(ctx, nc, "s_wldo", 2)
        s_wbs = ring_sems(ctx, nc, "s_wbs", 2)
        s_wix = ctx.enter_context(nc.semaphore("s_wix"))
        s_xel = ring_sems(ctx, nc, "s_xel", 2)
        s_cg2 = ctx.enter_context(nc.semaphore("s_cg2"))
        s_cgd = ctx.enter_context(nc.semaphore("s_cgd"))
        s_czl = ring_sems(ctx, nc, "s_czl", 2)
        s_x2 = ctx.enter_context(nc.semaphore("s_x2"))
        s_sx2 = ring_sems(ctx, nc, "s_sx2", 2)
        s_lnb = ctx.enter_context(nc.semaphore("s_lnb"))
        s_scb = ctx.enter_context(nc.semaphore("s_scb"))
        s_xbs = ring_sems(ctx, nc, "s_xbs", 2)
        s_bfm = ring_sems(ctx, nc, "s_bfm", 2)
        s_pb = ctx.enter_context(nc.semaphore("s_pb"))
        s_evb = ctx.enter_context(nc.semaphore("s_evb"))
        s_exb = ctx.enter_context(nc.semaphore("s_exb"))
        s_qkb = ctx.enter_context(nc.semaphore("s_qkb"))
        s_kvb = ctx.enter_context(nc.semaphore("s_kvb"))
        s_kve = ctx.enter_context(nc.semaphore("s_kve"))
        s_kse = ctx.enter_context(nc.semaphore("s_kse"))
        s_qs = ring_sems(ctx, nc, "s_qs", 2)
        s_ks = ring_sems(ctx, nc, "s_ks", 2)
        s_vs = ring_sems(ctx, nc, "s_vs", 2)
        s_watl = ctx.enter_context(nc.semaphore("s_watl"))
        s_fin = ctx.enter_context(nc.semaphore("s_fin"))
        s_out = ctx.enter_context(nc.semaphore("s_out"))
        vsync = SelfSync(ctx, nc, "vsync")
        asns = SelfSync(ctx, nc, "asns")
        block = ctx.enter_context(nc.Block())

        def wv(kind, p, k):
            return wslab[:, kind * 16384 + p * 8192 + k * 1024:
                         kind * 16384 + p * 8192 + (k + 1) * 1024]

        def watv(k, hh):
            return wslab[:, k * 3072 + hh * 512:k * 3072 + (hh + 1) * 512]

        NM = 8

        @block.gpsimd
        def _(g):
            g.dma_start(out=onesb[:, :], in_=c_onb).then_inc(s_onb, 16)
            for j, w in enumerate((sw1, sw3, sw2)):
                for k in range(8):
                    g.dma_start(out=wv(j, 0, k),
                                in_=w[k * 128:(k + 1) * 128, :]
                                ).then_inc(s_swl, 16)
            g.wait_ge(s_tab, 64)
            for i in range(NT):
                g.wait_ge(s_ei, i + 1)
                for j in range(4):
                    g.indirect_dma_start(
                        out=tbt[:, j * 4:(j + 1) * 4], out_offset=None,
                        in_=tab_d.rearrange("o (a b) -> (o a) b", b=4),
                        in_offset=IndirectOffsetOnAxis(ei32[:, j:j + 1], 0),
                    ).then_inc(s_tbg, 16)
                if globals().get("L3_DEBUG"):
                    g.dma_start(out=tbt_dbg[:, i * 16:(i + 1) * 16],
                                in_=tbt[:, 0:16]).then_inc(s_tbd, 16)
                g.wait_ge(s_dst, i + 1)
                ring_wait(g, s_xfl, i)
                if i == 0:
                    g.wait_ge(s_xez, 16)
                ring_wait(g, s_sct, i - 2, mult=4)
                for j in range(4):
                    g.indirect_dma_start(
                        out=xe_d,
                        out_offset=IndirectOffsetOnAxis(
                            di32[:, i * 8 + j:i * 8 + j + 1], 0),
                        in_=xft[:, (i % 2) * D:(i % 2 + 1) * D],
                        in_offset=None,
                    ).then_inc(s_sct[i % 2], 16)
            for u in range(NUNIT):
                p = u % 2
                if u >= 2:
                    gl_ = unit_last_group[u - 2]
                    g.wait_ge(s_og, SB_[gl_] + GL[gl_][2] // 128)
                if u < 2:
                    g.wait_ge(s_og, 32)
                if u < E:
                    for k in range(8):
                        g.dma_start(
                            out=wv(2, p, k),
                            in_=wbf_o[2 * E * D + u * D + k * 128:
                                      2 * E * D + u * D + (k + 1) * 128, :],
                        ).then_inc(s_wld[u % 2], 16)
                else:
                    og = u - E
                    g.wait_ge(s_wix, og + 1)
                    for cm in range(24):
                        j, k = cm // 8, cm % 8
                        g.indirect_dma_start(
                            out=wv(j, p, k), out_offset=None,
                            in_=wbf_o,
                            in_offset=IndirectOffsetOnAxis(
                                widx[:, (og % 2) * 24 + cm:
                                     (og % 2) * 24 + cm + 1], 0),
                        ).then_inc(s_wldo[u % 2], 16)
            if not second:
                g.wait_ge(s_og, NST)
                for k in range(8):
                    for hh in range(3):
                        g.dma_start(
                            out=wslab[:, k * 3072 + hh * 1024:
                                      k * 3072 + (hh + 1) * 1024],
                            in_=wat_i[k * 128:(k + 1) * 128,
                                      hh * 1024:(hh + 1) * 1024],
                        ).then_inc(s_watl, 16)
            g.wait_ge(s_oev, NST)
            ring_wait(g, s_sod, NST - 1)
            ring_wait(g, s_sod, NST - 2)
            for i in range(NT):
                if i >= 1:
                    g.wait_ge(s_cg2, 2 * i)
                for j in range(2):
                    g.indirect_dma_start(
                        out=gt[:, j * D:(j + 1) * D], out_offset=None,
                        in_=so_d,
                        in_offset=IndirectOffsetOnAxis(
                            di32[:, i * 8 + j:i * 8 + j + 1], 0),
                    ).then_inc(s_cgd, 16)
                g.wait_ge(s_cg2, 2 * i + 1)
                for j in range(2, 4):
                    g.indirect_dma_start(
                        out=gt[:, (j - 2) * D:(j - 1) * D], out_offset=None,
                        in_=so_d,
                        in_offset=IndirectOffsetOnAxis(
                            di32[:, i * 8 + j:i * 8 + j + 1], 0),
                    ).then_inc(s_cgd, 16)
                if i % 4 == 3 and not second:
                    for jj in range(i - 3, i + 1):
                        g.wait_ge(s_qkb, jj * 6 + 2)
                        sm, per = ring_inc(s_qs, jj)
                        g.dma_start(out=qb_o[jj * 128:(jj + 1) * 128, :],
                                    in_=qbt[:, (jj % 2) * D:(jj % 2 + 1) * D]
                                    ).then_inc(sm, per)
                        g.wait_ge(s_qkb, jj * 6 + 4)
                        sm, per = ring_inc(s_ks, jj)
                        g.dma_start(out=kb_o[jj * 128:(jj + 1) * 128, :],
                                    in_=kbt[:, (jj % 2) * D:(jj % 2 + 1) * D]
                                    ).then_inc(sm, per)
                        g.wait_ge(s_qkb, jj * 6 + 6)
                        sm, per = ring_inc(s_vs, jj)
                        g.dma_start(out=vb_o[jj * 128:(jj + 1) * 128, :],
                                    in_=vbt[:, (jj % 2) * D:(jj % 2 + 1) * D]
                                    ).then_inc(sm, per)
            if globals().get("L3_DEBUG"):
                g.wait_ge(s_tbd, 16 * NT)

        @block.sync
        def _(s):
            for ap_in, dst in ((c_io32, io32s[:, :]), (c_io28, io28s[:, :]),
                               (c_onr, onrs[:, :]), (pmask, pms[:, 0:1]),
                               (cts, ctss[:, :]), (myct, mycs[:, :]),
                               (c_cb, cbs[:, :])):
                s.dma_start(out=dst, in_=ap_in).then_inc(s_cs, 16)
            s.wait_ge(s_zro, 2)
            s.dma_start(out=so_d[SINK:SINK + 128, :],
                        in_=x2t[:, 0:D]).then_inc(s_xez, 16)
            s.wait_ge(s_su, 1)
            with nc.allow_non_contiguous_dma(reason="32x4B strided table"):
                for c3 in range(4):
                    src = row32[0:1, c3 * 32:(c3 + 1) * 32]
                    s.dma_start(
                        out=tab_d.rearrange("o (a b) -> o a b", b=4)[:, :, c3],
                        in_=src).then_inc(s_tab, 16)
            for i in range(NT):
                if i % 4 == 0:
                    gi = i // 4
                    kind, base, n, u = GL[gi]
                    if gi >= 2:
                        s.wait_ge(s_og, SB_[gi - 1])
                    sm, per = ring_inc(s_xel, gi)
                    for c in range(8):
                        s.dma_start(
                            out=xefm[:, (gi % 2) * 4096 + c * n:
                                     (gi % 2) * 4096 + (c + 1) * n],
                            in_=xf_i[base:base + n, c * 128:(c + 1) * 128],
                            transpose=True).then_inc(sm, per)
                if i >= 2:
                    s.wait_ge(s_dst, i - 1)
                    ring_wait(s, s_sct, i - 2, mult=4)
                sm, per = ring_inc(s_rol, i)
                s.dma_start(out=rot[:, (i % 2) * 12:(i % 2 + 1) * 12],
                            in_=ro_i[i * 128:(i + 1) * 128, :]).then_inc(sm, per)
                sm, per = ring_inc(s_xfl, i)
                s.dma_start(out=xft[:, (i % 2) * D:(i % 2 + 1) * D],
                            in_=xf_i[i * 128:(i + 1) * 128, :]).then_inc(sm, per)
            def _wload_s(ut):
                for k in range(8):
                    s.dma_start(
                        out=wv(0, ut % 2, k),
                        in_=wbf_o[0 * E * D + ut * D + k * 128:
                                  0 * E * D + ut * D + (k + 1) * 128, :],
                    ).then_inc(s_wldh[ut % 2], 16)
            for gi, (kind, base, n, u) in enumerate(GL):
                if kind == 0:
                    continue
                if gi == 8:
                    ring_wait(s, s_sct, NT - 1, mult=4)
                    ring_wait(s, s_sct, NT - 2, mult=4)
                    s.wait_ge(s_og, 32)
                    _wload_s(0)
                    _wload_s(1)
                if gi >= 2:
                    s.wait_ge(s_og, SB_[gi - 1])
                sm, per = ring_inc(s_xel, gi)
                for c in range(8):
                    s.dma_start(
                        out=xefm[:, (gi % 2) * 4096 + c * n:
                                 (gi % 2) * 4096 + (c + 1) * n],
                        in_=xe_d[base:base + n, c * 128:(c + 1) * 128],
                        transpose=True).then_inc(sm, per)
                if kind == 1 and (gi == NGR - 1 or GL[gi + 1][3] != u) \
                        and u + 2 < E:
                    s.wait_ge(s_og, SB_[gi] + n // 128)
                    _wload_s(u + 2)
            s.wait_ge(s_oev, NST)
            ring_wait(s, s_sod, NST - 1)
            ring_wait(s, s_sod, NST - 2)
            for i in range(NT):
                if i >= 1:
                    s.wait_ge(s_x2, i)
                ring_wait(s, s_sx2, i - 2)
                sm, per = ring_inc(s_czl, i)
                s.dma_start(out=x2t[:, (i % 2) * D:(i % 2 + 1) * D],
                            in_=z_d[i * 128:(i + 1) * 128, :]).then_inc(sm, per)
                s.dma_start(out=x1t[:, :], in_=x1[i * 128:(i + 1) * 128, :]
                            ).then_inc(sm, per)
                sm, per = ring_inc(s_rol, NT + i)
                s.dma_start(out=rot[:, (i % 2) * 12:(i % 2 + 1) * 12],
                            in_=ro_i[i * 128:(i + 1) * 128, :]).then_inc(sm, per)
                s.wait_ge(s_x2, i + 1)
                sm, per = ring_inc(s_sx2, i)
                s.dma_start(out=x2_o[i * 128:(i + 1) * 128, :],
                            in_=x2t[:, (i % 2) * D:(i % 2 + 1) * D]
                            ).then_inc(sm, per)
                if not second:
                    s.wait_ge(s_lnb, 3 * (i + 1))
                    sm, per = ring_inc(s_xbs, i)
                    s.dma_start(out=xb_d[i * 128:(i + 1) * 128, :],
                                in_=xbt[:, (i % 2) * D:(i % 2 + 1) * D]
                                ).then_inc(sm, per)
                if i % 4 == 3 and not second:
                    gi4 = i // 4
                    ring_wait(s, s_xbs, i)
                    ring_wait(s, s_xbs, i - 1)
                    if gi4 >= 1:
                        s.wait_ge(s_pb, 6 * 4 * gi4)
                    sm, per = ring_inc(s_bfm, gi4)
                    for c in range(8):
                        s.dma_start(
                            out=xbfm[:, c * 512:(c + 1) * 512],
                            in_=xb_d[gi4 * 512:(gi4 + 1) * 512,
                                     c * 128:(c + 1) * 128],
                            transpose=True).then_inc(sm, per)
            if globals().get("L3_DEBUG"):
                s.wait_ge(s_dst, NT)
                for i_ in range(NT):
                    s.dma_start(out=dst_dbg[i_ * 128:(i_ + 1) * 128, :],
                                in_=di32[:, i_ * 8:i_ * 8 + 4]
                                ).then_inc(s_out, 16)
            if not second:
                s.wait_ge(s_kve, NT)
                s.wait_ge(s_kse, NT)
                s.dma_start(out=sp_o, in_=spac[:, :]).then_inc(s_out, 16)
                s.dma_start(out=zp_o, in_=zpac[:, :]).then_inc(s_out, 16)
                s.wait_ge(s_out, 32)
            for r in ((s_sx2, s_qs, s_ks, s_vs) if not second
                      else (s_sx2,)):
                ring_wait(s, r, NT - 1)
                ring_wait(s, r, NT - 2)

        @block.tensor
        def _(t):
            t.wait_ge(s_cs, 16 * 7)
            t.wait_ge(s_onb, 16)
            t.matmul(smallp[0:1, 0:E], pms[:, 0:1], ctss[:, :],
                     start=True, stop=True).then_inc(s_sup, 1)
            t.wait_ge(s_su, 1)
            t.matmul(smallp[:, 0:96], onrs[:, :], row32[0:1, 0:96],
                     start=True, stop=True).then_inc(s_sup, 1)
            t.matmul(smallp[:, 96:96 + OV_N], onrs[:, :], ov28[0:1, 0:OV_N],
                     start=True, stop=True).then_inc(s_sup, 1)
            for gi, (kind, base, n, u) in enumerate(GL):
                ring_wait(t, s_xel, gi, mult=8)
                if kind == 0:
                    t.wait_ge(s_swl, 16 * 24)
                    p = 0
                elif u < E:
                    ring_wait(t, s_wld, u, mult=8)
                    ring_wait(t, s_wldh, u, mult=8)
                    ring_wait(t, s_wlda, u, mult=8)
                    p = u % 2
                else:
                    ring_wait(t, s_wldo, u - E, mult=24)
                    p = u % 2
                for m in range(NM):
                    nglob = HB[gi] + m
                    pr = nglob % 2
                    if nglob >= 2:
                        t.wait_ge(s_hev, nglob - 1)
                    h1p = (h1a, h1b)[pr]
                    h3p = (h3a, h3b)[pr]
                    for k in range(8):
                        t.matmul(h1p[:, 0:n],
                                 wv(0, p, k)[:, m * 128:(m + 1) * 128],
                                 xefm[:, (gi % 2) * 4096 + k * n:
                                      (gi % 2) * 4096 + (k + 1) * n],
                                 start=(k == 0), stop=(k == 7))
                    for k in range(8):
                        ins = t.matmul(h3p[:, 0:n],
                                       wv(1, p, k)[:, m * 128:(m + 1) * 128],
                                       xefm[:, (gi % 2) * 4096 + k * n:
                                            (gi % 2) * 4096 + (k + 1) * n],
                                       start=(k == 0), stop=(k == 7))
                    ins.then_inc(s_hm, 1)
                t.wait_ge(s_hev, HB[gi] + 8)
                for st in range(n // 128):
                    nst = SB_[gi] + st
                    if nst >= 1:
                        t.wait_ge(s_oev, nst)
                    for hh in range(2):
                        for k in range(8):
                            ins = t.matmul(
                                outp[:, hh * 512:(hh + 1) * 512],
                                hfm[:, k * n + st * 128:k * n + st * 128 + 128],
                                wv(2, p, k)[:, hh * 512:(hh + 1) * 512],
                                start=(k == 0), stop=(k == 7))
                    ins.then_inc(s_og, 1)
            if not second:
                t.wait_ge(s_watl, 16 * 24)
                for gi4 in range(NT // 4):
                    ring_wait(t, s_bfm, gi4, mult=8)
                    for ti in range(4):
                        i = gi4 * 4 + ti
                        for pg in range(6):
                            npg = i * 6 + pg
                            ps = (h1a, h1b)[npg % 2]
                            if npg >= 2:
                                t.wait_ge(s_qkb, npg - 1)
                            for c in range(8):
                                ins = t.matmul(
                                    ps[:, :],
                                    xbfm[:, c * 512 + ti * 128:
                                         c * 512 + ti * 128 + 128],
                                    watv(c, pg),
                                    start=(c == 0), stop=(c == 7))
                            ins.then_inc(s_pb, 1)
                        t.wait_ge(s_qkb, (i + 1) * 6)
                        t.wait_ge(s_kve, i)
                        for h in range(H):
                            t.matmul(
                                outp[0:64, h * 64:(h + 1) * 64],
                                kbt[:, (i % 2) * D + h * 64:(i % 2) * D + h * 64 + 64],
                                vbt[:, (i % 2) * D + h * 64:(i % 2) * D + h * 64 + 64],
                                start=True, stop=True)
                        t.wait_ge(s_kse, i)
                        for hh in range(2):
                            zpp = (h3a, h3b)[hh]
                            ins = t.matmul(
                                zpp[0:1, 0:512], onesb[:, :],
                                kbt[:, (i % 2) * D + hh * 512:
                                    (i % 2) * D + hh * 512 + 512],
                                start=True, stop=True)
                        ins.then_inc(s_kvb, 1)

        @block.vector
        def _(v):
            v.memset(stat[:, 5:6], EPS_LN)
            if not second:
                v.memset(spac[:, :], 0.0)
                v.memset(zpac[:, :], 0.0)
            v.memset(xbt[:, 0:D], 0.0).then_inc(s_zro, 1)
            v.memset(x2t[:, 0:D], 0.0).then_inc(s_zro, 1)
            v.wait_ge(s_cs, 16 * 7)
            v.wait_ge(s_sup, 1)
            ins = v.tensor_copy(row32[0:1, 0:E], smallp[0:1, 0:E])
            vsync.after(v, ins)
            ins = v.tensor_scalar(row32[0:1, 96:128], mycs[0:1, :],
                                  -float(BASE_C), None, ALU.add)
            vsync.after(v, ins)
            ins = v.tensor_scalar(row32[0:1, 96:128], row32[0:1, 96:128], 0.0,
                                  None, ALU.max)
            vsync.after(v, ins)
            ins = v.memset(row32[0:1, 128:160], 0.0)
            vsync.after(v, ins)
            for m_ in range(OV_N):
                ins = v.tensor_scalar(row32[0:1, 192:224], row32[0:1, 96:128],
                                      float(m_ * OV_GRAN) + 0.5, None,
                                      ALU.is_ge)
                vsync.after(v, ins)
                ins = v.tensor_tensor(row32[0:1, 128:160],
                                      row32[0:1, 128:160],
                                      row32[0:1, 192:224], ALU.add)
                vsync.after(v, ins)
            ins = v.tensor_scalar(row32[0:1, 32:64], row32[0:1, 128:160],
                                  float(OV_GRAN), None, ALU.mult)
            vsync.after(v, ins)
            ins = v.tensor_copy(row32[0:1, 160:192], row32[0:1, 32:64])
            vsync.after(v, ins)
            a_, b_ = 160, 192
            for dstep in (1, 2, 4, 8, 16):
                ins = v.tensor_copy(row32[0:1, b_:b_ + dstep],
                                    row32[0:1, a_:a_ + dstep])
                vsync.after(v, ins)
                ins = v.tensor_tensor(row32[0:1, b_ + dstep:b_ + 32],
                                      row32[0:1, a_ + dstep:a_ + 32],
                                      row32[0:1, a_:a_ + 32 - dstep], ALU.add)
                vsync.after(v, ins)
                a_, b_ = b_, a_
            ins = v.memset(row32[0:1, 64:65], 0.0)
            vsync.after(v, ins)
            ins = v.tensor_copy(row32[0:1, 65:96], row32[0:1, a_:a_ + 31])
            vsync.after(v, ins)
            ins = v.tensor_scalar(row32[0:1, 224:256], row32[0:1, 64:96],
                                  1.0 / OV_GRAN, None, ALU.mult)
            vsync.after(v, ins)
            ins = v.tensor_tensor(row32[0:1, 128:160], row32[0:1, 64:96],
                                  row32[0:1, 32:64], ALU.add)
            vsync.after(v, ins)
            ins = v.tensor_scalar(row32[0:1, 128:160], row32[0:1, 128:160],
                                  1.0 / OV_GRAN, None, ALU.mult)
            vsync.after(v, ins)
            ins = v.memset(ov28[0:1, 0:OV_N], 0.0)
            vsync.after(v, ins)
            for e in range(E):
                ins = v.tensor_scalar(ov28[0:1, 32:32 + OV_N], io28s[0:1, :],
                                      row32[0:1, 224 + e:225 + e], None,
                                      ALU.is_ge)
                vsync.after(v, ins)
                ins = v.tensor_scalar(ov28[0:1, 64:64 + OV_N], io28s[0:1, :],
                                      row32[0:1, 128 + e:129 + e], None,
                                      ALU.is_lt)
                vsync.after(v, ins)
                ins = v.tensor_tensor(ov28[0:1, 32:32 + OV_N],
                                      ov28[0:1, 32:32 + OV_N],
                                      ov28[0:1, 64:64 + OV_N], ALU.mult)
                vsync.after(v, ins)
                ins = v.scalar_tensor_tensor(ov28[0:1, 0:OV_N],
                                             ov28[0:1, 32:32 + OV_N],
                                             float(e), ov28[0:1, 0:OV_N],
                                             ALU.mult, ALU.add)
                vsync.after(v, ins)
            ins = v.memset(row32[0:1, 96:128], 0.0)
            ins.then_inc(s_su, 1)
            v.wait_ge(s_sup, 3)
            v.tensor_copy(bc96[:, :], smallp[:, 0:96])
            ins = v.tensor_scalar(ebc[:, :], smallp[:, 96:96 + OV_N], 1024.0,
                                  None, ALU.mult)
            vsync.after(v, ins)
            for og in range(2):
                ins = v.tensor_scalar(widxf[:, :], cbs[:, :],
                                      ebc[:, og:og + 1], None, ALU.add)
                vsync.after(v, ins)
                ins = v.tensor_copy(widx[:, (og % 2) * 24:(og % 2 + 1) * 24],
                                    widxf[:, :])
                ins.then_inc(s_wix, 1)
            nself = 0
            for i in range(NT):
                ring_wait(v, s_rol, i)
                ri = rot[:, (i % 2) * 12:(i % 2 + 1) * 12]
                v.tensor_copy(ei32[:, :], ri[:, 0:4]).then_inc(s_ei, 1)
                v.wait_ge(s_tbg, 16 * 4 * (i + 1))
                eidc = [ri[:, j:j + 1] for j in range(4)]
                posc = [ri[:, 4 + j:5 + j] for j in range(4)]
                tbo = [tbt[:, j * 4:j * 4 + 1] for j in range(4)]
                tbg = [tbt[:, j * 4 + 1:j * 4 + 2] for j in range(4)]
                tbs = [tbt[:, j * 4 + 2:j * 4 + 3] for j in range(4)]
                A = [dsc[:, 32 + j * 32:33 + j * 32] for j in range(4)]
                B = [dsc[:, 40 + j * 32:41 + j * 32] for j in range(4)]
                C = [dsc[:, 48 + j * 32:49 + j * 32] for j in range(4)]
                Dd = [dsc[:, 56 + j * 32:57 + j * 32] for j in range(4)]
                dst = [dsc[:, j * 8:j * 8 + 1] for j in range(4)]
                J = range(4)
                for j in J:
                    v.tensor_scalar(A[j], posc[j], float(OV_BASE - BASE_C),
                                    None, ALU.add)
                for j in J:
                    v.tensor_tensor(A[j], A[j], tbs[j], ALU.add)
                for j in J:
                    v.scalar_tensor_tensor(Dd[j], eidc[j], float(BASE_C),
                                           posc[j], ALU.mult, ALU.add)
                for j in J:
                    v.tensor_tensor(C[j], posc[j], tbg[j], ALU.subtract)
                for j in J:
                    v.tensor_scalar(B[j], A[j], float(NSLOT), None, ALU.is_lt)
                for j in J:
                    v.tensor_scalar(C[j], C[j], float(BASE_C), None, ALU.is_lt)
                for j in J:
                    v.tensor_tensor(B[j], B[j], C[j], ALU.mult)
                for j in J:
                    v.tensor_scalar(A[j], A[j], -float(SINK), None, ALU.add)
                for j in J:
                    v.tensor_tensor(A[j], A[j], B[j], ALU.mult)
                for j in J:
                    v.tensor_scalar(A[j], A[j], float(SINK), None, ALU.add)
                for j in J:
                    v.tensor_scalar(C[j], posc[j], float(BASE_C), None,
                                    ALU.is_lt)
                for j in J:
                    v.tensor_tensor(B[j], posc[j], tbo[j], ALU.add)
                for j in J:
                    v.tensor_scalar(B[j], B[j], float(CAP), None, ALU.is_lt)
                for j in J:
                    v.tensor_tensor(Dd[j], Dd[j], A[j], ALU.subtract)
                for j in J:
                    v.tensor_tensor(Dd[j], Dd[j], C[j], ALU.mult)
                for j in J:
                    v.tensor_tensor(Dd[j], Dd[j], A[j], ALU.add)
                for j in J:
                    v.tensor_scalar(Dd[j], Dd[j], -float(SINK), None, ALU.add)
                for j in J:
                    v.tensor_tensor(Dd[j], Dd[j], B[j], ALU.mult)
                for j in J:
                    v.tensor_scalar(Dd[j], Dd[j], float(SINK), None, ALU.add)
                for j in J:
                    v.tensor_scalar(Dd[j], Dd[j], float(SINK), None, ALU.min)
                for j in J:
                    ins = v.tensor_scalar(dst[j], Dd[j], 0.0, None, ALU.max)
                vsync.after(v, ins)
                ins = v.tensor_copy(
                    di32[:, i * 8:i * 8 + 4],
                    dsc[:, 0:32].rearrange("p (a b) -> p a b", b=8)[:, :, 0])
                ins.then_inc(s_dst, 1)
                if i % 4 == 3:
                    sg = i // 4
                    for m in range(NM):
                        nglob = sg * 8 + m
                        v.wait_ge(s_sil, nglob + 1)
                        ins = v.tensor_tensor(
                            hfm[:, m * 512:(m + 1) * 512],
                            hs_t[:, (nglob % 2) * 512:(nglob % 2 + 1) * 512],
                            ((h3a, h3b)[nglob % 2])[:, 0:512],
                            ALU.mult)
                        ins.then_inc(s_hev, 1)
                    for st in range(4):
                        nst = sg * 4 + st
                        v.wait_ge(s_og, nst + 1)
                        ring_wait(v, s_sod, nst - 2)
                        ins = v.tensor_copy(
                            outb[:, (nst % 2) * D:(nst % 2 + 1) * D],
                            outp[:, :])
                        ins.then_inc(s_oev, 1)
            for gi, (kind, base, n, u) in enumerate(GL):
                if kind == 0:
                    continue
                if kind == 2 and (u - E) + 2 < OV_N:
                    og = (u - E) + 2
                    ring_wait(v, s_wldo, og - 2, mult=24)
                    ins = v.tensor_scalar(widxf[:, :], cbs[:, :],
                                          ebc[:, og:og + 1], None, ALU.add)
                    vsync.after(v, ins)
                    ins = v.tensor_copy(
                        widx[:, (og % 2) * 24:(og % 2 + 1) * 24], widxf[:, :])
                    ins.then_inc(s_wix, 1)
                for m in range(NM):
                    nglob = HB[gi] + m
                    v.wait_ge(s_sil, nglob + 1)
                    ins = v.tensor_tensor(
                        hfm[:, m * n:(m + 1) * n],
                        hs_t[:, (nglob % 2) * 512:(nglob % 2) * 512 + n],
                        ((h3a, h3b)[nglob % 2])[:, 0:n],
                        ALU.mult)
                    ins.then_inc(s_hev, 1)
                for st in range(n // 128):
                    nst = SB_[gi] + st
                    v.wait_ge(s_og, nst + 1)
                    ring_wait(v, s_sod, nst - 2)
                    ins = v.tensor_copy(
                        outb[:, (nst % 2) * D:(nst % 2 + 1) * D], outp[:, :])
                    ins.then_inc(s_oev, 1)
            for i in range(NT):
                v.wait_ge(s_cgd, 16 * (4 * i + 2))
                ring_wait(v, s_czl, i, mult=2)
                ring_wait(v, s_rol, NT + i)
                ri = rot[:, (i % 2) * 12:(i % 2 + 1) * 12]
                xo = x2t[:, (i % 2) * D:(i % 2 + 1) * D]
                v.scalar_tensor_tensor(xo, gt[:, 0:D], ri[:, 8:9],
                                       xo, ALU.mult, ALU.add)
                ins = v.scalar_tensor_tensor(
                    xo, gt[:, D:2 * D], ri[:, 9:10], xo,
                    ALU.mult, ALU.add)
                ins.then_inc(s_cg2, 1)
                v.wait_ge(s_cgd, 16 * (4 * i + 4))
                for j in range(2, 4):
                    ins = v.scalar_tensor_tensor(
                        xo, gt[:, (j - 2) * D:(j - 1) * D],
                        ri[:, 8 + j:9 + j], xo,
                        ALU.mult, ALU.add)
                ins.then_inc(s_cg2, 1)
                v.tensor_tensor(xo, xo, x1t[:, :], ALU.add).then_inc(s_x2, 1)
                if not second:
                    v.wait_ge(s_x2, i + 1)
                    ins = v.reduce_sum(stat[:, 0:1], xo, axis=AX.X)
                    vsync.after(v, ins)
                    ins = v.tensor_scalar(stat[:, 1:2], stat[:, 0:1], -RLN, None,
                                          ALU.mult)
                    vsync.after(v, ins)
                    v.tensor_scalar(xm3[:, :], xo, stat[:, 1:2], None,
                                    ALU.add).then_inc(s_lnb, 1)
                    v.wait_ge(s_scb, 2 * i + 1)
                    v.reduce_sum(stat[:, 2:3], sq3[:, :], axis=AX.X).then_inc(s_lnb, 1)
                    v.wait_ge(s_scb, 2 * i + 2)
                    ring_wait(v, s_xbs, i - 2)
                    v.tensor_scalar(xbt[:, (i % 2) * D:(i % 2 + 1) * D],
                                    xm3[:, :], stat[:, 4:5], None,
                                    ALU.mult).then_inc(s_lnb, 1)
                    if i % 4 == 3:
                        for tj in range(4):
                            jj = (i // 4) * 4 + tj
                            for pg in range(6):
                                npg = jj * 6 + pg
                                ps = (h1a, h1b)[npg % 2]
                                v.wait_ge(s_pb, npg + 1)
                                hh = pg % 2
                                dofs = (jj % 2) * D + hh * 512
                                if pg < 4:
                                    if pg < 2:
                                        ring_wait(v, s_qs, jj - 2)
                                    else:
                                        ring_wait(v, s_ks, jj - 2)
                                    dst = qbt if pg < 2 else kbt
                                    v.tensor_scalar_min(ev1[:, :], ps[:, :], 0.0)
                                    v.tensor_scalar_max(ev2[:, :], ps[:, :],
                                                        0.0).then_inc(s_evb, 1)
                                    v.wait_ge(s_exb, jj * 4 + pg + 1)
                                    v.tensor_tensor(dst[:, dofs:dofs + 512],
                                                    ev3[:, :], ev2[:, :],
                                                    ALU.add).then_inc(s_qkb, 1)
                                else:
                                    ring_wait(v, s_vs, jj - 2)
                                    v.tensor_copy(vbt[:, dofs:dofs + 512],
                                                  ps[:, :]).then_inc(s_qkb, 1)
                            v.wait_ge(s_kvb, jj + 1)
                            v.tensor_tensor(spac[:, :], spac[:, :],
                                            outp[0:64, :], ALU.add).then_inc(s_kve, 1)
                            v.tensor_tensor(zpac[:, 0:512], zpac[:, 0:512],
                                            h3a[0:1, 0:512], ALU.add)
                            v.tensor_tensor(zpac[:, 512:1024], zpac[:, 512:1024],
                                            h3b[0:1, 0:512],
                                            ALU.add).then_inc(s_kse, 1)

        @block.scalar
        def _(a):
            def _wload_a(ut):
                for k in range(8):
                    a.dma_start(
                        out=wv(1, ut % 2, k),
                        in_=wbf_o[1 * E * D + ut * D + k * 128:
                                  1 * E * D + ut * D + (k + 1) * 128, :],
                    ).then_inc(s_wlda[ut % 2], 16)
            for gi, (kind, base, n, u) in enumerate(GL):
                if gi >= 1:
                    pk, pb_, pn, pu = GL[gi - 1]
                    for st in range(pn // 128):
                        nst = SB_[gi - 1] + st
                        a.wait_ge(s_oev, nst + 1)
                        dst_d = z_d if pk == 0 else so_d
                        dofs = pb_ + st * 128
                        sm, per = ring_inc(s_sod, nst)
                        a.dma_start(
                            out=dst_d[dofs:dofs + 128, :],
                            in_=outb[:, (nst % 2) * D:(nst % 2 + 1) * D]
                            ).then_inc(sm, per)
                if gi == 8:
                    a.wait_ge(s_og, 32)
                    _wload_a(0)
                    _wload_a(1)
                if gi >= 9 and GL[gi - 1][0] == 1 and GL[gi - 1][3] != u \
                        and GL[gi - 1][3] + 2 < E:
                    pk, pb_, pn, pu = GL[gi - 1]
                    a.wait_ge(s_og, SB_[gi - 1] + pn // 128)
                    _wload_a(pu + 2)
                for m in range(NM):
                    nglob = HB[gi] + m
                    a.wait_ge(s_hm, nglob + 1)
                    if nglob >= 2:
                        a.wait_ge(s_hev, nglob - 1)
                    a.activation(
                        hs_t[:, (nglob % 2) * 512:(nglob % 2) * 512 + n],
                        ((h1a, h1b)[nglob % 2])[:, 0:n],
                        AF.Silu).then_inc(s_sil, 1)
            gi = NGR - 1
            pk, pb_, pn, pu = GL[gi]
            for st in range(pn // 128):
                nst = SB_[gi] + st
                a.wait_ge(s_oev, nst + 1)
                sm, per = ring_inc(s_sod, nst)
                a.dma_start(out=so_d[pb_ + st * 128:pb_ + st * 128 + 128, :],
                            in_=outb[:, (nst % 2) * D:(nst % 2 + 1) * D]
                            ).then_inc(sm, per)
            if not second:
                for i in range(NT):
                    a.wait_ge(s_lnb, 3 * i + 1)
                    a.activation(sq3[:, :], xm3[:, :], AF.Square).then_inc(s_scb, 1)
                    a.wait_ge(s_lnb, 3 * i + 2)
                    ins = a.activation(stat[:, 3:4], stat[:, 2:3], AF.Ln,
                                       bias=stat[:, 5:6], scale=RLN)
                    asns.after(a, ins)
                    a.activation(stat[:, 4:5], stat[:, 3:4], AF.Exp,
                                 scale=-0.5).then_inc(s_scb, 1)
                    if i % 4 == 3:
                        for tj in range(4):
                            jj = (i // 4) * 4 + tj
                            for eg in range(4):
                                a.wait_ge(s_evb, jj * 4 + eg + 1)
                                a.activation(ev3[:, :], ev1[:, :],
                                             AF.Exp).then_inc(s_exb, 1)
    return nc




def build_l4():
    nc = bass.Bass("TRN2", target_bir_lowering=False,
                   detect_race_conditions=False)
    x2_i = nc.dram_tensor("x2", [TOK, D], F32, kind="ExternalInput").ap()
    qb_i = nc.dram_tensor("qb", [TOK, D], BF16, kind="ExternalInput").ap()
    kb_i = nc.dram_tensor("kb", [TOK, D], BF16, kind="ExternalInput").ap()
    vb_i = nc.dram_tensor("vb", [TOK, D], BF16, kind="ExternalInput").ap()
    sp3_i = nc.dram_tensor("sp3", [64, 3 * D], F32, kind="ExternalInput").ap()
    zp3_i = nc.dram_tensor("zp3", [1, 3 * D], F32, kind="ExternalInput").ap()
    wp_i = nc.dram_tensor("wp", [D, D], F32, kind="ExternalInput").ap()
    wg = nc.dram_tensor("wg", [E, D], F32, kind="ExternalInput").ap()
    c_id = nc.dram_tensor("c_id", [128, 128], BF16, kind="ExternalInput").ap()
    c_tru = nc.dram_tensor("c_tru", [128, 128], F32, kind="ExternalInput").ap()
    c_trub = nc.dram_tensor("c_trub", [128, 128], BF16,
                            kind="ExternalInput").ap()
    c_onrb = nc.dram_tensor("c_onrb", [1, 128], BF16,
                            kind="ExternalInput").ap()
    c_us = nc.dram_tensor("c_us", [128, 128], F32, kind="ExternalInput").ap()
    c_io32 = nc.dram_tensor("c_io32", [128, E], F32, kind="ExternalInput").ap()
    c_on128 = nc.dram_tensor("c_on128", [128, 1], F32,
                             kind="ExternalInput").ap()
    c_onr = nc.dram_tensor("c_onr", [1, 128], F32, kind="ExternalInput").ap()
    c_onb = nc.dram_tensor("c_onb", [128, 1], BF16, kind="ExternalInput").ap()

    x3_o = nc.dram_tensor("x3", [TOK, D], F32, kind="ExternalOutput").ap()
    xf2_o = nc.dram_tensor("xf2", [TOK, D], BF16, kind="ExternalOutput").ap()
    ro2_o = nc.dram_tensor("rout2", [TOK, 12], F32, kind="ExternalOutput").ap()
    ct2_o = nc.dram_tensor("counts2", [1, E], F32,
                           kind="ExternalOutput").ap()

    wg_bf_d = nc.dram_tensor("wg_bf_d4", [E, D], BF16, kind="Internal").ap()

    ctx = contextlib.ExitStack()
    with ctx:
        sb = lambda n, s, d: ctx.enter_context(nc.sbuf_tensor(n, s, d))
        pt = lambda n, s, d: ctx.enter_context(nc.psum_tensor(n, s, d))
        qbt = sb("qbt", [128, 2 * D], BF16)
        kbt = sb("kbt", [128, 2 * D], BF16)
        vbt = sb("vbt", [128, 2 * D], BF16)
        qte = sb("qte", [64, 8 * 128], BF16)
        qto = sb("qto", [64, 8 * 128], BF16)
        kte = sb("kte", [64, 8 * 128], BF16)
        kto = sb("kto", [64, 8 * 128], BF16)
        x2t = sb("x2t", [128, 2 * D], F32)
        wpsb = sb("wpsb", [128, 8 * D], BF16)
        wgt = sb("wgt", [128, 8 * E], BF16)
        wtile = sb("wtile", [128, D], BF16)
        ssb = sb("ssb", [64, D], F32)
        sbf = sb("sbf", [64, D], BF16)
        zrow = sb("zrow", [1, D], F32)
        zrbf = sb("zrbf", [1, D], BF16)
        sp3s = sb("sp3s", [64, 3 * D], F32)
        zp3s = sb("zp3s", [1, 3 * D], F32)
        atm = sb("atm", [128, 16 * 128], BF16)
        prodb = sb("prodb", [128, D], F32)
        dent = sb("dent", [128, 32], F32)
        ysb = sb("ysb", [128, D], BF16)
        ytsb = sb("ytsb", [128, D], BF16)
        x3t = sb("x3t", [128, 2 * D], F32)
        xm2 = sb("xm2", [128, D], F32)
        sq2 = sb("sq2", [128, D], F32)
        stat = sb("stat", [128, 8], F32)
        xfb = sb("xfb", [128, 2 * D], BF16)
        xffm = sb("xffm", [128, D], BF16)
        lg = sb("lg", [128, E], F32)
        sc_t = sb("sc_t", [128, E], F32)
        gs = sb("gs", [128, NGRP], F32)
        kg = sb("kg", [128, NGRP], F32)
        kg32 = sb("kg32", [128, E], F32)
        msk = sb("msk", [128, E], F32)
        selJ = sb("selJ", [128, 4 * E], F32)
        tmp32 = sb("tmp32", [128, E], F32)
        rsm = sb("rsm", [128, 4], F32)
        g8 = sb("g8", [128, 8], F32)
        m8 = sb("m8", [128, 8], F32)
        eidx = sb("eidx", [128, 8], mybir.dt.uint32)
        eidf = sb("eidf", [128, 8], F32)
        post = sb("post", [128, E], F32)
        routt = sb("routt", [128, 2 * 12], F32)
        runr = sb("runr", [1, E], F32)
        idsb = sb("idsb", [128, 128], BF16)
        trusb = sb("trusb", [128, 128], F32)
        trubb = sb("trubb", [128, 128], BF16)
        onrb = sb("onrb", [1, 128], BF16)
        ussb = sb("ussb", [128, 128], F32)
        iosb = sb("iosb", [128, E], F32)
        on1 = sb("on1", [128, 1], F32)
        onr = sb("onr", [1, 128], F32)
        onb = sb("onb", [128, 1], BF16)
        pb1 = pt("pb1", [128, 512], F32)
        pb2 = pt("pb2", [128, 512], F32)
        pk1 = pt("pk1", [128, 512], F32)
        pk2 = pt("pk2", [128, 512], F32)
        pb4 = pt("pb4", [128, 128], F32)
        pb5 = pt("pb5", [128, 128], F32)
        tp0 = pt("tp0", [128, 128], BF16)
        tp1 = pt("tp1", [128, 128], BF16)
        smallp = pk1[:, 0:128]

        s_cs = ctx.enter_context(nc.semaphore("s_cs"))
        s_wp = ctx.enter_context(nc.semaphore("s_wp"))
        s_cst = ring_sems(ctx, nc, "s_cst", 2)
        s_wst = ctx.enter_context(nc.semaphore("s_wst"))
        s_wt = ctx.enter_context(nc.semaphore("s_wt"))
        s_qb = ring_sems(ctx, nc, "s_qb", 2)
        s_kb = ring_sems(ctx, nc, "s_kb", 2)
        s_vb = ring_sems(ctx, nc, "s_vb", 2)
        s_x2 = ring_sems(ctx, nc, "s_x2", 2)
        s_at = ctx.enter_context(nc.semaphore("s_at"))
        s_ate = ctx.enter_context(nc.semaphore("s_ate"))
        s_kc = ctx.enter_context(nc.semaphore("s_kc"))
        s_dche = ctx.enter_context(nc.semaphore("s_dche"))
        s_zu = ctx.enter_context(nc.semaphore("s_zu"))
        s_zue = ctx.enter_context(nc.semaphore("s_zue"))
        s_num = ctx.enter_context(nc.semaphore("s_num"))
        s_ye = ctx.enter_context(nc.semaphore("s_ye"))
        s_su1 = ctx.enter_context(nc.semaphore("s_su1"))
        s_se = ctx.enter_context(nc.semaphore("s_se"))
        s_sbf = ctx.enter_context(nc.semaphore("s_sbf"))
        s_tp = ctx.enter_context(nc.semaphore("s_tp"))
        s_tpe = ctx.enter_context(nc.semaphore("s_tpe"))
        s_proj = ctx.enter_context(nc.semaphore("s_proj"))
        s_x3 = ctx.enter_context(nc.semaphore("s_x3"))
        s_x3e = ctx.enter_context(nc.semaphore("s_x3e"))
        s_ln = ctx.enter_context(nc.semaphore("s_ln"))
        s_sc = ctx.enter_context(nc.semaphore("s_sc"))
        s_xfb = ctx.enter_context(nc.semaphore("s_xfb"))
        s_gt = ctx.enter_context(nc.semaphore("s_gt"))
        s_rt = ctx.enter_context(nc.semaphore("s_rt"))
        s_ps = ctx.enter_context(nc.semaphore("s_ps"))
        s_rr = ctx.enter_context(nc.semaphore("s_rr"))
        s_ro = ctx.enter_context(nc.semaphore("s_ro"))
        s_sx3 = ring_sems(ctx, nc, "s_sx3", 2)
        s_sxf = ring_sems(ctx, nc, "s_sxf", 2)
        s_sro = ring_sems(ctx, nc, "s_sro", 2)
        s_out = ctx.enter_context(nc.semaphore("s_out"))
        vsync = SelfSync(ctx, nc, "vsync")
        asns = SelfSync(ctx, nc, "asns")
        block = ctx.enter_context(nc.Block())

        @block.gpsimd
        def _(g):
            for r in range(8):
                g.dma_start(out=wpsb[:, r * D:(r + 1) * D],
                            in_=wp_i[r * 128:(r + 1) * 128, :]
                            ).then_inc(s_wp, 16)
            g.dma_start(out=wtile[0:E, 0:D], in_=wg).then_inc(s_cst[0], 16)

        @block.sync
        def _(s):
            for dst, src in ((idsb[:, :], c_id), (trubb[:, :], c_trub),
                             (onrb[:, :], c_onrb),
                             (trusb[:, :], c_tru), (ussb[:, :], c_us),
                             (iosb[:, :], c_io32), (on1[:, :], c_on128),
                             (onr[:, :], c_onr), (onb[:, :], c_onb),
                             (sp3s[:, :], sp3_i), (zp3s[:, :], zp3_i)):
                s.dma_start(out=dst, in_=src).then_inc(s_cs, 16)
            ring_wait(s, s_cst, 0)
            s.dma_start(out=wg_bf_d, in_=wtile[0:E, 0:D]).then_inc(s_wst, 16)
            s.wait_ge(s_wst, 16)
            for k in range(8):
                s.dma_start(out=wgt[:, k * E:(k + 1) * E],
                            in_=wg_bf_d[0:E, k * 128:(k + 1) * 128],
                            transpose=True).then_inc(s_wt, 16)
            for i in range(NT):
                if i >= 2:
                    s.wait_ge(s_dche, i - 1)
                    s.wait_ge(s_num, 16 * (i - 1))
                    s.wait_ge(s_zu, i - 1)
                    s.wait_ge(s_x3, i - 1)
                sm, per = ring_inc(s_qb, i)
                s.dma_start(out=qbt[:, (i % 2) * D:(i % 2 + 1) * D],
                            in_=qb_i[i * 128:(i + 1) * 128, :]
                            ).then_inc(sm, per)
                sm, per = ring_inc(s_kb, i)
                s.dma_start(out=kbt[:, (i % 2) * D:(i % 2 + 1) * D],
                            in_=kb_i[i * 128:(i + 1) * 128, :]
                            ).then_inc(sm, per)
                sm, per = ring_inc(s_vb, i)
                s.dma_start(out=vbt[:, (i % 2) * D:(i % 2 + 1) * D],
                            in_=vb_i[i * 128:(i + 1) * 128, :]
                            ).then_inc(sm, per)
                sm, per = ring_inc(s_x2, i)
                s.dma_start(out=x2t[:, (i % 2) * D:(i % 2 + 1) * D],
                            in_=x2_i[i * 128:(i + 1) * 128, :]
                            ).then_inc(sm, per)
                s.wait_ge(s_x3, i + 1)
                sm, per = ring_inc(s_sx3, i)
                s.dma_start(out=x3_o[i * 128:(i + 1) * 128, :],
                            in_=x3t[:, (i % 2) * D:(i % 2 + 1) * D]
                            ).then_inc(sm, per)
                s.wait_ge(s_xfb, i + 1)
                sm, per = ring_inc(s_sxf, i)
                s.dma_start(out=xf2_o[i * 128:(i + 1) * 128, :],
                            in_=xfb[:, (i % 2) * D:(i % 2 + 1) * D]
                            ).then_inc(sm, per)
                s.wait_ge(s_ro, i + 1)
                sm, per = ring_inc(s_sro, i)
                s.dma_start(out=ro2_o[i * 128:(i + 1) * 128, :],
                            in_=routt[:, (i % 2) * 12:(i % 2 + 1) * 12]
                            ).then_inc(sm, per)
            s.wait_ge(s_rr, NT)
            s.dma_start(out=ct2_o, in_=runr[:, :]).then_inc(s_out, 16)
            s.wait_ge(s_out, 16)
            ring_wait(s, s_sx3, NT - 1)
            ring_wait(s, s_sx3, NT - 2)
            ring_wait(s, s_sxf, NT - 1)
            ring_wait(s, s_sxf, NT - 2)
            ring_wait(s, s_sro, NT - 1)
            ring_wait(s, s_sro, NT - 2)

        @block.tensor
        def _(t):
            t.wait_ge(s_cs, 16 * 11)
            t.wait_ge(s_wp, 16 * 8)
            t.wait_ge(s_wt, 16 * 8)
            ntp = 0
            for i in range(NT):
                ring_wait(t, s_qb, i)
                ring_wait(t, s_kb, i)
                ring_wait(t, s_vb, i)
                qh = lambda h: (qte, qto)[h % 2][:, (h // 2) * 128:
                                                 (h // 2) * 128 + 128]
                kh = lambda h: (kte, kto)[h % 2][:, (h // 2) * 128:
                                                 (h // 2) * 128 + 128]
                kn = lambda h: kbt[:, (i % 2) * D + h * 64:
                                   (i % 2) * D + (h + 1) * 64]
                vn = lambda h: vbt[:, (i % 2) * D + h * 64:
                                   (i % 2) * D + (h + 1) * 64]
                for b in range(16):
                    src = (qbt if b < 8 else kbt)[
                        :, (i % 2) * D + (b % 8) * 128:
                        (i % 2) * D + (b % 8) * 128 + 128]
                    if ntp >= 2:
                        t.wait_ge(s_tpe, ntp - 1)
                    ins = t.transpose((tp0, tp1)[ntp % 2][:, :], src,
                                      idsb[:, :])
                    ins.then_inc(s_tp, 1)
                    ntp += 1
                t.wait_ge(s_tpe, ntp)
                if i >= 1:
                    t.wait_ge(s_zue, i)
                    t.wait_ge(s_rr, i)
                t.wait_ge(s_sbf, i + 1)
                for hh in range(2):
                    pk = (pk1, pk2)[hh]
                    sl = slice((i % 2) * D + hh * 512,
                               (i % 2) * D + hh * 512 + 512)
                    t.matmul(pk[:, :], trubb[:, :], kbt[:, sl],
                             start=True, stop=False)
                    ins = t.matmul(pk[:, :], onrb[:, :],
                                   zrbf[:, hh * 512:hh * 512 + 512],
                                   start=False, stop=True)
                ins.then_inc(s_kc, 1)
                if i >= 1:
                    t.wait_ge(s_x3e, i)
                for h in range(H):
                    if 16 * i + h >= 2:
                        t.wait_ge(s_ate, 16 * i + h - 1)
                    ins = t.matmul((pb1, pb2)[h % 2][:, 0:128],
                                   kh(h), qh(h), start=True, stop=True)
                    ins.then_inc(s_at, 1)
                t.wait_ge(s_ate, 16 * (i + 1))
                for h in range(H):
                    if 16 * i + h >= 2:
                        t.wait_ge(s_ye, 16 * i + h - 1)
                        t.wait_ge(s_se, 16 * i + h - 1)
                    pY = (pb4, pb5)[h % 2][:, 0:64]
                    pS = (pb4, pb5)[h % 2][0:64, 64:128]
                    t.matmul(pY, atm[:, h * 128:(h + 1) * 128], vn(h),
                             start=True, stop=False)
                    t.matmul(pY, qh(h), sbf[:, h * 64:(h + 1) * 64],
                             start=False, stop=True).then_inc(s_num, 1)
                    ins = t.matmul(pS, kn(h), vn(h), start=True,
                                   stop=True)
                    ins.then_inc(s_su1, 1)
                t.wait_ge(s_dche, i + 1)
                for hh in range(2):
                    sl = slice((i % 2) * D + hh * 512,
                               (i % 2) * D + hh * 512 + 512)
                    ins = t.matmul((pk1, pk2)[hh][0:1, :], onb[:, :],
                                   kbt[:, sl], start=True, stop=True)
                ins.then_inc(s_zu, 1)
                for c in range(8):
                    if ntp >= 2:
                        t.wait_ge(s_tpe, ntp - 1)
                    t.wait_ge(s_ye, 16 * i + 2 * c + 2)
                    ins = t.transpose((tp0, tp1)[ntp % 2][:, :],
                                      ysb[:, c * 128:(c + 1) * 128],
                                      idsb[:, :])
                    ins.then_inc(s_tp, 1)
                    ntp += 1
                t.wait_ge(s_tpe, ntp)
                for hh in range(2):
                    for c in range(8):
                        ins = t.matmul(
                            (pb1, pb2)[hh][:, :],
                            ytsb[:, c * 128:(c + 1) * 128],
                            wpsb[:, c * D + hh * 512:c * D + hh * 512 + 512],
                            start=(c == 0), stop=(c == 7))
                ins.then_inc(s_proj, 1)
                t.wait_ge(s_xfb, i + 1)
                for c in range(8):
                    if ntp >= 2:
                        t.wait_ge(s_tpe, ntp - 1)
                    ins = t.transpose((tp0, tp1)[ntp % 2][:, :],
                                      xfb[:, (i % 2) * D + c * 128:
                                          (i % 2) * D + c * 128 + 128],
                                      idsb[:, :])
                    ins.then_inc(s_tp, 1)
                    ntp += 1
                t.wait_ge(s_tpe, ntp)
                t.wait_ge(s_zue, i + 1)
                if i >= 1:
                    t.wait_ge(s_rr, i)
                for c in range(8):
                    ins = t.matmul(smallp[:, 0:E],
                                   xffm[:, c * 128:(c + 1) * 128],
                                   wgt[:, c * E:(c + 1) * E],
                                   start=(c == 0), stop=(c == 7))
                ins.then_inc(s_gt, 1)
                t.wait_ge(s_rt, i + 1)
                t.wait_ge(s_rr, i)
                t.matmul(smallp[:, 32:32 + E], ussb[:, :], kg32[:, :],
                         start=True, stop=False)
                t.matmul(smallp[:, 32:32 + E], onr[:, :], runr[:, :],
                         start=False, stop=True)
                ins2 = t.matmul(smallp[0:1, 64:64 + E], on1[:, :],
                                kg32[:, :], start=True, stop=True)
                ins2.then_inc(s_ps, 1)

        @block.vector
        def _(v):
            v.memset(stat[:, 5:6], EPS_LN)
            v.memset(runr[:, :], 0.0)
            v.wait_ge(s_cs, 16 * 11)
            v.tensor_tensor(ssb[:, :], sp3s[:, 0:D], sp3s[:, D:2 * D],
                            ALU.add)
            ins = v.tensor_tensor(ssb[:, :], ssb[:, :], sp3s[:, 2 * D:3 * D],
                                  ALU.add)
            vsync.after(v, ins)
            v.tensor_copy(sbf[:, :], ssb[:, :])
            v.tensor_tensor(zrow[:, :], zp3s[:, 0:D], zp3s[:, D:2 * D],
                            ALU.add)
            ins = v.tensor_tensor(zrow[:, :], zrow[:, :],
                                  zp3s[:, 2 * D:3 * D], ALU.add)
            vsync.after(v, ins)
            v.tensor_copy(zrbf[:, :], zrow[:, :]).then_inc(s_sbf, 1)
            ntp = 0
            for i in range(NT):
                for b in range(16):
                    v.wait_ge(s_tp, ntp + 1)
                    tpp = (tp0, tp1)[ntp % 2]
                    de, do = ((qte, qto) if b < 8 else (kte, kto))
                    v.tensor_copy(de[:, (b % 8) * 128:(b % 8) * 128 + 128],
                                  tpp[0:64, :])
                    v.tensor_copy(do[:, (b % 8) * 128:(b % 8) * 128 + 128],
                                  tpp[64:128, :]).then_inc(s_tpe, 1)
                    ntp += 1
                for h in range(H):
                    v.wait_ge(s_at, 16 * i + h + 1)
                    v.tensor_tensor(atm[:, h * 128:(h + 1) * 128],
                                    (pb1, pb2)[h % 2][:, 0:128],
                                    trusb[:, :], ALU.mult).then_inc(s_ate, 1)
                v.wait_ge(s_kc, i + 1)
                ring_wait(v, s_qb, i)
                v.tensor_tensor(prodb[:, 0:512],
                                qbt[:, (i % 2) * D:(i % 2) * D + 512],
                                pk1[:, :], ALU.mult)
                ins = v.tensor_tensor(prodb[:, 512:1024],
                                      qbt[:, (i % 2) * D + 512:
                                          (i % 2 + 1) * D],
                                      pk2[:, :], ALU.mult)
                vsync.after(v, ins)
                ins = v.reduce_sum(dent[:, 0:16],
                                   prodb[:, :].rearrange("p (h d) -> p h d",
                                                         d=DH),
                                   axis=AX.X)
                ins.then_inc(s_dche, 1)
                v.wait_ge(s_dche, i + 1)
                ins = v.tensor_scalar(dent[:, 0:16], dent[:, 0:16], EPS_AT,
                                      None, ALU.add)
                vsync.after(v, ins)
                ins = v.reciprocal(dent[:, 16:32], dent[:, 0:16])
                vsync.after(v, ins)
                for h in range(H):
                    pY = (pb4, pb5)[h % 2][:, 0:64]
                    pS = (pb4, pb5)[h % 2][0:64, 64:128]
                    v.wait_ge(s_num, 16 * i + h + 1)
                    v.tensor_scalar(ysb[:, h * 64:(h + 1) * 64], pY,
                                    dent[:, 16 + h:17 + h], None,
                                    ALU.mult).then_inc(s_ye, 1)
                    v.wait_ge(s_su1, 16 * i + h + 1)
                    v.tensor_tensor(ssb[:, h * 64:(h + 1) * 64],
                                    ssb[:, h * 64:(h + 1) * 64], pS,
                                    ALU.add).then_inc(s_se, 1)
                v.wait_ge(s_zu, i + 1)
                v.tensor_tensor(zrow[:, 0:512], zrow[:, 0:512],
                                pk1[0:1, :], ALU.add)
                ins = v.tensor_tensor(zrow[:, 512:1024], zrow[:, 512:1024],
                                      pk2[0:1, :], ALU.add)
                ins.then_inc(s_zue, 1)
                v.wait_ge(s_zue, i + 1)
                v.tensor_copy(zrbf[:, :], zrow[:, :])
                ins = v.tensor_copy(sbf[:, :], ssb[:, :])
                ins.then_inc(s_sbf, 1)
                for c in range(8):
                    v.wait_ge(s_tp, ntp + 1)
                    if i >= 1 and c == 0:
                        v.wait_ge(s_proj, i)
                    v.tensor_copy(ytsb[:, c * 128:(c + 1) * 128],
                                  (tp0, tp1)[ntp % 2][:, :]).then_inc(s_tpe, 1)
                    ntp += 1
                v.wait_ge(s_proj, i + 1)
                ring_wait(v, s_x2, i)
                ring_wait(v, s_sx3, i - 2)
                xo = x3t[:, (i % 2) * D:(i % 2 + 1) * D]
                x2s = x2t[:, (i % 2) * D:(i % 2 + 1) * D]
                for hh in range(2):
                    sl = slice(hh * 512, (hh + 1) * 512)
                    ins = v.tensor_tensor(xo[:, sl], (pb1, pb2)[hh][:, :],
                                          x2s[:, sl], ALU.add)
                ins.then_inc(s_x3, 1)
                v.engine_nop().then_inc(s_x3e, 1)
                v.wait_ge(s_x3, i + 1)
                ins = v.reduce_sum(stat[:, 0:1], xo, axis=AX.X)
                vsync.after(v, ins)
                ins = v.tensor_scalar(stat[:, 1:2], stat[:, 0:1], -RLN, None,
                                      ALU.mult)
                vsync.after(v, ins)
                v.tensor_scalar(xm2[:, :], xo, stat[:, 1:2], None,
                                ALU.add).then_inc(s_ln, 1)
                v.wait_ge(s_sc, 3 * i + 1)
                v.reduce_sum(stat[:, 2:3], sq2[:, :], axis=AX.X).then_inc(
                    s_ln, 1)
                v.wait_ge(s_sc, 3 * i + 2)
                ring_wait(v, s_sxf, i - 2)
                v.tensor_scalar(xfb[:, (i % 2) * D:(i % 2 + 1) * D],
                                xm2[:, :], stat[:, 4:5], None,
                                ALU.mult).then_inc(s_xfb, 1)
                for c in range(8):
                    v.wait_ge(s_tp, ntp + 1)
                    v.tensor_copy(xffm[:, c * 128:(c + 1) * 128],
                                  (tp0, tp1)[ntp % 2][:, :]).then_inc(s_tpe, 1)
                    ntp += 1
                v.wait_ge(s_gt, i + 1)
                ins = v.tensor_copy(lg[:, :], smallp[:, 0:E])
                vsync.after(v, ins)
                v.reduce_max(stat[:, 6:7], lg[:, :], axis=AX.X,
                             negate=True).then_inc(s_ln, 1)
                v.wait_ge(s_sc, 3 * i + 3)
                ins = v.reduce_sum(stat[:, 7:8], sc_t[:, :], axis=AX.X)
                vsync.after(v, ins)
                ins = v.reciprocal(stat[:, 6:7], stat[:, 7:8])
                vsync.after(v, ins)
                ins = v.tensor_scalar(sc_t[:, :], sc_t[:, :], stat[:, 6:7],
                                      None, ALU.mult)
                vsync.after(v, ins)
                ins = v.reduce_max(gs[:, :],
                                   sc_t[:, :].rearrange("p (g k) -> p g k",
                                                        k=4),
                                   axis=AX.X)
                vsync.after(v, ins)
                ins = v.max(g8[:, :], gs[:, :])
                vsync.after(v, ins)
                ins = v.tensor_scalar(kg[:, :], gs[:, :], g8[:, LG - 1:LG],
                                      None, ALU.is_ge)
                vsync.after(v, ins)
                for jj in range(4):
                    ins = v.tensor_copy(
                        kg32[:, :].rearrange("p (g k) -> p g k", k=4)[:, :, jj],
                        kg[:, :])
                vsync.after(v, ins)
                ins = v.tensor_scalar(msk[:, :], kg32[:, :], -1.0, None,
                                      ALU.add)
                vsync.after(v, ins)
                ins = v.scalar_tensor_tensor(msk[:, :], msk[:, :], 1e8,
                                             sc_t[:, :], ALU.mult, ALU.add)
                vsync.after(v, ins)
                ins = v.max(m8[:, :], msk[:, :])
                vsync.after(v, ins)
                ins = v.max_index(eidx[:, :], m8[:, :], msk[:, :])
                vsync.after(v, ins)
                ins = v.tensor_copy(eidf[:, :], eidx[:, :])
                vsync.after(v, ins)
                v.tensor_scalar(kg32[:, :], msk[:, :], m8[:, TOPK - 1:TOPK],
                                None, ALU.is_ge).then_inc(s_rt, 1)
                ins = v.reduce_sum(rsm[:, 2:3], m8[:, 0:TOPK], axis=AX.X)
                vsync.after(v, ins)
                ins = v.tensor_scalar(rsm[:, 2:3], rsm[:, 2:3], 1e-9, None,
                                      ALU.add)
                vsync.after(v, ins)
                ins = v.reciprocal(rsm[:, 3:4], rsm[:, 2:3])
                vsync.after(v, ins)
                ring_wait(v, s_sro, i - 2)
                v.tensor_scalar(routt[:, (i % 2) * 12 + 8:(i % 2) * 12 + 12],
                                m8[:, 0:TOPK], rsm[:, 3:4], None, ALU.mult)
                v.tensor_copy(routt[:, (i % 2) * 12:(i % 2) * 12 + TOPK],
                              eidf[:, 0:TOPK])
                for j in range(TOPK):
                    v.tensor_scalar(selJ[:, j * E:(j + 1) * E], iosb[:, :],
                                    eidf[:, j:j + 1], None, ALU.is_equal)
                v.wait_ge(s_ps, i + 1)
                ins = v.tensor_copy(post[:, :], smallp[:, 32:32 + E])
                vsync.after(v, ins)
                for j in range(TOPK):
                    ins = v.tensor_tensor(tmp32[:, :],
                                          selJ[:, j * E:(j + 1) * E],
                                          post[:, :], ALU.mult)
                    vsync.after(v, ins)
                    ins = v.reduce_sum(routt[:, (i % 2) * 12 + 4 + j:
                                             (i % 2) * 12 + 4 + j + 1],
                                       tmp32[:, :], axis=AX.X)
                    vsync.after(v, ins)
                v.engine_nop().then_inc(s_ro, 1)
                v.tensor_tensor(runr[:, :], runr[:, :], smallp[0:1, 64:64 + E],
                                ALU.add).then_inc(s_rr, 1)

        @block.scalar
        def _(a):
            for i in range(NT):
                a.wait_ge(s_ln, 3 * i + 1)
                a.activation(sq2[:, :], xm2[:, :], AF.Square).then_inc(s_sc, 1)
                a.wait_ge(s_ln, 3 * i + 2)
                a.activation(stat[:, 3:4], stat[:, 2:3], AF.Ln,
                             bias=stat[:, 5:6], scale=RLN)
                asns(a)
                a.activation(stat[:, 4:5], stat[:, 3:4], AF.Exp,
                             scale=-0.5).then_inc(s_sc, 1)
                a.wait_ge(s_ln, 3 * i + 3)
                a.activation(sc_t[:, :], lg[:, :], AF.Exp,
                             bias=stat[:, 6:7]).then_inc(s_sc, 1)
    return nc



_BUILT = {}


def _programs():
    if not _BUILT:
        _BUILT["l1"] = build_l1()
        _BUILT["l2"] = build_l2()
        _BUILT["l3"] = build_l3()
        _BUILT["l4"] = build_l4()
        _BUILT["l5"] = build_l3(second=True)
    return _BUILT


def kernel(**inputs):
    P = _programs()
    cores = list(range(8))
    x0_full = inputs["L0_attn_a_x"]

    ones_bf = np.ones((128, 1), dtype=ml_dtypes.bfloat16)
    ident = np.eye(128, dtype=ml_dtypes.bfloat16)
    us = np.triu(np.ones((128, 128), np.float32), 1)
    tru = np.triu(np.ones((128, 128), np.float32))
    trub = np.triu(np.ones((128, 128))).astype(ml_dtypes.bfloat16)
    io32 = np.tile(np.arange(32, dtype=np.float32), (128, 1))
    io28 = np.arange(28, dtype=np.float32).reshape(1, 28)
    on128 = np.ones((128, 1), np.float32)
    onr = np.ones((1, 128), np.float32)
    onrb = np.ones((1, 128), ml_dtypes.bfloat16)
    on4 = np.ones((4, 128), np.float32)
    colbase = np.zeros((128, 24), np.float32)
    for cm in range(24):
        j, k = cm // 8, cm % 8
        colbase[:, cm] = j * 32 * 1024 + k * 128 + np.arange(128)
    bo_bc = np.tile(inputs["L0_attn_a_bo"].reshape(1, -1),
                    (128, 1)).astype(np.float32, copy=False)

    xsl = [np.ascontiguousarray(x0_full[c // 4,
                                        (c % 4) * TOK:(c % 4 + 1) * TOK])
           for c in cores]
    wcat = np.concatenate([inputs["L0_ffn_w1"].reshape(E * D, D),
                           inputs["L0_ffn_w3"].reshape(E * D, D),
                           inputs["L0_ffn_w2"].reshape(E * D, D)], axis=0)

    im1 = [{"x0": xsl[c], "wq": inputs["L0_attn_a_wq"],
            "wk": inputs["L0_attn_a_wk"], "wv": inputs["L0_attn_a_wv"],
            "c_ones_bf": ones_bf} for c in cores]
    r1 = run_bass_kernel_spmd(P["l1"], im1, core_ids=cores)

    im2 = []
    for c in cores:
        grp = range((c // 4) * 4, (c // 4) * 4 + 4)
        im2.append({
            "x0": xsl[c], "qf": r1.results[c]["qf"],
            "kv4": np.ascontiguousarray(np.concatenate(
                [r1.results[j]["kvp"] for j in grp], axis=1)),
            "ks4": np.ascontiguousarray(np.concatenate(
                [r1.results[j]["ksp"] for j in grp], axis=0)),
            "wo": inputs["L0_attn_a_wo"], "bo_bc": bo_bc,
            "wg": inputs["L0_ffn_wg"],
            "wcat": np.ascontiguousarray(wcat[c * 12288:(c + 1) * 12288]),
            "c_id": ident, "c_us": us,
            "c_io32": io32, "c_on128": on128, "c_onr": onr, "c_on4": on4,
        })
    r2 = run_bass_kernel_spmd(P["l2"], im2, core_ids=cores)

    counts_all = np.ascontiguousarray(np.concatenate(
        [r2.results[c]["counts"] for c in cores], axis=0))
    wbf_full = np.ascontiguousarray(np.concatenate(
        [r2.results[c]["wbfp"] for c in cores], axis=0))
    im3 = []
    for c in cores:
        pm = np.zeros((8, 1), np.float32)
        pm[:c, 0] = 1.0
        im3.append({
            "x1": r2.results[c]["x1"], "xf": r2.results[c]["xf"],
            "rout": r2.results[c]["rout"], "counts_all": counts_all,
            "mycounts": np.ascontiguousarray(counts_all[c:c + 1]),
            "prefmask": pm, "wbf": wbf_full,
            "sw1": inputs["L0_ffn_sw1"], "sw2": inputs["L0_ffn_sw2"],
            "sw3": inputs["L0_ffn_sw3"],
            "w_attn": inputs["L0_attn_b_w_attn"],
            "c_io32": io32, "c_io28": io28, "c_onr": onr,
            "c_colbase": colbase, "c_ones_bf": ones_bf,
        })
    r3 = run_bass_kernel_spmd(P["l3"], im3, core_ids=cores)

    z64 = np.zeros((64, D), np.float32)
    z1 = np.zeros((1, D), np.float32)
    im4 = []
    for c in cores:
        b, q = c // 4, c % 4
        preds = [b * 4 + qq for qq in range(q)]
        sps = [r3.results[j]["Sp"] for j in preds] + [z64] * (3 - len(preds))
        zps = [r3.results[j]["zp"] for j in preds] + [z1] * (3 - len(preds))
        im4.append({
            "x2": r3.results[c]["x2"], "qb": r3.results[c]["qb"],
            "kb": r3.results[c]["kb"], "vb": r3.results[c]["vb"],
            "sp3": np.ascontiguousarray(np.concatenate(sps, axis=1)),
            "zp3": np.ascontiguousarray(np.concatenate(zps, axis=1)),
            "wp": inputs["L0_attn_b_w_proj"], "wg": inputs["L0_ffn_wg"],
            "c_id": ident, "c_tru": tru, "c_trub": trub, "c_onrb": onrb,
            "c_us": us, "c_io32": io32, "c_on128": on128, "c_onr": onr,
            "c_onb": ones_bf,
        })
    r4 = run_bass_kernel_spmd(P["l4"], im4, core_ids=cores)

    counts2 = np.ascontiguousarray(np.concatenate(
        [r4.results[c]["counts2"] for c in cores], axis=0))
    im5 = []
    for c in cores:
        pm = np.zeros((8, 1), np.float32)
        pm[:c, 0] = 1.0
        im5.append({
            "x1": r4.results[c]["x3"], "xf": r4.results[c]["xf2"],
            "rout": r4.results[c]["rout2"], "counts_all": counts2,
            "mycounts": np.ascontiguousarray(counts2[c:c + 1]),
            "prefmask": pm, "wbf": wbf_full,
            "sw1": inputs["L0_ffn_sw1"], "sw2": inputs["L0_ffn_sw2"],
            "sw3": inputs["L0_ffn_sw3"],
            "c_io32": io32, "c_io28": io28, "c_onr": onr,
            "c_colbase": colbase, "c_ones_bf": ones_bf,
        })
    r5 = run_bass_kernel_spmd(P["l5"], im5, core_ids=cores)

    out = np.empty((2, 4 * TOK, D), np.float32)
    for c in cores:
        out[c // 4, (c % 4) * TOK:(c % 4 + 1) * TOK] = r5.results[c]["x2"]
    return out


# version 06 (final)
# speedup vs baseline: 1.1486x; 1.1486x over previous
import contextlib

import numpy as np
import ml_dtypes
import concourse.bass as bass
import concourse.mybir as mybir
from concourse.bass import IndirectOffsetOnAxis
from concourse.bass_utils import run_bass_kernel_spmd

F32 = mybir.dt.float32
BF16 = mybir.dt.bfloat16
I32 = mybir.dt.int32
AF = mybir.ActivationFunctionType
ALU = mybir.AluOpType
AX = mybir.AxisListType

TOK = 4096
D = 1024
H = 16
DH = 64
NT = TOK // 128
E = 32
TOPK = 4
NGRP = 8
LG = 4
CAP = 5120
EPS_LN = 1e-5
EPS_AT = 1e-6
BASE_C = 640
OV_GRAN = 512
OV_N = 14
NSLOT = E * BASE_C + OV_N * OV_GRAN
SINK = NSLOT
NROW = NSLOT + 128
OV_BASE = E * BASE_C
BIG = 1e30
RLN = 1.0 / D



def ring_sems(ctx, nc, name, n):
    return [ctx.enter_context(nc.semaphore(f"{name}{i}")) for i in range(n)]


def ring_inc(sems, idx, per=16):
    return sems[idx % len(sems)], per


def ring_wait(eng, sems, idx, per=16, mult=1):
    if idx < 0:
        return
    eng.wait_ge(sems[idx % len(sems)], per * mult * (idx // len(sems) + 1))


class SelfSync:

    def __init__(self, ctx, nc, name):
        self.sem = ctx.enter_context(nc.semaphore(name))
        self.n = 0

    def __call__(self, eng):
        self.n += 1
        nopf = getattr(eng, "engine_nop", None) or eng.nop
        nopf().then_inc(self.sem, 1)
        eng.wait_ge(self.sem, self.n)

    def after(self, eng, ins):
        self.n += 1
        ins.then_inc(self.sem, 1)
        eng.wait_ge(self.sem, self.n)



def build_l1():
    nc = bass.Bass("TRN2", target_bir_lowering=False,
                   detect_race_conditions=False)
    x0 = nc.dram_tensor("x0", [TOK, D], F32, kind="ExternalInput").ap()
    w_in = [nc.dram_tensor(nm, [D, D], F32, kind="ExternalInput").ap()
            for nm in ("wq", "wk", "wv")]
    ones_bf = nc.dram_tensor("c_ones_bf", [128, 1], BF16, kind="ExternalInput").ap()

    qf_o = nc.dram_tensor("qf", [TOK, D], BF16, kind="ExternalOutput").ap()
    kvp_o = nc.dram_tensor("kvp", [64, D], F32, kind="ExternalOutput").ap()
    ksp_o = nc.dram_tensor("ksp", [1, D], F32, kind="ExternalOutput").ap()

    xn_d = nc.dram_tensor("xn_d", [TOK, D], BF16, kind="Internal").ap()
    w_bf_d = [nc.dram_tensor(f"wbf_d{j}", [D, D], BF16, kind="Internal").ap()
              for j in range(3)]

    ctx = contextlib.ExitStack()
    with ctx:
        xin = ctx.enter_context(nc.sbuf_tensor("xin", [128, 2 * D], F32))
        xm = ctx.enter_context(nc.sbuf_tensor("xm", [128, D], F32))
        sq = ctx.enter_context(nc.sbuf_tensor("sq", [128, D], F32))
        stat = ctx.enter_context(nc.sbuf_tensor("stat", [128, 8], F32))
        xnbf = ctx.enter_context(nc.sbuf_tensor("xnbf", [128, 2 * D], BF16))
        wtile = ctx.enter_context(nc.sbuf_tensor("wtile", [128, 2 * D], BF16))
        wqt = ctx.enter_context(nc.sbuf_tensor("wqt", [128, 8192], BF16))
        wkt = ctx.enter_context(nc.sbuf_tensor("wkt", [128, 8192], BF16))
        wvt = ctx.enter_context(nc.sbuf_tensor("wvt", [128, 8192], BF16))
        xnfm = ctx.enter_context(nc.sbuf_tensor("xnfm", [128, 2 * 4096], BF16))
        qt = ctx.enter_context(nc.sbuf_tensor("qt", [128, 2 * D], BF16))
        kt = ctx.enter_context(nc.sbuf_tensor("kt", [128, 2 * D], BF16))
        vt = ctx.enter_context(nc.sbuf_tensor("vt", [128, 2 * D], BF16))
        ev1 = ctx.enter_context(nc.sbuf_tensor("ev1", [128, 512], F32))
        ev2 = ctx.enter_context(nc.sbuf_tensor("ev2", [128, 512], F32))
        ev3 = ctx.enter_context(nc.sbuf_tensor("ev3", [128, 512], F32))
        onesb = ctx.enter_context(nc.sbuf_tensor("onesb", [128, 1], BF16))
        kvs = ctx.enter_context(nc.sbuf_tensor("kvs", [64, D], F32))
        kss = ctx.enter_context(nc.sbuf_tensor("kss", [1, D], F32))
        pp0 = ctx.enter_context(nc.psum_tensor("pp0", [128, 512], F32))
        pp1 = ctx.enter_context(nc.psum_tensor("pp1", [128, 512], F32))
        kvp0 = ctx.enter_context(nc.psum_tensor("kvps0", [64, D], F32))
        kvp1 = ctx.enter_context(nc.psum_tensor("kvps1", [64, D], F32))
        ksp = ctx.enter_context(nc.psum_tensor("ksps", [1, D], F32))
        s_ld = ring_sems(ctx, nc, "s_ld", 2)
        s_cst = ring_sems(ctx, nc, "s_cst", 2)
        s_wst = ring_sems(ctx, nc, "s_wst", 2)
        s_one = ctx.enter_context(nc.semaphore("s_one"))
        s_wt = ctx.enter_context(nc.semaphore("s_wt"))
        s_xn = ring_sems(ctx, nc, "s_xn", 2)
        s_fm = ring_sems(ctx, nc, "s_fm", 2)
        s_ln = ctx.enter_context(nc.semaphore("s_ln"))
        s_sc = ctx.enter_context(nc.semaphore("s_sc"))
        s_pe = ctx.enter_context(nc.semaphore("s_pe"))
        s_ev = ctx.enter_context(nc.semaphore("s_ev"))
        s_ex = ctx.enter_context(nc.semaphore("s_ex"))
        s_qk = ctx.enter_context(nc.semaphore("s_qk"))
        s_kv = ctx.enter_context(nc.semaphore("s_kv"))
        s_kve = ctx.enter_context(nc.semaphore("s_kve"))
        s_kse = ctx.enter_context(nc.semaphore("s_kse"))
        s_qf = ring_sems(ctx, nc, "s_qf", 2)
        s_fin = ctx.enter_context(nc.semaphore("s_fin"))
        s_out = ctx.enter_context(nc.semaphore("s_out"))
        vsync = SelfSync(ctx, nc, "vsync")
        asns = SelfSync(ctx, nc, "asns")
        block = ctx.enter_context(nc.Block())
        wts = [wqt, wkt, wvt]
        NG = NT // 4

        @block.gpsimd
        def _(g):
            g.dma_start(out=onesb[:, :], in_=ones_bf).then_inc(s_one, 16)
            n = 0
            for j in range(3):
                for r in range(8):
                    ring_wait(g, s_wst, n - 2)
                    sm, per = ring_inc(s_cst, n)
                    g.dma_start(
                        out=wtile[:, (n % 2) * D:(n % 2 + 1) * D],
                        in_=w_in[j][r * 128:(r + 1) * 128, :],
                    ).then_inc(sm, per)
                    n += 1
            for i in range(NT):
                g.wait_ge(s_qk, i * 6 + 2)
                sm, per = ring_inc(s_qf, i)
                g.dma_start(
                    out=qf_o[i * 128:(i + 1) * 128, :],
                    in_=qt[:, (i % 2) * D:(i % 2 + 1) * D],
                ).then_inc(sm, per)
            g.wait_ge(s_kve, NT)
            g.wait_ge(s_kse, NT)
            g.dma_start(out=kvp_o, in_=kvs[:, :]).then_inc(s_out, 16)
            g.dma_start(out=ksp_o, in_=kss[:, :]).then_inc(s_out, 16)
            g.wait_ge(s_out, 32)
            ring_wait(g, s_qf, NT - 1)
            ring_wait(g, s_qf, NT - 2)

        @block.sync
        def _(s):
            n = 0
            for j in range(3):
                for r in range(8):
                    ring_wait(s, s_cst, n)
                    sm, per = ring_inc(s_wst, n)
                    s.dma_start(
                        out=w_bf_d[j][r * 128:(r + 1) * 128, :],
                        in_=wtile[:, (n % 2) * D:(n % 2 + 1) * D],
                    ).then_inc(sm, per)
                    n += 1
            for j in range(3):
                ring_wait(s, s_wst, 8 * (j + 1) - 1)
                ring_wait(s, s_wst, 8 * (j + 1) - 2)
                for k in range(8):
                    for h in range(2):
                        s.dma_start(
                            out=wts[j][:, k * 1024 + h * 512:k * 1024 + h * 512 + 512],
                            in_=w_bf_d[j][h * 512:(h + 1) * 512, k * 128:(k + 1) * 128],
                            transpose=True,
                        ).then_inc(s_wt, 16)
            for i in range(NT):
                if i >= 2:
                    s.wait_ge(s_ln, 3 * (i - 2) + 1)
                sm, per = ring_inc(s_ld, i)
                s.dma_start(
                    out=xin[:, (i % 2) * D:(i % 2 + 1) * D],
                    in_=x0[i * 128:(i + 1) * 128, :],
                ).then_inc(sm, per)
                s.wait_ge(s_ln, 3 * (i + 1))
                sm, per = ring_inc(s_xn, i)
                s.dma_start(
                    out=xn_d[i * 128:(i + 1) * 128, :],
                    in_=xnbf[:, (i % 2) * D:(i % 2 + 1) * D],
                ).then_inc(sm, per)
                if i % 4 == 3:
                    gi = i // 4
                    ring_wait(s, s_xn, i)
                    ring_wait(s, s_xn, i - 1)
                    if gi >= 2:
                        s.wait_ge(s_pe, 6 * 4 * (gi - 1))
                    sm, per = ring_inc(s_fm, gi)
                    for c in range(8):
                        s.dma_start(
                            out=xnfm[:, (gi % 2) * 4096 + c * 512:
                                     (gi % 2) * 4096 + c * 512 + 512],
                            in_=xn_d[gi * 512:(gi + 1) * 512, c * 128:(c + 1) * 128],
                            transpose=True,
                        ).then_inc(sm, per)

        @block.vector
        def _(v):
            v.memset(stat[:, 5:6], EPS_LN)
            v.memset(kvs[:, :], 0.0)
            v.memset(kss[:, :], 0.0)
            for gi in range(NT // 4):
                for ti in range(4):
                    i = gi * 4 + ti
                    xi = xin[:, (i % 2) * D:(i % 2 + 1) * D]
                    ring_wait(v, s_ld, i)
                    ins = v.reduce_sum(stat[:, 0:1], xi, axis=AX.X)
                    vsync.after(v, ins)
                    ins = v.tensor_scalar(stat[:, 1:2], stat[:, 0:1], -RLN,
                                          None, ALU.mult)
                    vsync.after(v, ins)
                    v.tensor_scalar(xm[:, :], xi, stat[:, 1:2], None,
                                    ALU.add).then_inc(s_ln, 1)
                    v.wait_ge(s_sc, 2 * i + 1)
                    v.reduce_sum(stat[:, 2:3], sq[:, :],
                                 axis=AX.X).then_inc(s_ln, 1)
                    v.wait_ge(s_sc, 2 * i + 2)
                    ring_wait(v, s_xn, i - 2)
                    v.tensor_scalar(xnbf[:, (i % 2) * D:(i % 2 + 1) * D],
                                    xm[:, :], stat[:, 4:5], None,
                                    ALU.mult).then_inc(s_ln, 1)
                for ti in range(4):
                    i = gi * 4 + ti
                    for pg in range(6):
                        npg = i * 6 + pg
                        ps = (pp0, pp1)[npg % 2]
                        v.wait_ge(s_pe, npg + 1)
                        hh = pg % 2
                        dofs = (i % 2) * D + hh * 512
                        if pg < 4:
                            if pg < 2:
                                ring_wait(v, s_qf, i - 2)
                            elif i >= 2:
                                v.wait_ge(s_kv, i - 1)
                            dst = qt if pg < 2 else kt
                            v.tensor_scalar_min(ev1[:, :], ps[:, :], 0.0)
                            v.tensor_scalar_max(ev2[:, :], ps[:, :],
                                                0.0).then_inc(s_ev, 1)
                            v.wait_ge(s_ex, i * 4 + pg + 1)
                            v.tensor_tensor(dst[:, dofs:dofs + 512],
                                            ev3[:, :], ev2[:, :],
                                            ALU.add).then_inc(s_qk, 1)
                        else:
                            if i >= 2:
                                v.wait_ge(s_kv, i - 1)
                            v.tensor_copy(vt[:, dofs:dofs + 512],
                                          ps[:, :]).then_inc(s_qk, 1)
                    v.wait_ge(s_kv, i + 1)
                    kvpi = (kvp0, kvp1)[i % 2]
                    v.tensor_tensor(kvs[:, :], kvs[:, :], kvpi[:, :],
                                    ALU.add).then_inc(s_kve, 1)
                    v.tensor_tensor(kss[:, :], kss[:, :], ksp[:, :],
                                    ALU.add).then_inc(s_kse, 1)

        @block.scalar
        def _(a):
            for gi in range(NT // 4):
                for ti in range(4):
                    i = gi * 4 + ti
                    a.wait_ge(s_ln, 3 * i + 1)
                    a.activation(sq[:, :], xm[:, :], AF.Square).then_inc(s_sc, 1)
                    a.wait_ge(s_ln, 3 * i + 2)
                    ins = a.activation(stat[:, 3:4], stat[:, 2:3], AF.Ln,
                                       bias=stat[:, 5:6], scale=RLN)
                    asns.after(a, ins)
                    a.activation(stat[:, 4:5], stat[:, 3:4], AF.Exp,
                                 scale=-0.5).then_inc(s_sc, 1)
                for ti in range(4):
                    i = gi * 4 + ti
                    for eg in range(4):
                        a.wait_ge(s_ev, i * 4 + eg + 1)
                        a.activation(ev3[:, :], ev1[:, :],
                                     AF.Exp).then_inc(s_ex, 1)

        @block.tensor
        def _(t):
            t.wait_ge(s_wt, 16 * 48)
            for gi in range(NG):
                ring_wait(t, s_fm, gi, mult=8)
                for ti in range(4):
                    i = gi * 4 + ti
                    mofs = (gi % 2) * 4096
                    for pg in range(6):
                        npg = i * 6 + pg
                        ps = (pp0, pp1)[npg % 2]
                        j = pg // 2
                        hh = pg % 2
                        if npg >= 2:
                            t.wait_ge(s_qk, npg - 1)
                        for c in range(8):
                            ins = t.matmul(
                                ps[:, :],
                                xnfm[:, mofs + c * 512 + ti * 128:
                                     mofs + c * 512 + ti * 128 + 128],
                                wts[j][:, c * 1024 + hh * 512:
                                       c * 1024 + hh * 512 + 512],
                                start=(c == 0), stop=(c == 7),
                            )
                        ins.then_inc(s_pe, 1)
                    t.wait_ge(s_qk, (i + 1) * 6)
                    if i >= 2:
                        t.wait_ge(s_kve, i - 1)
                    kvpi = (kvp0, kvp1)[i % 2]
                    for h in range(H):
                        t.matmul(
                            kvpi[:, h * 64:(h + 1) * 64],
                            kt[:, (i % 2) * D + h * 64:(i % 2) * D + h * 64 + 64],
                            vt[:, (i % 2) * D + h * 64:(i % 2) * D + h * 64 + 64],
                            start=True, stop=True,
                        )
                    if i >= 1:
                        t.wait_ge(s_kse, i)
                    for hh in range(2):
                        ins = t.matmul(
                            ksp[:, hh * 512:(hh + 1) * 512],
                            onesb[:, :],
                            kt[:, (i % 2) * D + hh * 512:(i % 2) * D + hh * 512 + 512],
                            start=True, stop=True,
                        )
                    ins.then_inc(s_kv, 1)
    return nc



def build_l2():
    nc = bass.Bass("TRN2", target_bir_lowering=False,
                   detect_race_conditions=False)
    x0 = nc.dram_tensor("x0", [TOK, D], F32, kind="ExternalInput").ap()
    qf_i = nc.dram_tensor("qf", [TOK, D], BF16, kind="ExternalInput").ap()
    kv4 = nc.dram_tensor("kv4", [64, 4 * D], F32, kind="ExternalInput").ap()
    ks4 = nc.dram_tensor("ks4", [4, D], F32, kind="ExternalInput").ap()
    wo = nc.dram_tensor("wo", [D, D], F32, kind="ExternalInput").ap()
    bo_bc_i = nc.dram_tensor("bo_bc", [128, D], F32, kind="ExternalInput").ap()
    wg = nc.dram_tensor("wg", [E, D], F32, kind="ExternalInput").ap()
    wcat = nc.dram_tensor("wcat", [3 * E * D // 8, D], F32,
                          kind="ExternalInput").ap()
    c_id = nc.dram_tensor("c_id", [128, 128], BF16, kind="ExternalInput").ap()
    c_us = nc.dram_tensor("c_us", [128, 128], F32, kind="ExternalInput").ap()
    c_io32 = nc.dram_tensor("c_io32", [128, E], F32, kind="ExternalInput").ap()
    c_on128 = nc.dram_tensor("c_on128", [128, 1], F32, kind="ExternalInput").ap()
    c_onr = nc.dram_tensor("c_onr", [1, 128], F32, kind="ExternalInput").ap()
    c_on4 = nc.dram_tensor("c_on4", [4, 128], F32, kind="ExternalInput").ap()

    x1_o = nc.dram_tensor("x1", [TOK, D], F32, kind="ExternalOutput").ap()
    xf_o = nc.dram_tensor("xf", [TOK, D], BF16, kind="ExternalOutput").ap()
    ro_o = nc.dram_tensor("rout", [TOK, 12], F32, kind="ExternalOutput").ap()
    ct_o = nc.dram_tensor("counts", [1, E], F32, kind="ExternalOutput").ap()
    wbfp_o = nc.dram_tensor("wbfp", [3 * E * D // 8, D], BF16,
                            kind="ExternalOutput").ap()

    wo_bf_d = nc.dram_tensor("wo_bf_d", [D, D], BF16, kind="Internal").ap()
    wg_bf_d = nc.dram_tensor("wg_bf_d", [E, D], BF16, kind="Internal").ap()

    ctx = contextlib.ExitStack()
    with ctx:
        sb = lambda n, s, d: ctx.enter_context(nc.sbuf_tensor(n, s, d))
        pt = lambda n, s, d: ctx.enter_context(nc.psum_tensor(n, s, d))
        xin = sb("xin", [128, 2 * D], F32)
        qin = sb("qin", [128, 2 * D], BF16)
        prodb = sb("prodb", [128, D], F32)
        den = sb("den", [128, 2 * H], F32)
        qfs = sb("qfs", [128, D], BF16)
        qfsfme = sb("qfsfme", [64, D], BF16)
        qfsfmo = sb("qfsfmo", [64, D], BF16)
        mrg = sb("mrg", [128, D], BF16)
        mrgfm = sb("mrgfm", [128, D], BF16)
        x1t = sb("x1t", [128, 2 * D], F32)
        xm2 = sb("xm2", [128, D], F32)
        sq2 = sb("sq2", [128, D], F32)
        stat = sb("stat", [128, 8], F32)
        xfb = sb("xfb", [128, 2 * D], BF16)
        xffm = sb("xffm", [128, D], BF16)
        lg = sb("lg", [128, E], F32)
        sc_t = sb("sc_t", [128, E], F32)
        gs = sb("gs", [128, NGRP], F32)
        kg = sb("kg", [128, NGRP], F32)
        kg32 = sb("kg32", [128, E], F32)
        msk = sb("msk", [128, E], F32)
        selJ = sb("selJ", [128, 4 * E], F32)
        tmp32 = sb("tmp32", [128, E], F32)
        wcol = sb("wcol", [128, 4], F32)
        rsm = sb("rsm", [128, 4], F32)
        g8 = sb("g8", [128, 8], F32)
        m8 = sb("m8", [128, 8], F32)
        eidx = sb("eidx", [128, 8], mybir.dt.uint32)
        eidf = sb("eidf", [128, 8], F32)
        post = sb("post", [128, E], F32)
        routt = sb("routt", [128, 2 * 12], F32)
        runr = sb("runr", [1, E], F32)
        wot = sb("wot", [128, 8192], BF16)
        wgt = sb("wgt", [128, 8 * E], BF16)
        wtile = sb("wtile", [128, 2 * D], BF16)
        idsb = sb("idsb", [128, 128], BF16)
        ussb = sb("ussb", [128, 128], F32)
        iosb = sb("iosb", [128, E], F32)
        on1 = sb("on1", [128, 1], F32)
        onr = sb("onr", [1, 128], F32)
        on4 = sb("on4", [4, 128], F32)
        bobc = sb("bobc", [128, D], F32)
        ksbc = sb("ksbc", [128, D], F32)
        ks4s = sb("ks4s", [4, D], F32)
        kv4s = sb("kv4s", [64, 4 * D], F32)
        kvsum = sb("kvsum", [64, D], F32)
        kvte = sb("kvte", [64, 512], BF16)
        kvto = sb("kvto", [64, 512], BF16)
        tp0 = pt("tp0", [128, 128], BF16)
        tp1 = pt("tp1", [128, 128], BF16)
        mg = pt("mg", [128, D], F32)
        op0 = pt("op0", [128, 512], F32)
        op1 = pt("op1", [128, 512], F32)
        smallp = pt("smallp", [128, 128], F32)
        s_ld = ring_sems(ctx, nc, "s_ld", 2)
        s_qi = ring_sems(ctx, nc, "s_qi", 2)
        s_cst = ring_sems(ctx, nc, "s_cst", 2)
        s_wst = ring_sems(ctx, nc, "s_wst", 2)
        s_sx1 = ring_sems(ctx, nc, "s_sx1", 2)
        s_sxf = ring_sems(ctx, nc, "s_sxf", 2)
        s_sro = ring_sems(ctx, nc, "s_sro", 2)
        s_wt = ctx.enter_context(nc.semaphore("s_wt"))
        s_cs2 = ctx.enter_context(nc.semaphore("s_cs2"))
        s_su = ctx.enter_context(nc.semaphore("s_su"))
        s_sup = ctx.enter_context(nc.semaphore("s_sup"))
        s_den = ctx.enter_context(nc.semaphore("s_den"))
        s_tp = ctx.enter_context(nc.semaphore("s_tp"))
        s_tpe = ctx.enter_context(nc.semaphore("s_tpe"))
        s_qkv = ctx.enter_context(nc.semaphore("s_qkv"))
        s_mev = ctx.enter_context(nc.semaphore("s_mev"))
        s_op = ctx.enter_context(nc.semaphore("s_op"))
        s_x1 = ctx.enter_context(nc.semaphore("s_x1"))
        s_ln = ctx.enter_context(nc.semaphore("s_ln"))
        s_sc = ctx.enter_context(nc.semaphore("s_sc"))
        s_xfb = ctx.enter_context(nc.semaphore("s_xfb"))
        s_gt = ctx.enter_context(nc.semaphore("s_gt"))
        s_rt = ctx.enter_context(nc.semaphore("s_rt"))
        s_ps = ctx.enter_context(nc.semaphore("s_ps"))
        s_rr = ctx.enter_context(nc.semaphore("s_rr"))
        s_ro = ctx.enter_context(nc.semaphore("s_ro"))
        s_out = ctx.enter_context(nc.semaphore("s_out"))
        vsync = SelfSync(ctx, nc, "vsync")
        asns = SelfSync(ctx, nc, "asns")
        block = ctx.enter_context(nc.Block())

        @block.gpsimd
        def _(g):
            n = 0
            for r in range(8):
                ring_wait(g, s_wst, n - 2)
                sm, per = ring_inc(s_cst, n)
                g.dma_start(out=wtile[:, (n % 2) * D:(n % 2 + 1) * D],
                            in_=wo[r * 128:(r + 1) * 128, :]).then_inc(sm, per)
                n += 1
            ring_wait(g, s_wst, n - 2)
            sm, per = ring_inc(s_cst, n)
            g.dma_start(out=wtile[0:E, (n % 2) * D:(n % 2 + 1) * D],
                        in_=wg).then_inc(sm, per)
            n += 1
            for r in range(96):
                ring_wait(g, s_wst, n - 2)
                sm, per = ring_inc(s_cst, n)
                g.dma_start(out=wtile[:, (n % 2) * D:(n % 2 + 1) * D],
                            in_=wcat[r * 128:(r + 1) * 128, :]
                            ).then_inc(sm, per)
                n += 1

        @block.sync
        def _(s):
            s.dma_start(out=idsb[:, :], in_=c_id).then_inc(s_cs2, 16)
            s.dma_start(out=ussb[:, :], in_=c_us).then_inc(s_cs2, 16)
            s.dma_start(out=iosb[:, :], in_=c_io32).then_inc(s_cs2, 16)
            s.dma_start(out=on1[:, :], in_=c_on128).then_inc(s_cs2, 16)
            s.dma_start(out=onr[:, :], in_=c_onr).then_inc(s_cs2, 16)
            s.dma_start(out=on4[:, :], in_=c_on4).then_inc(s_cs2, 16)
            s.dma_start(out=bobc[:, :], in_=bo_bc_i).then_inc(s_cs2, 16)
            s.dma_start(out=ks4s[:, :], in_=ks4).then_inc(s_cs2, 16)
            s.dma_start(out=kv4s[:, :], in_=kv4).then_inc(s_cs2, 16)
            n = 0
            for r in range(8):
                ring_wait(s, s_cst, n)
                sm, per = ring_inc(s_wst, n)
                s.dma_start(out=wo_bf_d[r * 128:(r + 1) * 128, :],
                            in_=wtile[:, (n % 2) * D:(n % 2 + 1) * D]
                            ).then_inc(sm, per)
                n += 1
            ring_wait(s, s_cst, n)
            sm, per = ring_inc(s_wst, n)
            s.dma_start(out=wg_bf_d,
                        in_=wtile[0:E, (n % 2) * D:(n % 2 + 1) * D]
                        ).then_inc(sm, per)
            n += 1
            ring_wait(s, s_wst, n - 1)
            ring_wait(s, s_wst, n - 2)
            for k in range(8):
                for h in range(2):
                    s.dma_start(
                        out=wot[:, k * 1024 + h * 512:k * 1024 + h * 512 + 512],
                        in_=wo_bf_d[h * 512:(h + 1) * 512, k * 128:(k + 1) * 128],
                        transpose=True).then_inc(s_wt, 16)
            for k in range(8):
                s.dma_start(out=wgt[:, k * E:(k + 1) * E],
                            in_=wg_bf_d[0:E, k * 128:(k + 1) * 128],
                            transpose=True).then_inc(s_wt, 16)
            for i in range(NT):
                if i >= 2:
                    s.wait_ge(s_x1, i - 1)
                    s.wait_ge(s_den, i - 1)
                sm, per = ring_inc(s_ld, i)
                s.dma_start(out=xin[:, (i % 2) * D:(i % 2 + 1) * D],
                            in_=x0[i * 128:(i + 1) * 128, :]).then_inc(sm, per)
                sm, per = ring_inc(s_qi, i)
                s.dma_start(out=qin[:, (i % 2) * D:(i % 2 + 1) * D],
                            in_=qf_i[i * 128:(i + 1) * 128, :]).then_inc(sm, per)
                s.wait_ge(s_x1, i + 1)
                sm, per = ring_inc(s_sx1, i)
                s.dma_start(out=x1_o[i * 128:(i + 1) * 128, :],
                            in_=x1t[:, (i % 2) * D:(i % 2 + 1) * D]
                            ).then_inc(sm, per)
                s.wait_ge(s_xfb, i + 1)
                sm, per = ring_inc(s_sxf, i)
                s.dma_start(out=xf_o[i * 128:(i + 1) * 128, :],
                            in_=xfb[:, (i % 2) * D:(i % 2 + 1) * D]
                            ).then_inc(sm, per)
                s.wait_ge(s_ro, i + 1)
                sm, per = ring_inc(s_sro, i)
                s.dma_start(out=ro_o[i * 128:(i + 1) * 128, :],
                            in_=routt[:, (i % 2) * 12:(i % 2 + 1) * 12]
                            ).then_inc(sm, per)
            s.wait_ge(s_rr, NT)
            s.dma_start(out=ct_o, in_=runr[:, :]).then_inc(s_out, 16)
            s.wait_ge(s_out, 16)
            ring_wait(s, s_sx1, NT - 1)
            ring_wait(s, s_sx1, NT - 2)
            ring_wait(s, s_sxf, NT - 1)
            ring_wait(s, s_sxf, NT - 2)
            ring_wait(s, s_sro, NT - 1)
            ring_wait(s, s_sro, NT - 2)
            ring_wait(s, s_wst, 96 + 9 - 1)
            ring_wait(s, s_wst, 96 + 9 - 2)
            s.wait_ge(s_out, 16)

        @block.tensor
        def _(t):
            t.wait_ge(s_cs2, 16 * 9)
            for hh in range(2):
                t.matmul((op0, op1)[hh][:, :], on4[:, :],
                         ks4s[:, hh * 512:(hh + 1) * 512],
                         start=True, stop=True).then_inc(s_sup, 1)
            t.wait_ge(s_wt, 16 * 24)
            ntp = 0
            for i in range(NT):
                t.wait_ge(s_den, i + 1)
                for c in range(8):
                    if ntp >= 2:
                        t.wait_ge(s_tpe, ntp - 1)
                    ins = t.transpose((tp0, tp1)[ntp % 2][:, :],
                                      qfs[:, c * 128:(c + 1) * 128], idsb[:, :])
                    ins.then_inc(s_tp, 1)
                    ntp += 1
                t.wait_ge(s_tpe, ntp)
                if i >= 1:
                    t.wait_ge(s_mev, i)
                for h in range(H):
                    qsrc = qfsfme if h % 2 == 0 else qfsfmo
                    ksrc = kvte if h % 2 == 0 else kvto
                    ins = t.matmul(
                        mg[:, h * 64:(h + 1) * 64],
                        qsrc[:, (h // 2) * 128:(h // 2) * 128 + 128],
                        ksrc[:, (h // 2) * 64:(h // 2) * 64 + 64],
                        start=True, stop=True)
                ins.then_inc(s_qkv, 1)
                t.wait_ge(s_mev, i + 1)
                for c in range(8):
                    if ntp >= 2:
                        t.wait_ge(s_tpe, ntp - 1)
                    ins = t.transpose((tp0, tp1)[ntp % 2][:, :],
                                      mrg[:, c * 128:(c + 1) * 128], idsb[:, :])
                    ins.then_inc(s_tp, 1)
                    ntp += 1
                t.wait_ge(s_tpe, ntp)
                if i >= 1:
                    t.wait_ge(s_x1, i)
                else:
                    t.wait_ge(s_su, 1)
                for hh in range(2):
                    for c in range(8):
                        ins = t.matmul(
                            (op0, op1)[hh][:, :],
                            mrgfm[:, c * 128:(c + 1) * 128],
                            wot[:, c * 1024 + hh * 512:c * 1024 + hh * 512 + 512],
                            start=(c == 0), stop=(c == 7))
                ins.then_inc(s_op, 1)
                t.wait_ge(s_xfb, i + 1)
                for c in range(8):
                    if ntp >= 2:
                        t.wait_ge(s_tpe, ntp - 1)
                    ins = t.transpose((tp0, tp1)[ntp % 2][:, :],
                                      xfb[:, (i % 2) * D + c * 128:
                                          (i % 2) * D + c * 128 + 128],
                                      idsb[:, :])
                    ins.then_inc(s_tp, 1)
                    ntp += 1
                t.wait_ge(s_tpe, ntp)
                if i >= 1:
                    t.wait_ge(s_rr, i)
                for c in range(8):
                    ins = t.matmul(smallp[:, 0:E],
                                   xffm[:, c * 128:(c + 1) * 128],
                                   wgt[:, c * E:(c + 1) * E],
                                   start=(c == 0), stop=(c == 7))
                ins.then_inc(s_gt, 1)
                t.wait_ge(s_rt, i + 1)
                t.wait_ge(s_rr, i)
                t.matmul(smallp[:, 32:32 + E], ussb[:, :], kg32[:, :],
                         start=True, stop=False)
                t.matmul(smallp[:, 32:32 + E], onr[:, :], runr[:, :],
                         start=False, stop=True)
                ins2 = t.matmul(smallp[0:1, 64:64 + E], on1[:, :],
                                kg32[:, :], start=True, stop=True)
                ins2.then_inc(s_ps, 1)

        @block.vector
        def _(v):
            v.memset(stat[:, 5:6], EPS_LN)
            v.memset(runr[:, :], 0.0)
            v.wait_ge(s_cs2, 16 * 9)
            v.tensor_tensor(kvsum[:, :], kv4s[:, 0:D], kv4s[:, D:2 * D], ALU.add)
            v.tensor_tensor(kvsum[:, :], kvsum[:, :], kv4s[:, 2 * D:3 * D], ALU.add)
            ins = v.tensor_tensor(kvsum[:, :], kvsum[:, :],
                                  kv4s[:, 3 * D:4 * D], ALU.add)
            vsync.after(v, ins)
            v.tensor_copy(
                kvte[:, :].rearrange("p (g d) -> p g d", d=64),
                kvsum[:, :].rearrange("p (g two d) -> p g two d", two=2, d=64)[:, :, 0, :])
            v.tensor_copy(
                kvto[:, :].rearrange("p (g d) -> p g d", d=64),
                kvsum[:, :].rearrange("p (g two d) -> p g two d", two=2, d=64)[:, :, 1, :])
            v.wait_ge(s_sup, 2)
            v.tensor_copy(ksbc[:, 0:512], op0[:, :])
            v.tensor_copy(ksbc[:, 512:1024], op1[:, :]).then_inc(s_su, 1)
            ntp = 0
            for i in range(NT):
                qi = qin[:, (i % 2) * D:(i % 2 + 1) * D]
                xi = xin[:, (i % 2) * D:(i % 2 + 1) * D]
                ring_wait(v, s_qi, i)
                ins = v.tensor_tensor(prodb[:, :], ksbc[:, :], qi, ALU.mult)
                vsync.after(v, ins)
                ins = v.reduce_sum(den[:, 0:H],
                                   prodb[:, :].rearrange("p (h d) -> p h d",
                                                         d=DH),
                                   axis=AX.X)
                vsync.after(v, ins)
                ins = v.tensor_scalar(den[:, 0:H], den[:, 0:H], EPS_AT, None,
                                      ALU.add)
                vsync.after(v, ins)
                ins = v.reciprocal(den[:, H:2 * H], den[:, 0:H])
                vsync.after(v, ins)
                for h in range(H):
                    ins = v.tensor_scalar(qfs[:, h * 64:(h + 1) * 64],
                                          qi[:, h * 64:(h + 1) * 64],
                                          den[:, H + h:H + h + 1], None,
                                          ALU.mult)
                ins.then_inc(s_den, 1)
                for c in range(8):
                    v.wait_ge(s_tp, ntp + 1)
                    tpp = (tp0, tp1)[ntp % 2]
                    v.tensor_copy(qfsfme[:, c * 128:(c + 1) * 128],
                                  tpp[0:64, :])
                    v.tensor_copy(qfsfmo[:, c * 128:(c + 1) * 128],
                                  tpp[64:128, :]).then_inc(s_tpe, 1)
                    ntp += 1
                v.wait_ge(s_qkv, i + 1)
                v.tensor_copy(mrg[:, :], mg[:, :]).then_inc(s_mev, 1)
                for c in range(8):
                    v.wait_ge(s_tp, ntp + 1)
                    v.tensor_copy(mrgfm[:, c * 128:(c + 1) * 128],
                                  (tp0, tp1)[ntp % 2][:, :]).then_inc(s_tpe, 1)
                    ntp += 1
                v.wait_ge(s_op, i + 1)
                ring_wait(v, s_ld, i)
                ring_wait(v, s_sx1, i - 2)
                xo = x1t[:, (i % 2) * D:(i % 2 + 1) * D]
                for hh in range(2):
                    ps = (op0, op1)[hh]
                    sl = slice(hh * 512, (hh + 1) * 512)
                    v.tensor_tensor(xo[:, sl], ps[:, :], bobc[:, sl], ALU.add)
                    ins = v.tensor_tensor(xo[:, sl], xo[:, sl], xi[:, sl],
                                          ALU.add)
                ins.then_inc(s_x1, 1)
                v.wait_ge(s_x1, i + 1)
                ins = v.reduce_sum(stat[:, 0:1], xo, axis=AX.X)
                vsync.after(v, ins)
                ins = v.tensor_scalar(stat[:, 1:2], stat[:, 0:1], -RLN, None,
                                      ALU.mult)
                vsync.after(v, ins)
                v.tensor_scalar(xm2[:, :], xo, stat[:, 1:2], None,
                                ALU.add).then_inc(s_ln, 1)
                v.wait_ge(s_sc, 3 * i + 1)
                v.reduce_sum(stat[:, 2:3], sq2[:, :], axis=AX.X).then_inc(s_ln, 1)
                v.wait_ge(s_sc, 3 * i + 2)
                ring_wait(v, s_sxf, i - 2)
                v.tensor_scalar(xfb[:, (i % 2) * D:(i % 2 + 1) * D],
                                xm2[:, :], stat[:, 4:5], None,
                                ALU.mult).then_inc(s_xfb, 1)
                for c in range(8):
                    v.wait_ge(s_tp, ntp + 1)
                    v.tensor_copy(xffm[:, c * 128:(c + 1) * 128],
                                  (tp0, tp1)[ntp % 2][:, :]).then_inc(s_tpe, 1)
                    ntp += 1
                v.wait_ge(s_gt, i + 1)
                ins = v.tensor_copy(lg[:, :], smallp[:, 0:E])
                vsync.after(v, ins)
                v.reduce_max(stat[:, 6:7], lg[:, :], axis=AX.X,
                             negate=True).then_inc(s_ln, 1)
                v.wait_ge(s_sc, 3 * i + 3)
                ins = v.reduce_sum(stat[:, 7:8], sc_t[:, :], axis=AX.X)
                vsync.after(v, ins)
                ins = v.reciprocal(stat[:, 6:7], stat[:, 7:8])
                vsync.after(v, ins)
                ins = v.tensor_scalar(sc_t[:, :], sc_t[:, :], stat[:, 6:7],
                                      None, ALU.mult)
                vsync.after(v, ins)
                ins = v.reduce_max(gs[:, :],
                                   sc_t[:, :].rearrange("p (g k) -> p g k",
                                                        k=4),
                                   axis=AX.X)
                vsync.after(v, ins)
                ins = v.max(g8[:, :], gs[:, :])
                vsync.after(v, ins)
                ins = v.tensor_scalar(kg[:, :], gs[:, :], g8[:, LG - 1:LG],
                                      None, ALU.is_ge)
                vsync.after(v, ins)
                for jj in range(4):
                    ins = v.tensor_copy(
                        kg32[:, :].rearrange("p (g k) -> p g k", k=4)[:, :, jj],
                        kg[:, :])
                vsync.after(v, ins)
                ins = v.tensor_scalar(msk[:, :], kg32[:, :], -1.0, None,
                                      ALU.add)
                vsync.after(v, ins)
                ins = v.scalar_tensor_tensor(msk[:, :], msk[:, :], 1e8,
                                             sc_t[:, :], ALU.mult, ALU.add)
                vsync.after(v, ins)
                ins = v.max(m8[:, :], msk[:, :])
                vsync.after(v, ins)
                ins = v.max_index(eidx[:, :], m8[:, :], msk[:, :])
                vsync.after(v, ins)
                ins = v.tensor_copy(eidf[:, :], eidx[:, :])
                vsync.after(v, ins)
                v.tensor_scalar(kg32[:, :], msk[:, :], m8[:, TOPK - 1:TOPK],
                                None, ALU.is_ge).then_inc(s_rt, 1)
                ins = v.reduce_sum(rsm[:, 2:3], m8[:, 0:TOPK], axis=AX.X)
                vsync.after(v, ins)
                ins = v.tensor_scalar(rsm[:, 2:3], rsm[:, 2:3], 1e-9, None,
                                      ALU.add)
                vsync.after(v, ins)
                ins = v.reciprocal(rsm[:, 3:4], rsm[:, 2:3])
                vsync.after(v, ins)
                ring_wait(v, s_sro, i - 2)
                v.tensor_scalar(routt[:, (i % 2) * 12 + 8:(i % 2) * 12 + 12],
                                m8[:, 0:TOPK], rsm[:, 3:4], None, ALU.mult)
                v.tensor_copy(routt[:, (i % 2) * 12:(i % 2) * 12 + TOPK],
                              eidf[:, 0:TOPK])
                for j in range(TOPK):
                    v.tensor_scalar(selJ[:, j * E:(j + 1) * E], iosb[:, :],
                                    eidf[:, j:j + 1], None, ALU.is_equal)
                v.wait_ge(s_ps, i + 1)
                ins = v.tensor_copy(post[:, :], smallp[:, 32:32 + E])
                vsync.after(v, ins)
                for j in range(TOPK):
                    ins = v.tensor_tensor(tmp32[:, :],
                                          selJ[:, j * E:(j + 1) * E],
                                          post[:, :], ALU.mult)
                    vsync.after(v, ins)
                    ins = v.reduce_sum(routt[:, (i % 2) * 12 + 4 + j:
                                             (i % 2) * 12 + 4 + j + 1],
                                       tmp32[:, :], axis=AX.X)
                    vsync.after(v, ins)
                v.engine_nop().then_inc(s_ro, 1)
                v.tensor_tensor(runr[:, :], runr[:, :], smallp[0:1, 64:64 + E],
                                ALU.add).then_inc(s_rr, 1)

        @block.scalar
        def _(a):
            for i in range(NT):
                a.wait_ge(s_ln, 3 * i + 1)
                a.activation(sq2[:, :], xm2[:, :], AF.Square).then_inc(s_sc, 1)
                a.wait_ge(s_ln, 3 * i + 2)
                a.activation(stat[:, 3:4], stat[:, 2:3], AF.Ln,
                             bias=stat[:, 5:6], scale=RLN)
                asns(a)
                a.activation(stat[:, 4:5], stat[:, 3:4], AF.Exp,
                             scale=-0.5).then_inc(s_sc, 1)
                a.wait_ge(s_ln, 3 * i + 3)
                a.activation(sc_t[:, :], lg[:, :], AF.Exp,
                             bias=stat[:, 6:7]).then_inc(s_sc, 1)
                for r3_ in range(3):
                    n_ = 9 + i * 3 + r3_
                    ring_wait(a, s_cst, n_)
                    sm, per = ring_inc(s_wst, n_)
                    a.dma_start(out=wbfp_o[(n_ - 9) * 128:(n_ - 8) * 128, :],
                                in_=wtile[:, (n_ % 2) * D:(n_ % 2 + 1) * D]
                                ).then_inc(sm, per)
    return nc




def l3_sched():
    gl = []
    for sgrp in range(8):
        gl.append((0, sgrp * 512, 512, -1))
    for e in range(E):
        gl.append((1, e * BASE_C, 512, e))
        gl.append((1, e * BASE_C + 512, 128, e))
    for og in range(OV_N):
        gl.append((2, OV_BASE + og * OV_GRAN, OV_GRAN, E + og))
    return gl


def build_l3(second=False):
    nc = bass.Bass("TRN2", target_bir_lowering=False,
                   detect_race_conditions=False)
    x1 = nc.dram_tensor("x1", [TOK, D], F32, kind="ExternalInput").ap()
    xf_i = nc.dram_tensor("xf", [TOK, D], BF16, kind="ExternalInput").ap()
    ro_i = nc.dram_tensor("rout", [TOK, 12], F32, kind="ExternalInput").ap()
    cts = nc.dram_tensor("counts_all", [8, E], F32, kind="ExternalInput").ap()
    myct = nc.dram_tensor("mycounts", [1, E], F32, kind="ExternalInput").ap()
    pmask = nc.dram_tensor("prefmask", [8, 1], F32, kind="ExternalInput").ap()
    sw1 = nc.dram_tensor("sw1", [D, D], F32, kind="ExternalInput").ap()
    sw2 = nc.dram_tensor("sw2", [D, D], F32, kind="ExternalInput").ap()
    sw3 = nc.dram_tensor("sw3", [D, D], F32, kind="ExternalInput").ap()
    if not second:
        wat_i = nc.dram_tensor("w_attn", [D, 3 * D], F32,
                               kind="ExternalInput").ap()
    c_io32 = nc.dram_tensor("c_io32", [128, E], F32, kind="ExternalInput").ap()
    c_io28 = nc.dram_tensor("c_io28", [1, OV_N], F32, kind="ExternalInput").ap()
    c_onr = nc.dram_tensor("c_onr", [1, 128], F32, kind="ExternalInput").ap()
    c_cb = nc.dram_tensor("c_colbase", [128, 24], F32, kind="ExternalInput").ap()
    c_onb = nc.dram_tensor("c_ones_bf", [128, 1], BF16, kind="ExternalInput").ap()

    x2_o = nc.dram_tensor("x2", [TOK, D], F32, kind="ExternalOutput").ap()
    wbf_o = nc.dram_tensor("wbf", [3 * E * D, D], BF16,
                           kind="ExternalInput").ap()
    if not second:
        qb_o = nc.dram_tensor("qb", [TOK, D], BF16,
                              kind="ExternalOutput").ap()
        kb_o = nc.dram_tensor("kb", [TOK, D], BF16,
                              kind="ExternalOutput").ap()
        vb_o = nc.dram_tensor("vb", [TOK, D], BF16,
                              kind="ExternalOutput").ap()
        sp_o = nc.dram_tensor("Sp", [64, D], F32, kind="ExternalOutput").ap()
        zp_o = nc.dram_tensor("zp", [1, D], F32, kind="ExternalOutput").ap()

    _l3kind = "ExternalOutput" if globals().get("L3_DEBUG") else "Internal"
    xe_d = nc.dram_tensor("xe_d", [NROW, D], BF16, kind=_l3kind).ap()
    so_d = nc.dram_tensor("so_d", [NROW, D], F32, kind=_l3kind).ap()
    z_d = nc.dram_tensor("z_d", [TOK, D], F32, kind=_l3kind).ap()
    if not second:
        xb_d = nc.dram_tensor("xb_d", [TOK, D], BF16, kind="Internal").ap()
    tab_d = nc.dram_tensor(
        "tab_d", [1, 4 * E], F32,
        kind="ExternalOutput" if globals().get("L3_DEBUG") else "Internal"
    ).ap()
    if globals().get("L3_DEBUG"):
        dst_dbg = nc.dram_tensor("dst_dbg", [TOK, 4], I32,
                                 kind="ExternalOutput").ap()
        tbt_dbg = nc.dram_tensor("tbt_dbg", [128, 16 * NT], F32,
                                 kind="ExternalOutput").ap()

    GL = l3_sched()
    NGR = len(GL)
    HB, SB_, nh, ns = [], [], 0, 0
    for (kind, base, n, u) in GL:
        HB.append(nh)
        SB_.append(ns)
        nh += 8
        ns += n // 128
    NHC, NST = nh, ns
    unit_last_group = {}
    for gi, (kind, base, n, u) in enumerate(GL):
        if u >= 0:
            unit_last_group[u] = gi
    NUNIT = E + OV_N

    ctx = contextlib.ExitStack()
    with ctx:
        sb = lambda n_, s, d: ctx.enter_context(nc.sbuf_tensor(n_, s, d))
        pt = lambda n_, s, d: ctx.enter_context(nc.psum_tensor(n_, s, d))
        wslab = sb("wslab", [128, 49152], BF16)
        xefm = sb("xefm", [128, 2 * 4096], BF16)
        hfm = sb("hfm", [128, 4096], BF16)
        hs_t = sb("hs_t", [128, 1024], F32)
        outb = sb("outb", [128, 2 * D], F32)
        rot = sb("rot", [128, 2 * 12], F32)
        xft = sb("xft", [128, 2 * D], BF16)
        tbt = sb("tbt", [128, 16], F32)
        ei32 = sb("ei32", [128, 4], I32)
        di32 = sb("di32", [128, 8 * NT], I32)
        dsc = sb("dsc", [128, 160], F32)
        row32 = sb("row32", [1, 256], F32)
        ov28 = sb("ov28", [1, 96], F32)
        ebc = sb("ebc", [128, OV_N], F32)
        bc96 = sb("bc96", [128, 96], F32)
        widxf = sb("widxf", [128, 24], F32)
        widx = sb("widx", [128, 2 * 24], I32)
        gt = sb("gt", [128, 2 * D], F32)
        x1t = sb("x1t", [128, D], F32)
        x2t = sb("x2t", [128, 2 * D], F32)
        xm3 = sb("xm3", [128, D], F32)
        sq3 = sb("sq3", [128, D], F32)
        stat = sb("stat", [128, 8], F32)
        xbt = sb("xbt", [128, 2 * D], BF16)
        xbfm = sb("xbfm", [128, 4096], BF16)
        qbt = sb("qbt", [128, 2 * D], BF16)
        kbt = sb("kbt", [128, 2 * D], BF16)
        vbt = sb("vbt", [128, 2 * D], BF16)
        ev1 = sb("ev1", [128, 512], F32)
        ev2 = sb("ev2", [128, 512], F32)
        ev3 = sb("ev3", [128, 512], F32)
        onesb = sb("onesb", [128, 1], BF16)
        io32s = sb("io32s", [128, E], F32)
        cbs = sb("cbs", [128, 24], F32)
        onrs = sb("onrs", [1, 128], F32)
        pms = sb("pms", [8, 128], F32)
        ctss = sb("ctss", [8, E], F32)
        mycs = sb("mycs", [1, E], F32)
        io28s = sb("io28s", [1, OV_N], F32)
        spac = sb("spac", [64, D], F32)
        zpac = sb("zpac", [1, D], F32)
        h1a = pt("h1a", [128, 512], F32)
        h1b = pt("h1b", [128, 512], F32)
        h3a = pt("h3a", [128, 512], F32)
        h3b = pt("h3b", [128, 512], F32)
        outp = pt("outp", [128, D], F32)
        smallp = pt("smallp", [128, 128], F32)
        s_cs = ctx.enter_context(nc.semaphore("s_cs"))
        s_onb = ctx.enter_context(nc.semaphore("s_onb"))
        s_zro = ctx.enter_context(nc.semaphore("s_zro"))
        s_xez = ctx.enter_context(nc.semaphore("s_xez"))
        s_swl = ctx.enter_context(nc.semaphore("s_swl"))
        s_sup = ctx.enter_context(nc.semaphore("s_sup"))
        s_su = ctx.enter_context(nc.semaphore("s_su"))
        s_tab = ctx.enter_context(nc.semaphore("s_tab"))
        if globals().get("L3_DEBUG"):
            s_tbd = ctx.enter_context(nc.semaphore("s_tbd"))
        s_rol = ring_sems(ctx, nc, "s_rol", 2)
        s_xfl = ring_sems(ctx, nc, "s_xfl", 2)
        s_ei = ctx.enter_context(nc.semaphore("s_ei"))
        s_tbg = ctx.enter_context(nc.semaphore("s_tbg"))
        s_dst = ctx.enter_context(nc.semaphore("s_dst"))
        s_sct = ring_sems(ctx, nc, "s_sct", 2)
        s_hm = ctx.enter_context(nc.semaphore("s_hm"))
        s_sil = ctx.enter_context(nc.semaphore("s_sil"))
        s_hev = ctx.enter_context(nc.semaphore("s_hev"))
        s_hfb = ctx.enter_context(nc.semaphore("s_hfb"))
        s_og = ctx.enter_context(nc.semaphore("s_og"))
        s_oev = ctx.enter_context(nc.semaphore("s_oev"))
        s_sod = ring_sems(ctx, nc, "s_sod", 2)
        s_wld = ring_sems(ctx, nc, "s_wld", 2)
        s_wldh = ring_sems(ctx, nc, "s_wldh", 2)
        s_wlda = ring_sems(ctx, nc, "s_wlda", 2)
        s_wldo = ring_sems(ctx, nc, "s_wldo", 2)
        s_wbs = ring_sems(ctx, nc, "s_wbs", 2)
        s_wix = ctx.enter_context(nc.semaphore("s_wix"))
        s_xel = ring_sems(ctx, nc, "s_xel", 2)
        s_cg2 = ctx.enter_context(nc.semaphore("s_cg2"))
        s_cgd = ctx.enter_context(nc.semaphore("s_cgd"))
        s_czl = ring_sems(ctx, nc, "s_czl", 2)
        s_x2 = ctx.enter_context(nc.semaphore("s_x2"))
        s_sx2 = ring_sems(ctx, nc, "s_sx2", 2)
        s_lnb = ctx.enter_context(nc.semaphore("s_lnb"))
        s_scb = ctx.enter_context(nc.semaphore("s_scb"))
        s_xbs = ring_sems(ctx, nc, "s_xbs", 2)
        s_bfm = ring_sems(ctx, nc, "s_bfm", 2)
        s_pb = ctx.enter_context(nc.semaphore("s_pb"))
        s_evb = ctx.enter_context(nc.semaphore("s_evb"))
        s_exb = ctx.enter_context(nc.semaphore("s_exb"))
        s_qkb = ctx.enter_context(nc.semaphore("s_qkb"))
        s_kvb = ctx.enter_context(nc.semaphore("s_kvb"))
        s_kve = ctx.enter_context(nc.semaphore("s_kve"))
        s_kse = ctx.enter_context(nc.semaphore("s_kse"))
        s_qs = ring_sems(ctx, nc, "s_qs", 2)
        s_ks = ring_sems(ctx, nc, "s_ks", 2)
        s_vs = ring_sems(ctx, nc, "s_vs", 2)
        s_watl = ctx.enter_context(nc.semaphore("s_watl"))
        s_fin = ctx.enter_context(nc.semaphore("s_fin"))
        s_out = ctx.enter_context(nc.semaphore("s_out"))
        vsync = SelfSync(ctx, nc, "vsync")
        asns = SelfSync(ctx, nc, "asns")
        block = ctx.enter_context(nc.Block())

        def wv(kind, p, k):
            return wslab[:, kind * 16384 + p * 8192 + k * 1024:
                         kind * 16384 + p * 8192 + (k + 1) * 1024]

        def watv(k, hh):
            return wslab[:, k * 3072 + hh * 512:k * 3072 + (hh + 1) * 512]

        NM = 8

        @block.gpsimd
        def _(g):
            g.dma_start(out=onesb[:, :], in_=c_onb).then_inc(s_onb, 16)
            for j, w in enumerate((sw1, sw3, sw2)):
                for k in range(8):
                    g.dma_start(out=wv(j, 0, k),
                                in_=w[k * 128:(k + 1) * 128, :]
                                ).then_inc(s_swl, 16)
            g.wait_ge(s_tab, 64)
            for i in range(NT):
                g.wait_ge(s_ei, i + 1)
                for j in range(4):
                    g.indirect_dma_start(
                        out=tbt[:, j * 4:(j + 1) * 4], out_offset=None,
                        in_=tab_d.rearrange("o (a b) -> (o a) b", b=4),
                        in_offset=IndirectOffsetOnAxis(ei32[:, j:j + 1], 0),
                    ).then_inc(s_tbg, 16)
                if globals().get("L3_DEBUG"):
                    g.dma_start(out=tbt_dbg[:, i * 16:(i + 1) * 16],
                                in_=tbt[:, 0:16]).then_inc(s_tbd, 16)
                g.wait_ge(s_dst, i + 1)
                ring_wait(g, s_xfl, i)
                if i == 0:
                    g.wait_ge(s_xez, 16 * (NROW // 128 + 1))
                ring_wait(g, s_sct, i - 2, mult=4)
                for j in range(4):
                    g.indirect_dma_start(
                        out=xe_d,
                        out_offset=IndirectOffsetOnAxis(
                            di32[:, i * 8 + j:i * 8 + j + 1], 0),
                        in_=xft[:, (i % 2) * D:(i % 2 + 1) * D],
                        in_offset=None,
                    ).then_inc(s_sct[i % 2], 16)
            for u in range(NUNIT):
                p = u % 2
                if u >= 2:
                    gl_ = unit_last_group[u - 2]
                    g.wait_ge(s_og, SB_[gl_] + GL[gl_][2] // 128)
                if u < 2:
                    g.wait_ge(s_og, 32)
                if u < E:
                    for j in range(3):
                        for k in range(8):
                            g.dma_start(
                                out=wv(j, p, k),
                                in_=wbf_o[j * E * D + u * D + k * 128:
                                          j * E * D + u * D + (k + 1) * 128,
                                          :],
                            ).then_inc(s_wld[u % 2], 16)
                else:
                    og = u - E
                    g.wait_ge(s_wix, og + 1)
                    for cm in range(24):
                        j, k = cm // 8, cm % 8
                        g.indirect_dma_start(
                            out=wv(j, p, k), out_offset=None,
                            in_=wbf_o,
                            in_offset=IndirectOffsetOnAxis(
                                widx[:, (og % 2) * 24 + cm:
                                     (og % 2) * 24 + cm + 1], 0),
                        ).then_inc(s_wldo[u % 2], 16)
            if not second:
                g.wait_ge(s_og, NST)
                for k in range(8):
                    for hh in range(3):
                        g.dma_start(
                            out=wslab[:, k * 3072 + hh * 1024:
                                      k * 3072 + (hh + 1) * 1024],
                            in_=wat_i[k * 128:(k + 1) * 128,
                                      hh * 1024:(hh + 1) * 1024],
                        ).then_inc(s_watl, 16)
            g.wait_ge(s_oev, NST)
            ring_wait(g, s_sod, NST - 1)
            ring_wait(g, s_sod, NST - 2)
            for i in range(NT):
                if i >= 1:
                    g.wait_ge(s_cg2, 2 * i)
                for j in range(2):
                    g.indirect_dma_start(
                        out=gt[:, j * D:(j + 1) * D], out_offset=None,
                        in_=so_d,
                        in_offset=IndirectOffsetOnAxis(
                            di32[:, i * 8 + j:i * 8 + j + 1], 0),
                    ).then_inc(s_cgd, 16)
                g.wait_ge(s_cg2, 2 * i + 1)
                for j in range(2, 4):
                    g.indirect_dma_start(
                        out=gt[:, (j - 2) * D:(j - 1) * D], out_offset=None,
                        in_=so_d,
                        in_offset=IndirectOffsetOnAxis(
                            di32[:, i * 8 + j:i * 8 + j + 1], 0),
                    ).then_inc(s_cgd, 16)
                if i % 4 == 3 and not second:
                    for jj in range(i - 3, i + 1):
                        g.wait_ge(s_qkb, jj * 6 + 2)
                        sm, per = ring_inc(s_qs, jj)
                        g.dma_start(out=qb_o[jj * 128:(jj + 1) * 128, :],
                                    in_=qbt[:, (jj % 2) * D:(jj % 2 + 1) * D]
                                    ).then_inc(sm, per)
                        g.wait_ge(s_qkb, jj * 6 + 4)
                        sm, per = ring_inc(s_ks, jj)
                        g.dma_start(out=kb_o[jj * 128:(jj + 1) * 128, :],
                                    in_=kbt[:, (jj % 2) * D:(jj % 2 + 1) * D]
                                    ).then_inc(sm, per)
                        g.wait_ge(s_qkb, jj * 6 + 6)
                        sm, per = ring_inc(s_vs, jj)
                        g.dma_start(out=vb_o[jj * 128:(jj + 1) * 128, :],
                                    in_=vbt[:, (jj % 2) * D:(jj % 2 + 1) * D]
                                    ).then_inc(sm, per)
            if globals().get("L3_DEBUG"):
                g.wait_ge(s_tbd, 16 * NT)

        @block.sync
        def _(s):
            for ap_in, dst in ((c_io32, io32s[:, :]), (c_io28, io28s[:, :]),
                               (c_onr, onrs[:, :]), (pmask, pms[:, 0:1]),
                               (cts, ctss[:, :]), (myct, mycs[:, :]),
                               (c_cb, cbs[:, :])):
                s.dma_start(out=dst, in_=ap_in).then_inc(s_cs, 16)
            s.wait_ge(s_zro, 2)
            s.dma_start(out=so_d[SINK:SINK + 128, :],
                        in_=x2t[:, 0:D]).then_inc(s_xez, 16)
            s.wait_ge(s_su, 1)
            with nc.allow_non_contiguous_dma(reason="32x4B strided table"):
                for c3 in range(4):
                    src = row32[0:1, c3 * 32:(c3 + 1) * 32]
                    s.dma_start(
                        out=tab_d.rearrange("o (a b) -> o a b", b=4)[:, :, c3],
                        in_=src).then_inc(s_tab, 16)
            for i in range(NT):
                if i % 4 == 0:
                    gi = i // 4
                    kind, base, n, u = GL[gi]
                    if gi >= 2:
                        s.wait_ge(s_og, SB_[gi - 1])
                    sm, per = ring_inc(s_xel, gi)
                    for c in range(8):
                        s.dma_start(
                            out=xefm[:, (gi % 2) * 4096 + c * n:
                                     (gi % 2) * 4096 + (c + 1) * n],
                            in_=xf_i[base:base + n, c * 128:(c + 1) * 128],
                            transpose=True).then_inc(sm, per)
                if i >= 2:
                    s.wait_ge(s_dst, i - 1)
                    ring_wait(s, s_sct, i - 2, mult=4)
                sm, per = ring_inc(s_rol, i)
                s.dma_start(out=rot[:, (i % 2) * 12:(i % 2 + 1) * 12],
                            in_=ro_i[i * 128:(i + 1) * 128, :]).then_inc(sm, per)
                sm, per = ring_inc(s_xfl, i)
                s.dma_start(out=xft[:, (i % 2) * D:(i % 2 + 1) * D],
                            in_=xf_i[i * 128:(i + 1) * 128, :]).then_inc(sm, per)
            for gi, (kind, base, n, u) in enumerate(GL):
                if kind == 0:
                    continue
                if gi == 8:
                    ring_wait(s, s_sct, NT - 1, mult=4)
                    ring_wait(s, s_sct, NT - 2, mult=4)
                if gi >= 2:
                    s.wait_ge(s_og, SB_[gi - 1])
                sm, per = ring_inc(s_xel, gi)
                for c in range(8):
                    s.dma_start(
                        out=xefm[:, (gi % 2) * 4096 + c * n:
                                 (gi % 2) * 4096 + (c + 1) * n],
                        in_=xe_d[base:base + n, c * 128:(c + 1) * 128],
                        transpose=True).then_inc(sm, per)
            s.wait_ge(s_oev, NST)
            ring_wait(s, s_sod, NST - 1)
            ring_wait(s, s_sod, NST - 2)
            for i in range(NT):
                if i >= 1:
                    s.wait_ge(s_x2, i)
                ring_wait(s, s_sx2, i - 2)
                sm, per = ring_inc(s_czl, i)
                s.dma_start(out=x2t[:, (i % 2) * D:(i % 2 + 1) * D],
                            in_=z_d[i * 128:(i + 1) * 128, :]).then_inc(sm, per)
                s.dma_start(out=x1t[:, :], in_=x1[i * 128:(i + 1) * 128, :]
                            ).then_inc(sm, per)
                sm, per = ring_inc(s_rol, NT + i)
                s.dma_start(out=rot[:, (i % 2) * 12:(i % 2 + 1) * 12],
                            in_=ro_i[i * 128:(i + 1) * 128, :]).then_inc(sm, per)
                s.wait_ge(s_x2, i + 1)
                sm, per = ring_inc(s_sx2, i)
                s.dma_start(out=x2_o[i * 128:(i + 1) * 128, :],
                            in_=x2t[:, (i % 2) * D:(i % 2 + 1) * D]
                            ).then_inc(sm, per)
                if not second:
                    s.wait_ge(s_lnb, 3 * (i + 1))
                    sm, per = ring_inc(s_xbs, i)
                    s.dma_start(out=xb_d[i * 128:(i + 1) * 128, :],
                                in_=xbt[:, (i % 2) * D:(i % 2 + 1) * D]
                                ).then_inc(sm, per)
                if i % 4 == 3 and not second:
                    gi4 = i // 4
                    ring_wait(s, s_xbs, i)
                    ring_wait(s, s_xbs, i - 1)
                    if gi4 >= 1:
                        s.wait_ge(s_pb, 6 * 4 * gi4)
                    sm, per = ring_inc(s_bfm, gi4)
                    for c in range(8):
                        s.dma_start(
                            out=xbfm[:, c * 512:(c + 1) * 512],
                            in_=xb_d[gi4 * 512:(gi4 + 1) * 512,
                                     c * 128:(c + 1) * 128],
                            transpose=True).then_inc(sm, per)
            if globals().get("L3_DEBUG"):
                s.wait_ge(s_dst, NT)
                for i_ in range(NT):
                    s.dma_start(out=dst_dbg[i_ * 128:(i_ + 1) * 128, :],
                                in_=di32[:, i_ * 8:i_ * 8 + 4]
                                ).then_inc(s_out, 16)
            if not second:
                s.wait_ge(s_kve, NT)
                s.wait_ge(s_kse, NT)
                s.dma_start(out=sp_o, in_=spac[:, :]).then_inc(s_out, 16)
                s.dma_start(out=zp_o, in_=zpac[:, :]).then_inc(s_out, 16)
                s.wait_ge(s_out, 32)
            for r in ((s_sx2, s_qs, s_ks, s_vs) if not second
                      else (s_sx2,)):
                ring_wait(s, r, NT - 1)
                ring_wait(s, r, NT - 2)

        @block.tensor
        def _(t):
            t.wait_ge(s_cs, 16 * 7)
            t.wait_ge(s_onb, 16)
            t.matmul(smallp[0:1, 0:E], pms[:, 0:1], ctss[:, :],
                     start=True, stop=True).then_inc(s_sup, 1)
            t.wait_ge(s_su, 1)
            t.matmul(smallp[:, 0:96], onrs[:, :], row32[0:1, 0:96],
                     start=True, stop=True).then_inc(s_sup, 1)
            t.matmul(smallp[:, 96:96 + OV_N], onrs[:, :], ov28[0:1, 0:OV_N],
                     start=True, stop=True).then_inc(s_sup, 1)
            for gi, (kind, base, n, u) in enumerate(GL):
                ring_wait(t, s_xel, gi, mult=8)
                if kind == 0:
                    t.wait_ge(s_swl, 16 * 24)
                    p = 0
                elif u < E:
                    ring_wait(t, s_wld, u, mult=24)
                    p = u % 2
                else:
                    ring_wait(t, s_wldo, u - E, mult=24)
                    p = u % 2
                for m in range(NM):
                    nglob = HB[gi] + m
                    pr = nglob % 2
                    if nglob >= 2:
                        t.wait_ge(s_hev, nglob - 1)
                    h1p = (h1a, h1b)[pr]
                    h3p = (h3a, h3b)[pr]
                    for k in range(8):
                        t.matmul(h1p[:, 0:n],
                                 wv(0, p, k)[:, m * 128:(m + 1) * 128],
                                 xefm[:, (gi % 2) * 4096 + k * n:
                                      (gi % 2) * 4096 + (k + 1) * n],
                                 start=(k == 0), stop=(k == 7))
                    for k in range(8):
                        ins = t.matmul(h3p[:, 0:n],
                                       wv(1, p, k)[:, m * 128:(m + 1) * 128],
                                       xefm[:, (gi % 2) * 4096 + k * n:
                                            (gi % 2) * 4096 + (k + 1) * n],
                                       start=(k == 0), stop=(k == 7))
                    ins.then_inc(s_hm, 1)
                t.wait_ge(s_hev, HB[gi] + 8)
                for st in range(n // 128):
                    nst = SB_[gi] + st
                    if nst >= 1:
                        t.wait_ge(s_oev, nst)
                    for hh in range(2):
                        for k in range(8):
                            ins = t.matmul(
                                outp[:, hh * 512:(hh + 1) * 512],
                                hfm[:, k * n + st * 128:k * n + st * 128 + 128],
                                wv(2, p, k)[:, hh * 512:(hh + 1) * 512],
                                start=(k == 0), stop=(k == 7))
                    ins.then_inc(s_og, 1)
            if not second:
                t.wait_ge(s_watl, 16 * 24)
                for gi4 in range(NT // 4):
                    ring_wait(t, s_bfm, gi4, mult=8)
                    for ti in range(4):
                        i = gi4 * 4 + ti
                        for pg in range(6):
                            npg = i * 6 + pg
                            ps = (h1a, h1b)[npg % 2]
                            if npg >= 2:
                                t.wait_ge(s_qkb, npg - 1)
                            for c in range(8):
                                ins = t.matmul(
                                    ps[:, :],
                                    xbfm[:, c * 512 + ti * 128:
                                         c * 512 + ti * 128 + 128],
                                    watv(c, pg),
                                    start=(c == 0), stop=(c == 7))
                            ins.then_inc(s_pb, 1)
                        t.wait_ge(s_qkb, (i + 1) * 6)
                        t.wait_ge(s_kve, i)
                        for h in range(H):
                            t.matmul(
                                outp[0:64, h * 64:(h + 1) * 64],
                                kbt[:, (i % 2) * D + h * 64:(i % 2) * D + h * 64 + 64],
                                vbt[:, (i % 2) * D + h * 64:(i % 2) * D + h * 64 + 64],
                                start=True, stop=True)
                        t.wait_ge(s_kse, i)
                        for hh in range(2):
                            zpp = (h3a, h3b)[hh]
                            ins = t.matmul(
                                zpp[0:1, 0:512], onesb[:, :],
                                kbt[:, (i % 2) * D + hh * 512:
                                    (i % 2) * D + hh * 512 + 512],
                                start=True, stop=True)
                        ins.then_inc(s_kvb, 1)

        @block.vector
        def _(v):
            v.memset(stat[:, 5:6], EPS_LN)
            if not second:
                v.memset(spac[:, :], 0.0)
                v.memset(zpac[:, :], 0.0)
            v.memset(xbt[:, 0:D], 0.0).then_inc(s_zro, 1)
            v.memset(x2t[:, 0:D], 0.0).then_inc(s_zro, 1)
            v.wait_ge(s_cs, 16 * 7)
            v.wait_ge(s_sup, 1)
            ins = v.tensor_copy(row32[0:1, 0:E], smallp[0:1, 0:E])
            vsync.after(v, ins)
            ins = v.tensor_scalar(row32[0:1, 96:128], mycs[0:1, :],
                                  -float(BASE_C), None, ALU.add)
            vsync.after(v, ins)
            ins = v.tensor_scalar(row32[0:1, 96:128], row32[0:1, 96:128], 0.0,
                                  None, ALU.max)
            vsync.after(v, ins)
            ins = v.memset(row32[0:1, 128:160], 0.0)
            vsync.after(v, ins)
            for m_ in range(OV_N):
                ins = v.tensor_scalar(row32[0:1, 192:224], row32[0:1, 96:128],
                                      float(m_ * OV_GRAN) + 0.5, None,
                                      ALU.is_ge)
                vsync.after(v, ins)
                ins = v.tensor_tensor(row32[0:1, 128:160],
                                      row32[0:1, 128:160],
                                      row32[0:1, 192:224], ALU.add)
                vsync.after(v, ins)
            ins = v.tensor_scalar(row32[0:1, 32:64], row32[0:1, 128:160],
                                  float(OV_GRAN), None, ALU.mult)
            vsync.after(v, ins)
            ins = v.tensor_copy(row32[0:1, 160:192], row32[0:1, 32:64])
            vsync.after(v, ins)
            a_, b_ = 160, 192
            for dstep in (1, 2, 4, 8, 16):
                ins = v.tensor_copy(row32[0:1, b_:b_ + dstep],
                                    row32[0:1, a_:a_ + dstep])
                vsync.after(v, ins)
                ins = v.tensor_tensor(row32[0:1, b_ + dstep:b_ + 32],
                                      row32[0:1, a_ + dstep:a_ + 32],
                                      row32[0:1, a_:a_ + 32 - dstep], ALU.add)
                vsync.after(v, ins)
                a_, b_ = b_, a_
            ins = v.memset(row32[0:1, 64:65], 0.0)
            vsync.after(v, ins)
            ins = v.tensor_copy(row32[0:1, 65:96], row32[0:1, a_:a_ + 31])
            vsync.after(v, ins)
            ins = v.tensor_scalar(row32[0:1, 224:256], row32[0:1, 64:96],
                                  1.0 / OV_GRAN, None, ALU.mult)
            vsync.after(v, ins)
            ins = v.tensor_tensor(row32[0:1, 128:160], row32[0:1, 64:96],
                                  row32[0:1, 32:64], ALU.add)
            vsync.after(v, ins)
            ins = v.tensor_scalar(row32[0:1, 128:160], row32[0:1, 128:160],
                                  1.0 / OV_GRAN, None, ALU.mult)
            vsync.after(v, ins)
            ins = v.memset(ov28[0:1, 0:OV_N], 0.0)
            vsync.after(v, ins)
            for e in range(E):
                ins = v.tensor_scalar(ov28[0:1, 32:32 + OV_N], io28s[0:1, :],
                                      row32[0:1, 224 + e:225 + e], None,
                                      ALU.is_ge)
                vsync.after(v, ins)
                ins = v.tensor_scalar(ov28[0:1, 64:64 + OV_N], io28s[0:1, :],
                                      row32[0:1, 128 + e:129 + e], None,
                                      ALU.is_lt)
                vsync.after(v, ins)
                ins = v.tensor_tensor(ov28[0:1, 32:32 + OV_N],
                                      ov28[0:1, 32:32 + OV_N],
                                      ov28[0:1, 64:64 + OV_N], ALU.mult)
                vsync.after(v, ins)
                ins = v.scalar_tensor_tensor(ov28[0:1, 0:OV_N],
                                             ov28[0:1, 32:32 + OV_N],
                                             float(e), ov28[0:1, 0:OV_N],
                                             ALU.mult, ALU.add)
                vsync.after(v, ins)
            ins = v.memset(row32[0:1, 96:128], 0.0)
            ins.then_inc(s_su, 1)
            v.wait_ge(s_sup, 3)
            v.tensor_copy(bc96[:, :], smallp[:, 0:96])
            ins = v.tensor_scalar(ebc[:, :], smallp[:, 96:96 + OV_N], 1024.0,
                                  None, ALU.mult)
            vsync.after(v, ins)
            for og in range(2):
                ins = v.tensor_scalar(widxf[:, :], cbs[:, :],
                                      ebc[:, og:og + 1], None, ALU.add)
                vsync.after(v, ins)
                ins = v.tensor_copy(widx[:, (og % 2) * 24:(og % 2 + 1) * 24],
                                    widxf[:, :])
                ins.then_inc(s_wix, 1)
            nself = 0
            for i in range(NT):
                ring_wait(v, s_rol, i)
                ri = rot[:, (i % 2) * 12:(i % 2 + 1) * 12]
                v.tensor_copy(ei32[:, :], ri[:, 0:4]).then_inc(s_ei, 1)
                v.wait_ge(s_tbg, 16 * 4 * (i + 1))
                eidc = [ri[:, j:j + 1] for j in range(4)]
                posc = [ri[:, 4 + j:5 + j] for j in range(4)]
                tbo = [tbt[:, j * 4:j * 4 + 1] for j in range(4)]
                tbg = [tbt[:, j * 4 + 1:j * 4 + 2] for j in range(4)]
                tbs = [tbt[:, j * 4 + 2:j * 4 + 3] for j in range(4)]
                A = [dsc[:, 32 + j * 32:33 + j * 32] for j in range(4)]
                B = [dsc[:, 40 + j * 32:41 + j * 32] for j in range(4)]
                C = [dsc[:, 48 + j * 32:49 + j * 32] for j in range(4)]
                Dd = [dsc[:, 56 + j * 32:57 + j * 32] for j in range(4)]
                dst = [dsc[:, j * 8:j * 8 + 1] for j in range(4)]
                J = range(4)
                for j in J:
                    v.tensor_scalar(A[j], posc[j], float(OV_BASE - BASE_C),
                                    None, ALU.add)
                for j in J:
                    v.tensor_tensor(A[j], A[j], tbs[j], ALU.add)
                for j in J:
                    v.scalar_tensor_tensor(Dd[j], eidc[j], float(BASE_C),
                                           posc[j], ALU.mult, ALU.add)
                for j in J:
                    v.tensor_tensor(C[j], posc[j], tbg[j], ALU.subtract)
                for j in J:
                    v.tensor_scalar(B[j], A[j], float(NSLOT), None, ALU.is_lt)
                for j in J:
                    v.tensor_scalar(C[j], C[j], float(BASE_C), None, ALU.is_lt)
                for j in J:
                    v.tensor_tensor(B[j], B[j], C[j], ALU.mult)
                for j in J:
                    v.tensor_scalar(A[j], A[j], -float(SINK), None, ALU.add)
                for j in J:
                    v.tensor_tensor(A[j], A[j], B[j], ALU.mult)
                for j in J:
                    v.tensor_scalar(A[j], A[j], float(SINK), None, ALU.add)
                for j in J:
                    v.tensor_scalar(C[j], posc[j], float(BASE_C), None,
                                    ALU.is_lt)
                for j in J:
                    v.tensor_tensor(B[j], posc[j], tbo[j], ALU.add)
                for j in J:
                    v.tensor_scalar(B[j], B[j], float(CAP), None, ALU.is_lt)
                for j in J:
                    v.tensor_tensor(Dd[j], Dd[j], A[j], ALU.subtract)
                for j in J:
                    v.tensor_tensor(Dd[j], Dd[j], C[j], ALU.mult)
                for j in J:
                    v.tensor_tensor(Dd[j], Dd[j], A[j], ALU.add)
                for j in J:
                    v.tensor_scalar(Dd[j], Dd[j], -float(SINK), None, ALU.add)
                for j in J:
                    v.tensor_tensor(Dd[j], Dd[j], B[j], ALU.mult)
                for j in J:
                    v.tensor_scalar(Dd[j], Dd[j], float(SINK), None, ALU.add)
                for j in J:
                    v.tensor_scalar(Dd[j], Dd[j], float(SINK), None, ALU.min)
                for j in J:
                    ins = v.tensor_scalar(dst[j], Dd[j], 0.0, None, ALU.max)
                vsync.after(v, ins)
                ins = v.tensor_copy(
                    di32[:, i * 8:i * 8 + 4],
                    dsc[:, 0:32].rearrange("p (a b) -> p a b", b=8)[:, :, 0])
                ins.then_inc(s_dst, 1)
                if i % 4 == 3:
                    sg = i // 4
                    for m in range(NM):
                        nglob = sg * 8 + m
                        v.wait_ge(s_sil, nglob + 1)
                        ins = v.tensor_tensor(
                            hfm[:, m * 512:(m + 1) * 512],
                            hs_t[:, (nglob % 2) * 512:(nglob % 2 + 1) * 512],
                            ((h3a, h3b)[nglob % 2])[:, 0:512],
                            ALU.mult)
                        ins.then_inc(s_hev, 1)
                    for st in range(4):
                        nst = sg * 4 + st
                        v.wait_ge(s_og, nst + 1)
                        ring_wait(v, s_sod, nst - 2)
                        ins = v.tensor_copy(
                            outb[:, (nst % 2) * D:(nst % 2 + 1) * D],
                            outp[:, :])
                        ins.then_inc(s_oev, 1)
            for gi, (kind, base, n, u) in enumerate(GL):
                if kind == 0:
                    continue
                if kind == 2 and (u - E) + 2 < OV_N:
                    og = (u - E) + 2
                    ring_wait(v, s_wldo, og - 2, mult=24)
                    ins = v.tensor_scalar(widxf[:, :], cbs[:, :],
                                          ebc[:, og:og + 1], None, ALU.add)
                    vsync.after(v, ins)
                    ins = v.tensor_copy(
                        widx[:, (og % 2) * 24:(og % 2 + 1) * 24], widxf[:, :])
                    ins.then_inc(s_wix, 1)
                for m in range(NM):
                    nglob = HB[gi] + m
                    v.wait_ge(s_sil, nglob + 1)
                    ins = v.tensor_tensor(
                        hfm[:, m * n:(m + 1) * n],
                        hs_t[:, (nglob % 2) * 512:(nglob % 2) * 512 + n],
                        ((h3a, h3b)[nglob % 2])[:, 0:n],
                        ALU.mult)
                    ins.then_inc(s_hev, 1)
                for st in range(n // 128):
                    nst = SB_[gi] + st
                    v.wait_ge(s_og, nst + 1)
                    ring_wait(v, s_sod, nst - 2)
                    ins = v.tensor_copy(
                        outb[:, (nst % 2) * D:(nst % 2 + 1) * D], outp[:, :])
                    ins.then_inc(s_oev, 1)
            for i in range(NT):
                v.wait_ge(s_cgd, 16 * (4 * i + 2))
                ring_wait(v, s_czl, i, mult=2)
                ring_wait(v, s_rol, NT + i)
                ri = rot[:, (i % 2) * 12:(i % 2 + 1) * 12]
                xo = x2t[:, (i % 2) * D:(i % 2 + 1) * D]
                v.scalar_tensor_tensor(xo, gt[:, 0:D], ri[:, 8:9],
                                       xo, ALU.mult, ALU.add)
                ins = v.scalar_tensor_tensor(
                    xo, gt[:, D:2 * D], ri[:, 9:10], xo,
                    ALU.mult, ALU.add)
                ins.then_inc(s_cg2, 1)
                v.wait_ge(s_cgd, 16 * (4 * i + 4))
                for j in range(2, 4):
                    ins = v.scalar_tensor_tensor(
                        xo, gt[:, (j - 2) * D:(j - 1) * D],
                        ri[:, 8 + j:9 + j], xo,
                        ALU.mult, ALU.add)
                ins.then_inc(s_cg2, 1)
                v.tensor_tensor(xo, xo, x1t[:, :], ALU.add).then_inc(s_x2, 1)
                if not second:
                    v.wait_ge(s_x2, i + 1)
                    ins = v.reduce_sum(stat[:, 0:1], xo, axis=AX.X)
                    vsync.after(v, ins)
                    ins = v.tensor_scalar(stat[:, 1:2], stat[:, 0:1], -RLN, None,
                                          ALU.mult)
                    vsync.after(v, ins)
                    v.tensor_scalar(xm3[:, :], xo, stat[:, 1:2], None,
                                    ALU.add).then_inc(s_lnb, 1)
                    v.wait_ge(s_scb, 2 * i + 1)
                    v.reduce_sum(stat[:, 2:3], sq3[:, :], axis=AX.X).then_inc(s_lnb, 1)
                    v.wait_ge(s_scb, 2 * i + 2)
                    ring_wait(v, s_xbs, i - 2)
                    v.tensor_scalar(xbt[:, (i % 2) * D:(i % 2 + 1) * D],
                                    xm3[:, :], stat[:, 4:5], None,
                                    ALU.mult).then_inc(s_lnb, 1)
                    if i % 4 == 3:
                        for tj in range(4):
                            jj = (i // 4) * 4 + tj
                            for pg in range(6):
                                npg = jj * 6 + pg
                                ps = (h1a, h1b)[npg % 2]
                                v.wait_ge(s_pb, npg + 1)
                                hh = pg % 2
                                dofs = (jj % 2) * D + hh * 512
                                if pg < 4:
                                    if pg < 2:
                                        ring_wait(v, s_qs, jj - 2)
                                    else:
                                        ring_wait(v, s_ks, jj - 2)
                                    dst = qbt if pg < 2 else kbt
                                    v.tensor_scalar_min(ev1[:, :], ps[:, :], 0.0)
                                    v.tensor_scalar_max(ev2[:, :], ps[:, :],
                                                        0.0).then_inc(s_evb, 1)
                                    v.wait_ge(s_exb, jj * 4 + pg + 1)
                                    v.tensor_tensor(dst[:, dofs:dofs + 512],
                                                    ev3[:, :], ev2[:, :],
                                                    ALU.add).then_inc(s_qkb, 1)
                                else:
                                    ring_wait(v, s_vs, jj - 2)
                                    v.tensor_copy(vbt[:, dofs:dofs + 512],
                                                  ps[:, :]).then_inc(s_qkb, 1)
                            v.wait_ge(s_kvb, jj + 1)
                            v.tensor_tensor(spac[:, :], spac[:, :],
                                            outp[0:64, :], ALU.add).then_inc(s_kve, 1)
                            v.tensor_tensor(zpac[:, 0:512], zpac[:, 0:512],
                                            h3a[0:1, 0:512], ALU.add)
                            v.tensor_tensor(zpac[:, 512:1024], zpac[:, 512:1024],
                                            h3b[0:1, 0:512],
                                            ALU.add).then_inc(s_kse, 1)

        @block.scalar
        def _(a):
            a.wait_ge(s_zro, 1)
            for r in range(NROW // 128):
                a.dma_start(out=xe_d[r * 128:(r + 1) * 128, :],
                            in_=xbt[:, 0:D]).then_inc(s_xez, 16)
            for gi, (kind, base, n, u) in enumerate(GL):
                if gi >= 1:
                    pk, pb_, pn, pu = GL[gi - 1]
                    for st in range(pn // 128):
                        nst = SB_[gi - 1] + st
                        a.wait_ge(s_oev, nst + 1)
                        dst_d = z_d if pk == 0 else so_d
                        dofs = pb_ + st * 128
                        sm, per = ring_inc(s_sod, nst)
                        a.dma_start(
                            out=dst_d[dofs:dofs + 128, :],
                            in_=outb[:, (nst % 2) * D:(nst % 2 + 1) * D]
                            ).then_inc(sm, per)
                for m in range(NM):
                    nglob = HB[gi] + m
                    a.wait_ge(s_hm, nglob + 1)
                    if nglob >= 2:
                        a.wait_ge(s_hev, nglob - 1)
                    a.activation(
                        hs_t[:, (nglob % 2) * 512:(nglob % 2) * 512 + n],
                        ((h1a, h1b)[nglob % 2])[:, 0:n],
                        AF.Silu).then_inc(s_sil, 1)
            gi = NGR - 1
            pk, pb_, pn, pu = GL[gi]
            for st in range(pn // 128):
                nst = SB_[gi] + st
                a.wait_ge(s_oev, nst + 1)
                sm, per = ring_inc(s_sod, nst)
                a.dma_start(out=so_d[pb_ + st * 128:pb_ + st * 128 + 128, :],
                            in_=outb[:, (nst % 2) * D:(nst % 2 + 1) * D]
                            ).then_inc(sm, per)
            if not second:
                for i in range(NT):
                    a.wait_ge(s_lnb, 3 * i + 1)
                    a.activation(sq3[:, :], xm3[:, :], AF.Square).then_inc(s_scb, 1)
                    a.wait_ge(s_lnb, 3 * i + 2)
                    ins = a.activation(stat[:, 3:4], stat[:, 2:3], AF.Ln,
                                       bias=stat[:, 5:6], scale=RLN)
                    asns.after(a, ins)
                    a.activation(stat[:, 4:5], stat[:, 3:4], AF.Exp,
                                 scale=-0.5).then_inc(s_scb, 1)
                    if i % 4 == 3:
                        for tj in range(4):
                            jj = (i // 4) * 4 + tj
                            for eg in range(4):
                                a.wait_ge(s_evb, jj * 4 + eg + 1)
                                a.activation(ev3[:, :], ev1[:, :],
                                             AF.Exp).then_inc(s_exb, 1)
    return nc




def build_l4():
    nc = bass.Bass("TRN2", target_bir_lowering=False,
                   detect_race_conditions=False)
    x2_i = nc.dram_tensor("x2", [TOK, D], F32, kind="ExternalInput").ap()
    qb_i = nc.dram_tensor("qb", [TOK, D], BF16, kind="ExternalInput").ap()
    kb_i = nc.dram_tensor("kb", [TOK, D], BF16, kind="ExternalInput").ap()
    vb_i = nc.dram_tensor("vb", [TOK, D], BF16, kind="ExternalInput").ap()
    sp3_i = nc.dram_tensor("sp3", [64, 3 * D], F32, kind="ExternalInput").ap()
    zp3_i = nc.dram_tensor("zp3", [1, 3 * D], F32, kind="ExternalInput").ap()
    wp_i = nc.dram_tensor("wp", [D, D], F32, kind="ExternalInput").ap()
    wg = nc.dram_tensor("wg", [E, D], F32, kind="ExternalInput").ap()
    c_id = nc.dram_tensor("c_id", [128, 128], BF16, kind="ExternalInput").ap()
    c_tru = nc.dram_tensor("c_tru", [128, 128], F32, kind="ExternalInput").ap()
    c_trub = nc.dram_tensor("c_trub", [128, 128], BF16,
                            kind="ExternalInput").ap()
    c_onrb = nc.dram_tensor("c_onrb", [1, 128], BF16,
                            kind="ExternalInput").ap()
    c_us = nc.dram_tensor("c_us", [128, 128], F32, kind="ExternalInput").ap()
    c_io32 = nc.dram_tensor("c_io32", [128, E], F32, kind="ExternalInput").ap()
    c_on128 = nc.dram_tensor("c_on128", [128, 1], F32,
                             kind="ExternalInput").ap()
    c_onr = nc.dram_tensor("c_onr", [1, 128], F32, kind="ExternalInput").ap()
    c_onb = nc.dram_tensor("c_onb", [128, 1], BF16, kind="ExternalInput").ap()

    x3_o = nc.dram_tensor("x3", [TOK, D], F32, kind="ExternalOutput").ap()
    xf2_o = nc.dram_tensor("xf2", [TOK, D], BF16, kind="ExternalOutput").ap()
    ro2_o = nc.dram_tensor("rout2", [TOK, 12], F32, kind="ExternalOutput").ap()
    ct2_o = nc.dram_tensor("counts2", [1, E], F32,
                           kind="ExternalOutput").ap()

    wg_bf_d = nc.dram_tensor("wg_bf_d4", [E, D], BF16, kind="Internal").ap()

    ctx = contextlib.ExitStack()
    with ctx:
        sb = lambda n, s, d: ctx.enter_context(nc.sbuf_tensor(n, s, d))
        pt = lambda n, s, d: ctx.enter_context(nc.psum_tensor(n, s, d))
        qbt = sb("qbt", [128, 2 * D], BF16)
        kbt = sb("kbt", [128, 2 * D], BF16)
        vbt = sb("vbt", [128, 2 * D], BF16)
        qte = sb("qte", [64, 8 * 128], BF16)
        qto = sb("qto", [64, 8 * 128], BF16)
        kte = sb("kte", [64, 8 * 128], BF16)
        kto = sb("kto", [64, 8 * 128], BF16)
        x2t = sb("x2t", [128, 2 * D], F32)
        wpsb = sb("wpsb", [128, 8 * D], BF16)
        wgt = sb("wgt", [128, 8 * E], BF16)
        wtile = sb("wtile", [128, D], BF16)
        ssb = sb("ssb", [64, D], F32)
        sbf = sb("sbf", [64, D], BF16)
        zrow = sb("zrow", [1, D], F32)
        zrbf = sb("zrbf", [1, D], BF16)
        sp3s = sb("sp3s", [64, 3 * D], F32)
        zp3s = sb("zp3s", [1, 3 * D], F32)
        atm = sb("atm", [128, 16 * 128], BF16)
        prodb = sb("prodb", [128, D], F32)
        dent = sb("dent", [128, 32], F32)
        ysb = sb("ysb", [128, D], BF16)
        ytsb = sb("ytsb", [128, D], BF16)
        x3t = sb("x3t", [128, 2 * D], F32)
        xm2 = sb("xm2", [128, D], F32)
        sq2 = sb("sq2", [128, D], F32)
        stat = sb("stat", [128, 8], F32)
        xfb = sb("xfb", [128, 2 * D], BF16)
        xffm = sb("xffm", [128, D], BF16)
        lg = sb("lg", [128, E], F32)
        sc_t = sb("sc_t", [128, E], F32)
        gs = sb("gs", [128, NGRP], F32)
        kg = sb("kg", [128, NGRP], F32)
        kg32 = sb("kg32", [128, E], F32)
        msk = sb("msk", [128, E], F32)
        selJ = sb("selJ", [128, 4 * E], F32)
        tmp32 = sb("tmp32", [128, E], F32)
        rsm = sb("rsm", [128, 4], F32)
        g8 = sb("g8", [128, 8], F32)
        m8 = sb("m8", [128, 8], F32)
        eidx = sb("eidx", [128, 8], mybir.dt.uint32)
        eidf = sb("eidf", [128, 8], F32)
        post = sb("post", [128, E], F32)
        routt = sb("routt", [128, 2 * 12], F32)
        runr = sb("runr", [1, E], F32)
        idsb = sb("idsb", [128, 128], BF16)
        trusb = sb("trusb", [128, 128], F32)
        trubb = sb("trubb", [128, 128], BF16)
        onrb = sb("onrb", [1, 128], BF16)
        ussb = sb("ussb", [128, 128], F32)
        iosb = sb("iosb", [128, E], F32)
        on1 = sb("on1", [128, 1], F32)
        onr = sb("onr", [1, 128], F32)
        onb = sb("onb", [128, 1], BF16)
        pb1 = pt("pb1", [128, 512], F32)
        pb2 = pt("pb2", [128, 512], F32)
        pk1 = pt("pk1", [128, 512], F32)
        pk2 = pt("pk2", [128, 512], F32)
        pb4 = pt("pb4", [128, 128], F32)
        pb5 = pt("pb5", [128, 128], F32)
        tp0 = pt("tp0", [128, 128], BF16)
        tp1 = pt("tp1", [128, 128], BF16)
        smallp = pk1[:, 0:128]

        s_cs = ctx.enter_context(nc.semaphore("s_cs"))
        s_wp = ctx.enter_context(nc.semaphore("s_wp"))
        s_cst = ring_sems(ctx, nc, "s_cst", 2)
        s_wst = ctx.enter_context(nc.semaphore("s_wst"))
        s_wt = ctx.enter_context(nc.semaphore("s_wt"))
        s_qb = ring_sems(ctx, nc, "s_qb", 2)
        s_kb = ring_sems(ctx, nc, "s_kb", 2)
        s_vb = ring_sems(ctx, nc, "s_vb", 2)
        s_x2 = ring_sems(ctx, nc, "s_x2", 2)
        s_at = ctx.enter_context(nc.semaphore("s_at"))
        s_ate = ctx.enter_context(nc.semaphore("s_ate"))
        s_kc = ctx.enter_context(nc.semaphore("s_kc"))
        s_dche = ctx.enter_context(nc.semaphore("s_dche"))
        s_zu = ctx.enter_context(nc.semaphore("s_zu"))
        s_zue = ctx.enter_context(nc.semaphore("s_zue"))
        s_num = ctx.enter_context(nc.semaphore("s_num"))
        s_ye = ctx.enter_context(nc.semaphore("s_ye"))
        s_su1 = ctx.enter_context(nc.semaphore("s_su1"))
        s_se = ctx.enter_context(nc.semaphore("s_se"))
        s_sbf = ctx.enter_context(nc.semaphore("s_sbf"))
        s_tp = ctx.enter_context(nc.semaphore("s_tp"))
        s_tpe = ctx.enter_context(nc.semaphore("s_tpe"))
        s_proj = ctx.enter_context(nc.semaphore("s_proj"))
        s_x3 = ctx.enter_context(nc.semaphore("s_x3"))
        s_x3e = ctx.enter_context(nc.semaphore("s_x3e"))
        s_ln = ctx.enter_context(nc.semaphore("s_ln"))
        s_sc = ctx.enter_context(nc.semaphore("s_sc"))
        s_xfb = ctx.enter_context(nc.semaphore("s_xfb"))
        s_gt = ctx.enter_context(nc.semaphore("s_gt"))
        s_rt = ctx.enter_context(nc.semaphore("s_rt"))
        s_ps = ctx.enter_context(nc.semaphore("s_ps"))
        s_rr = ctx.enter_context(nc.semaphore("s_rr"))
        s_ro = ctx.enter_context(nc.semaphore("s_ro"))
        s_sx3 = ring_sems(ctx, nc, "s_sx3", 2)
        s_sxf = ring_sems(ctx, nc, "s_sxf", 2)
        s_sro = ring_sems(ctx, nc, "s_sro", 2)
        s_out = ctx.enter_context(nc.semaphore("s_out"))
        vsync = SelfSync(ctx, nc, "vsync")
        asns = SelfSync(ctx, nc, "asns")
        block = ctx.enter_context(nc.Block())

        @block.gpsimd
        def _(g):
            for r in range(8):
                g.dma_start(out=wpsb[:, r * D:(r + 1) * D],
                            in_=wp_i[r * 128:(r + 1) * 128, :]
                            ).then_inc(s_wp, 16)
            g.dma_start(out=wtile[0:E, 0:D], in_=wg).then_inc(s_cst[0], 16)

        @block.sync
        def _(s):
            for dst, src in ((idsb[:, :], c_id), (trubb[:, :], c_trub),
                             (onrb[:, :], c_onrb),
                             (trusb[:, :], c_tru), (ussb[:, :], c_us),
                             (iosb[:, :], c_io32), (on1[:, :], c_on128),
                             (onr[:, :], c_onr), (onb[:, :], c_onb),
                             (sp3s[:, :], sp3_i), (zp3s[:, :], zp3_i)):
                s.dma_start(out=dst, in_=src).then_inc(s_cs, 16)
            ring_wait(s, s_cst, 0)
            s.dma_start(out=wg_bf_d, in_=wtile[0:E, 0:D]).then_inc(s_wst, 16)
            s.wait_ge(s_wst, 16)
            for k in range(8):
                s.dma_start(out=wgt[:, k * E:(k + 1) * E],
                            in_=wg_bf_d[0:E, k * 128:(k + 1) * 128],
                            transpose=True).then_inc(s_wt, 16)
            for i in range(NT):
                if i >= 2:
                    s.wait_ge(s_dche, i - 1)
                    s.wait_ge(s_num, 16 * (i - 1))
                    s.wait_ge(s_zu, i - 1)
                    s.wait_ge(s_x3, i - 1)
                sm, per = ring_inc(s_qb, i)
                s.dma_start(out=qbt[:, (i % 2) * D:(i % 2 + 1) * D],
                            in_=qb_i[i * 128:(i + 1) * 128, :]
                            ).then_inc(sm, per)
                sm, per = ring_inc(s_kb, i)
                s.dma_start(out=kbt[:, (i % 2) * D:(i % 2 + 1) * D],
                            in_=kb_i[i * 128:(i + 1) * 128, :]
                            ).then_inc(sm, per)
                sm, per = ring_inc(s_vb, i)
                s.dma_start(out=vbt[:, (i % 2) * D:(i % 2 + 1) * D],
                            in_=vb_i[i * 128:(i + 1) * 128, :]
                            ).then_inc(sm, per)
                sm, per = ring_inc(s_x2, i)
                s.dma_start(out=x2t[:, (i % 2) * D:(i % 2 + 1) * D],
                            in_=x2_i[i * 128:(i + 1) * 128, :]
                            ).then_inc(sm, per)
                s.wait_ge(s_x3, i + 1)
                sm, per = ring_inc(s_sx3, i)
                s.dma_start(out=x3_o[i * 128:(i + 1) * 128, :],
                            in_=x3t[:, (i % 2) * D:(i % 2 + 1) * D]
                            ).then_inc(sm, per)
                s.wait_ge(s_xfb, i + 1)
                sm, per = ring_inc(s_sxf, i)
                s.dma_start(out=xf2_o[i * 128:(i + 1) * 128, :],
                            in_=xfb[:, (i % 2) * D:(i % 2 + 1) * D]
                            ).then_inc(sm, per)
                s.wait_ge(s_ro, i + 1)
                sm, per = ring_inc(s_sro, i)
                s.dma_start(out=ro2_o[i * 128:(i + 1) * 128, :],
                            in_=routt[:, (i % 2) * 12:(i % 2 + 1) * 12]
                            ).then_inc(sm, per)
            s.wait_ge(s_rr, NT)
            s.dma_start(out=ct2_o, in_=runr[:, :]).then_inc(s_out, 16)
            s.wait_ge(s_out, 16)
            ring_wait(s, s_sx3, NT - 1)
            ring_wait(s, s_sx3, NT - 2)
            ring_wait(s, s_sxf, NT - 1)
            ring_wait(s, s_sxf, NT - 2)
            ring_wait(s, s_sro, NT - 1)
            ring_wait(s, s_sro, NT - 2)

        @block.tensor
        def _(t):
            t.wait_ge(s_cs, 16 * 11)
            t.wait_ge(s_wp, 16 * 8)
            t.wait_ge(s_wt, 16 * 8)
            ntp = 0
            for i in range(NT):
                ring_wait(t, s_qb, i)
                ring_wait(t, s_kb, i)
                ring_wait(t, s_vb, i)
                qh = lambda h: (qte, qto)[h % 2][:, (h // 2) * 128:
                                                 (h // 2) * 128 + 128]
                kh = lambda h: (kte, kto)[h % 2][:, (h // 2) * 128:
                                                 (h // 2) * 128 + 128]
                kn = lambda h: kbt[:, (i % 2) * D + h * 64:
                                   (i % 2) * D + (h + 1) * 64]
                vn = lambda h: vbt[:, (i % 2) * D + h * 64:
                                   (i % 2) * D + (h + 1) * 64]
                for b in range(16):
                    src = (qbt if b < 8 else kbt)[
                        :, (i % 2) * D + (b % 8) * 128:
                        (i % 2) * D + (b % 8) * 128 + 128]
                    if ntp >= 2:
                        t.wait_ge(s_tpe, ntp - 1)
                    ins = t.transpose((tp0, tp1)[ntp % 2][:, :], src,
                                      idsb[:, :])
                    ins.then_inc(s_tp, 1)
                    ntp += 1
                t.wait_ge(s_tpe, ntp)
                if i >= 1:
                    t.wait_ge(s_zue, i)
                    t.wait_ge(s_rr, i)
                t.wait_ge(s_sbf, i + 1)
                for hh in range(2):
                    pk = (pk1, pk2)[hh]
                    sl = slice((i % 2) * D + hh * 512,
                               (i % 2) * D + hh * 512 + 512)
                    t.matmul(pk[:, :], trubb[:, :], kbt[:, sl],
                             start=True, stop=False)
                    ins = t.matmul(pk[:, :], onrb[:, :],
                                   zrbf[:, hh * 512:hh * 512 + 512],
                                   start=False, stop=True)
                ins.then_inc(s_kc, 1)
                if i >= 1:
                    t.wait_ge(s_x3e, i)
                for h in range(H):
                    if 16 * i + h >= 2:
                        t.wait_ge(s_ate, 16 * i + h - 1)
                    ins = t.matmul((pb1, pb2)[h % 2][:, 0:128],
                                   kh(h), qh(h), start=True, stop=True)
                    ins.then_inc(s_at, 1)
                t.wait_ge(s_ate, 16 * (i + 1))
                for h in range(H):
                    if 16 * i + h >= 2:
                        t.wait_ge(s_ye, 16 * i + h - 1)
                        t.wait_ge(s_se, 16 * i + h - 1)
                    pY = (pb4, pb5)[h % 2][:, 0:64]
                    pS = (pb4, pb5)[h % 2][0:64, 64:128]
                    t.matmul(pY, atm[:, h * 128:(h + 1) * 128], vn(h),
                             start=True, stop=False)
                    t.matmul(pY, qh(h), sbf[:, h * 64:(h + 1) * 64],
                             start=False, stop=True).then_inc(s_num, 1)
                    ins = t.matmul(pS, kn(h), vn(h), start=True,
                                   stop=True)
                    ins.then_inc(s_su1, 1)
                t.wait_ge(s_dche, i + 1)
                for hh in range(2):
                    sl = slice((i % 2) * D + hh * 512,
                               (i % 2) * D + hh * 512 + 512)
                    ins = t.matmul((pk1, pk2)[hh][0:1, :], onb[:, :],
                                   kbt[:, sl], start=True, stop=True)
                ins.then_inc(s_zu, 1)
                for c in range(8):
                    if ntp >= 2:
                        t.wait_ge(s_tpe, ntp - 1)
                    t.wait_ge(s_ye, 16 * i + 2 * c + 2)
                    ins = t.transpose((tp0, tp1)[ntp % 2][:, :],
                                      ysb[:, c * 128:(c + 1) * 128],
                                      idsb[:, :])
                    ins.then_inc(s_tp, 1)
                    ntp += 1
                t.wait_ge(s_tpe, ntp)
                for hh in range(2):
                    for c in range(8):
                        ins = t.matmul(
                            (pb1, pb2)[hh][:, :],
                            ytsb[:, c * 128:(c + 1) * 128],
                            wpsb[:, c * D + hh * 512:c * D + hh * 512 + 512],
                            start=(c == 0), stop=(c == 7))
                ins.then_inc(s_proj, 1)
                t.wait_ge(s_xfb, i + 1)
                for c in range(8):
                    if ntp >= 2:
                        t.wait_ge(s_tpe, ntp - 1)
                    ins = t.transpose((tp0, tp1)[ntp % 2][:, :],
                                      xfb[:, (i % 2) * D + c * 128:
                                          (i % 2) * D + c * 128 + 128],
                                      idsb[:, :])
                    ins.then_inc(s_tp, 1)
                    ntp += 1
                t.wait_ge(s_tpe, ntp)
                t.wait_ge(s_zue, i + 1)
                if i >= 1:
                    t.wait_ge(s_rr, i)
                for c in range(8):
                    ins = t.matmul(smallp[:, 0:E],
                                   xffm[:, c * 128:(c + 1) * 128],
                                   wgt[:, c * E:(c + 1) * E],
                                   start=(c == 0), stop=(c == 7))
                ins.then_inc(s_gt, 1)
                t.wait_ge(s_rt, i + 1)
                t.wait_ge(s_rr, i)
                t.matmul(smallp[:, 32:32 + E], ussb[:, :], kg32[:, :],
                         start=True, stop=False)
                t.matmul(smallp[:, 32:32 + E], onr[:, :], runr[:, :],
                         start=False, stop=True)
                ins2 = t.matmul(smallp[0:1, 64:64 + E], on1[:, :],
                                kg32[:, :], start=True, stop=True)
                ins2.then_inc(s_ps, 1)

        @block.vector
        def _(v):
            v.memset(stat[:, 5:6], EPS_LN)
            v.memset(runr[:, :], 0.0)
            v.wait_ge(s_cs, 16 * 11)
            v.tensor_tensor(ssb[:, :], sp3s[:, 0:D], sp3s[:, D:2 * D],
                            ALU.add)
            ins = v.tensor_tensor(ssb[:, :], ssb[:, :], sp3s[:, 2 * D:3 * D],
                                  ALU.add)
            vsync.after(v, ins)
            v.tensor_copy(sbf[:, :], ssb[:, :])
            v.tensor_tensor(zrow[:, :], zp3s[:, 0:D], zp3s[:, D:2 * D],
                            ALU.add)
            ins = v.tensor_tensor(zrow[:, :], zrow[:, :],
                                  zp3s[:, 2 * D:3 * D], ALU.add)
            vsync.after(v, ins)
            v.tensor_copy(zrbf[:, :], zrow[:, :]).then_inc(s_sbf, 1)
            ntp = 0
            for i in range(NT):
                for b in range(16):
                    v.wait_ge(s_tp, ntp + 1)
                    tpp = (tp0, tp1)[ntp % 2]
                    de, do = ((qte, qto) if b < 8 else (kte, kto))
                    v.tensor_copy(de[:, (b % 8) * 128:(b % 8) * 128 + 128],
                                  tpp[0:64, :])
                    v.tensor_copy(do[:, (b % 8) * 128:(b % 8) * 128 + 128],
                                  tpp[64:128, :]).then_inc(s_tpe, 1)
                    ntp += 1
                for h in range(H):
                    v.wait_ge(s_at, 16 * i + h + 1)
                    v.tensor_tensor(atm[:, h * 128:(h + 1) * 128],
                                    (pb1, pb2)[h % 2][:, 0:128],
                                    trusb[:, :], ALU.mult).then_inc(s_ate, 1)
                v.wait_ge(s_kc, i + 1)
                ring_wait(v, s_qb, i)
                v.tensor_tensor(prodb[:, 0:512],
                                qbt[:, (i % 2) * D:(i % 2) * D + 512],
                                pk1[:, :], ALU.mult)
                ins = v.tensor_tensor(prodb[:, 512:1024],
                                      qbt[:, (i % 2) * D + 512:
                                          (i % 2 + 1) * D],
                                      pk2[:, :], ALU.mult)
                vsync.after(v, ins)
                ins = v.reduce_sum(dent[:, 0:16],
                                   prodb[:, :].rearrange("p (h d) -> p h d",
                                                         d=DH),
                                   axis=AX.X)
                ins.then_inc(s_dche, 1)
                v.wait_ge(s_dche, i + 1)
                ins = v.tensor_scalar(dent[:, 0:16], dent[:, 0:16], EPS_AT,
                                      None, ALU.add)
                vsync.after(v, ins)
                ins = v.reciprocal(dent[:, 16:32], dent[:, 0:16])
                vsync.after(v, ins)
                for h in range(H):
                    pY = (pb4, pb5)[h % 2][:, 0:64]
                    pS = (pb4, pb5)[h % 2][0:64, 64:128]
                    v.wait_ge(s_num, 16 * i + h + 1)
                    v.tensor_scalar(ysb[:, h * 64:(h + 1) * 64], pY,
                                    dent[:, 16 + h:17 + h], None,
                                    ALU.mult).then_inc(s_ye, 1)
                    v.wait_ge(s_su1, 16 * i + h + 1)
                    v.tensor_tensor(ssb[:, h * 64:(h + 1) * 64],
                                    ssb[:, h * 64:(h + 1) * 64], pS,
                                    ALU.add).then_inc(s_se, 1)
                v.wait_ge(s_zu, i + 1)
                v.tensor_tensor(zrow[:, 0:512], zrow[:, 0:512],
                                pk1[0:1, :], ALU.add)
                ins = v.tensor_tensor(zrow[:, 512:1024], zrow[:, 512:1024],
                                      pk2[0:1, :], ALU.add)
                ins.then_inc(s_zue, 1)
                v.wait_ge(s_zue, i + 1)
                v.tensor_copy(zrbf[:, :], zrow[:, :])
                ins = v.tensor_copy(sbf[:, :], ssb[:, :])
                ins.then_inc(s_sbf, 1)
                for c in range(8):
                    v.wait_ge(s_tp, ntp + 1)
                    if i >= 1 and c == 0:
                        v.wait_ge(s_proj, i)
                    v.tensor_copy(ytsb[:, c * 128:(c + 1) * 128],
                                  (tp0, tp1)[ntp % 2][:, :]).then_inc(s_tpe, 1)
                    ntp += 1
                v.wait_ge(s_proj, i + 1)
                ring_wait(v, s_x2, i)
                ring_wait(v, s_sx3, i - 2)
                xo = x3t[:, (i % 2) * D:(i % 2 + 1) * D]
                x2s = x2t[:, (i % 2) * D:(i % 2 + 1) * D]
                for hh in range(2):
                    sl = slice(hh * 512, (hh + 1) * 512)
                    ins = v.tensor_tensor(xo[:, sl], (pb1, pb2)[hh][:, :],
                                          x2s[:, sl], ALU.add)
                ins.then_inc(s_x3, 1)
                v.engine_nop().then_inc(s_x3e, 1)
                v.wait_ge(s_x3, i + 1)
                ins = v.reduce_sum(stat[:, 0:1], xo, axis=AX.X)
                vsync.after(v, ins)
                ins = v.tensor_scalar(stat[:, 1:2], stat[:, 0:1], -RLN, None,
                                      ALU.mult)
                vsync.after(v, ins)
                v.tensor_scalar(xm2[:, :], xo, stat[:, 1:2], None,
                                ALU.add).then_inc(s_ln, 1)
                v.wait_ge(s_sc, 3 * i + 1)
                v.reduce_sum(stat[:, 2:3], sq2[:, :], axis=AX.X).then_inc(
                    s_ln, 1)
                v.wait_ge(s_sc, 3 * i + 2)
                ring_wait(v, s_sxf, i - 2)
                v.tensor_scalar(xfb[:, (i % 2) * D:(i % 2 + 1) * D],
                                xm2[:, :], stat[:, 4:5], None,
                                ALU.mult).then_inc(s_xfb, 1)
                for c in range(8):
                    v.wait_ge(s_tp, ntp + 1)
                    v.tensor_copy(xffm[:, c * 128:(c + 1) * 128],
                                  (tp0, tp1)[ntp % 2][:, :]).then_inc(s_tpe, 1)
                    ntp += 1
                v.wait_ge(s_gt, i + 1)
                ins = v.tensor_copy(lg[:, :], smallp[:, 0:E])
                vsync.after(v, ins)
                v.reduce_max(stat[:, 6:7], lg[:, :], axis=AX.X,
                             negate=True).then_inc(s_ln, 1)
                v.wait_ge(s_sc, 3 * i + 3)
                ins = v.reduce_sum(stat[:, 7:8], sc_t[:, :], axis=AX.X)
                vsync.after(v, ins)
                ins = v.reciprocal(stat[:, 6:7], stat[:, 7:8])
                vsync.after(v, ins)
                ins = v.tensor_scalar(sc_t[:, :], sc_t[:, :], stat[:, 6:7],
                                      None, ALU.mult)
                vsync.after(v, ins)
                ins = v.reduce_max(gs[:, :],
                                   sc_t[:, :].rearrange("p (g k) -> p g k",
                                                        k=4),
                                   axis=AX.X)
                vsync.after(v, ins)
                ins = v.max(g8[:, :], gs[:, :])
                vsync.after(v, ins)
                ins = v.tensor_scalar(kg[:, :], gs[:, :], g8[:, LG - 1:LG],
                                      None, ALU.is_ge)
                vsync.after(v, ins)
                for jj in range(4):
                    ins = v.tensor_copy(
                        kg32[:, :].rearrange("p (g k) -> p g k", k=4)[:, :, jj],
                        kg[:, :])
                vsync.after(v, ins)
                ins = v.tensor_scalar(msk[:, :], kg32[:, :], -1.0, None,
                                      ALU.add)
                vsync.after(v, ins)
                ins = v.scalar_tensor_tensor(msk[:, :], msk[:, :], 1e8,
                                             sc_t[:, :], ALU.mult, ALU.add)
                vsync.after(v, ins)
                ins = v.max(m8[:, :], msk[:, :])
                vsync.after(v, ins)
                ins = v.max_index(eidx[:, :], m8[:, :], msk[:, :])
                vsync.after(v, ins)
                ins = v.tensor_copy(eidf[:, :], eidx[:, :])
                vsync.after(v, ins)
                v.tensor_scalar(kg32[:, :], msk[:, :], m8[:, TOPK - 1:TOPK],
                                None, ALU.is_ge).then_inc(s_rt, 1)
                ins = v.reduce_sum(rsm[:, 2:3], m8[:, 0:TOPK], axis=AX.X)
                vsync.after(v, ins)
                ins = v.tensor_scalar(rsm[:, 2:3], rsm[:, 2:3], 1e-9, None,
                                      ALU.add)
                vsync.after(v, ins)
                ins = v.reciprocal(rsm[:, 3:4], rsm[:, 2:3])
                vsync.after(v, ins)
                ring_wait(v, s_sro, i - 2)
                v.tensor_scalar(routt[:, (i % 2) * 12 + 8:(i % 2) * 12 + 12],
                                m8[:, 0:TOPK], rsm[:, 3:4], None, ALU.mult)
                v.tensor_copy(routt[:, (i % 2) * 12:(i % 2) * 12 + TOPK],
                              eidf[:, 0:TOPK])
                for j in range(TOPK):
                    v.tensor_scalar(selJ[:, j * E:(j + 1) * E], iosb[:, :],
                                    eidf[:, j:j + 1], None, ALU.is_equal)
                v.wait_ge(s_ps, i + 1)
                ins = v.tensor_copy(post[:, :], smallp[:, 32:32 + E])
                vsync.after(v, ins)
                for j in range(TOPK):
                    ins = v.tensor_tensor(tmp32[:, :],
                                          selJ[:, j * E:(j + 1) * E],
                                          post[:, :], ALU.mult)
                    vsync.after(v, ins)
                    ins = v.reduce_sum(routt[:, (i % 2) * 12 + 4 + j:
                                             (i % 2) * 12 + 4 + j + 1],
                                       tmp32[:, :], axis=AX.X)
                    vsync.after(v, ins)
                v.engine_nop().then_inc(s_ro, 1)
                v.tensor_tensor(runr[:, :], runr[:, :], smallp[0:1, 64:64 + E],
                                ALU.add).then_inc(s_rr, 1)

        @block.scalar
        def _(a):
            for i in range(NT):
                a.wait_ge(s_ln, 3 * i + 1)
                a.activation(sq2[:, :], xm2[:, :], AF.Square).then_inc(s_sc, 1)
                a.wait_ge(s_ln, 3 * i + 2)
                a.activation(stat[:, 3:4], stat[:, 2:3], AF.Ln,
                             bias=stat[:, 5:6], scale=RLN)
                asns(a)
                a.activation(stat[:, 4:5], stat[:, 3:4], AF.Exp,
                             scale=-0.5).then_inc(s_sc, 1)
                a.wait_ge(s_ln, 3 * i + 3)
                a.activation(sc_t[:, :], lg[:, :], AF.Exp,
                             bias=stat[:, 6:7]).then_inc(s_sc, 1)
    return nc



_BUILT = {}


def _programs():
    if not _BUILT:
        _BUILT["l1"] = build_l1()
        _BUILT["l2"] = build_l2()
        _BUILT["l3"] = build_l3()
        _BUILT["l4"] = build_l4()
        _BUILT["l5"] = build_l3(second=True)
    return _BUILT


def kernel(**inputs):
    P = _programs()
    cores = list(range(8))
    x0_full = inputs["L0_attn_a_x"]

    ones_bf = np.ones((128, 1), dtype=ml_dtypes.bfloat16)
    ident = np.eye(128, dtype=ml_dtypes.bfloat16)
    us = np.triu(np.ones((128, 128), np.float32), 1)
    tru = np.triu(np.ones((128, 128), np.float32))
    trub = np.triu(np.ones((128, 128))).astype(ml_dtypes.bfloat16)
    io32 = np.tile(np.arange(32, dtype=np.float32), (128, 1))
    io28 = np.arange(OV_N, dtype=np.float32).reshape(1, OV_N)
    on128 = np.ones((128, 1), np.float32)
    onr = np.ones((1, 128), np.float32)
    onrb = np.ones((1, 128), ml_dtypes.bfloat16)
    on4 = np.ones((4, 128), np.float32)
    colbase = np.zeros((128, 24), np.float32)
    for cm in range(24):
        j, k = cm // 8, cm % 8
        colbase[:, cm] = j * 32 * 1024 + k * 128 + np.arange(128)
    bo_bc = np.tile(inputs["L0_attn_a_bo"].reshape(1, -1),
                    (128, 1)).astype(np.float32, copy=False)

    xsl = [np.ascontiguousarray(x0_full[c // 4,
                                        (c % 4) * TOK:(c % 4 + 1) * TOK])
           for c in cores]
    wcat = np.concatenate([inputs["L0_ffn_w1"].reshape(E * D, D),
                           inputs["L0_ffn_w3"].reshape(E * D, D),
                           inputs["L0_ffn_w2"].reshape(E * D, D)], axis=0)

    im1 = [{"x0": xsl[c], "wq": inputs["L0_attn_a_wq"],
            "wk": inputs["L0_attn_a_wk"], "wv": inputs["L0_attn_a_wv"],
            "c_ones_bf": ones_bf} for c in cores]
    r1 = run_bass_kernel_spmd(P["l1"], im1, core_ids=cores)

    im2 = []
    for c in cores:
        grp = range((c // 4) * 4, (c // 4) * 4 + 4)
        im2.append({
            "x0": xsl[c], "qf": r1.results[c]["qf"],
            "kv4": np.ascontiguousarray(np.concatenate(
                [r1.results[j]["kvp"] for j in grp], axis=1)),
            "ks4": np.ascontiguousarray(np.concatenate(
                [r1.results[j]["ksp"] for j in grp], axis=0)),
            "wo": inputs["L0_attn_a_wo"], "bo_bc": bo_bc,
            "wg": inputs["L0_ffn_wg"],
            "wcat": np.ascontiguousarray(wcat[c * 12288:(c + 1) * 12288]),
            "c_id": ident, "c_us": us,
            "c_io32": io32, "c_on128": on128, "c_onr": onr, "c_on4": on4,
        })
    r2 = run_bass_kernel_spmd(P["l2"], im2, core_ids=cores)

    counts_all = np.ascontiguousarray(np.concatenate(
        [r2.results[c]["counts"] for c in cores], axis=0))
    wbf_full = np.ascontiguousarray(np.concatenate(
        [r2.results[c]["wbfp"] for c in cores], axis=0))
    im3 = []
    for c in cores:
        pm = np.zeros((8, 1), np.float32)
        pm[:c, 0] = 1.0
        im3.append({
            "x1": r2.results[c]["x1"], "xf": r2.results[c]["xf"],
            "rout": r2.results[c]["rout"], "counts_all": counts_all,
            "mycounts": np.ascontiguousarray(counts_all[c:c + 1]),
            "prefmask": pm, "wbf": wbf_full,
            "sw1": inputs["L0_ffn_sw1"], "sw2": inputs["L0_ffn_sw2"],
            "sw3": inputs["L0_ffn_sw3"],
            "w_attn": inputs["L0_attn_b_w_attn"],
            "c_io32": io32, "c_io28": io28, "c_onr": onr,
            "c_colbase": colbase, "c_ones_bf": ones_bf,
        })
    r3 = run_bass_kernel_spmd(P["l3"], im3, core_ids=cores)

    z64 = np.zeros((64, D), np.float32)
    z1 = np.zeros((1, D), np.float32)
    im4 = []
    for c in cores:
        b, q = c // 4, c % 4
        preds = [b * 4 + qq for qq in range(q)]
        sps = [r3.results[j]["Sp"] for j in preds] + [z64] * (3 - len(preds))
        zps = [r3.results[j]["zp"] for j in preds] + [z1] * (3 - len(preds))
        im4.append({
            "x2": r3.results[c]["x2"], "qb": r3.results[c]["qb"],
            "kb": r3.results[c]["kb"], "vb": r3.results[c]["vb"],
            "sp3": np.ascontiguousarray(np.concatenate(sps, axis=1)),
            "zp3": np.ascontiguousarray(np.concatenate(zps, axis=1)),
            "wp": inputs["L0_attn_b_w_proj"], "wg": inputs["L0_ffn_wg"],
            "c_id": ident, "c_tru": tru, "c_trub": trub, "c_onrb": onrb,
            "c_us": us, "c_io32": io32, "c_on128": on128, "c_onr": onr,
            "c_onb": ones_bf,
        })
    r4 = run_bass_kernel_spmd(P["l4"], im4, core_ids=cores)

    counts2 = np.ascontiguousarray(np.concatenate(
        [r4.results[c]["counts2"] for c in cores], axis=0))
    im5 = []
    for c in cores:
        pm = np.zeros((8, 1), np.float32)
        pm[:c, 0] = 1.0
        im5.append({
            "x1": r4.results[c]["x3"], "xf": r4.results[c]["xf2"],
            "rout": r4.results[c]["rout2"], "counts_all": counts2,
            "mycounts": np.ascontiguousarray(counts2[c:c + 1]),
            "prefmask": pm, "wbf": wbf_full,
            "sw1": inputs["L0_ffn_sw1"], "sw2": inputs["L0_ffn_sw2"],
            "sw3": inputs["L0_ffn_sw3"],
            "c_io32": io32, "c_io28": io28, "c_onr": onr,
            "c_colbase": colbase, "c_ones_bf": ones_bf,
        })
    r5 = run_bass_kernel_spmd(P["l5"], im5, core_ids=cores)

    out = np.empty((2, 4 * TOK, D), np.float32)
    for c in cores:
        out[c // 4, (c % 4) * TOK:(c % 4 + 1) * TOK] = r5.results[c]["x2"]
    return out


# version 07
# speedup vs baseline: 1.1870x; 1.0334x over previous
import contextlib

import numpy as np
import ml_dtypes
import concourse.bass as bass
import concourse.mybir as mybir
from concourse.bass import IndirectOffsetOnAxis
from concourse.bass_utils import run_bass_kernel_spmd

F32 = mybir.dt.float32
BF16 = mybir.dt.bfloat16
I32 = mybir.dt.int32
AF = mybir.ActivationFunctionType
ALU = mybir.AluOpType
AX = mybir.AxisListType

TOK = 4096
D = 1024
H = 16
DH = 64
NT = TOK // 128
E = 32
TOPK = 4
NGRP = 8
LG = 4
CAP = 5120
EPS_LN = 1e-5
EPS_AT = 1e-6
BASE_C = 640
OV_GRAN = 512
OV_N = 14
NSLOT = E * BASE_C + OV_N * OV_GRAN
SINK = NSLOT
NROW = NSLOT + 128
OV_BASE = E * BASE_C
BIG = 1e30
RLN = 1.0 / D



def ring_sems(ctx, nc, name, n):
    return [ctx.enter_context(nc.semaphore(f"{name}{i}")) for i in range(n)]


def ring_inc(sems, idx, per=16):
    return sems[idx % len(sems)], per


def ring_wait(eng, sems, idx, per=16, mult=1):
    if idx < 0:
        return
    eng.wait_ge(sems[idx % len(sems)], per * mult * (idx // len(sems) + 1))


class SelfSync:

    def __init__(self, ctx, nc, name):
        self.sem = ctx.enter_context(nc.semaphore(name))
        self.n = 0

    def __call__(self, eng):
        self.n += 1
        nopf = getattr(eng, "engine_nop", None) or eng.nop
        nopf().then_inc(self.sem, 1)
        eng.wait_ge(self.sem, self.n)

    def after(self, eng, ins):
        self.n += 1
        ins.then_inc(self.sem, 1)
        eng.wait_ge(self.sem, self.n)



def build_l1():
    nc = bass.Bass("TRN2", target_bir_lowering=False,
                   detect_race_conditions=False)
    x0 = nc.dram_tensor("x0", [TOK, D], F32, kind="ExternalInput").ap()
    w_in = [nc.dram_tensor(nm, [D, D], F32, kind="ExternalInput").ap()
            for nm in ("wq", "wk", "wv")]
    ones_bf = nc.dram_tensor("c_ones_bf", [128, 1], BF16, kind="ExternalInput").ap()

    qf_o = nc.dram_tensor("qf", [TOK, D], BF16, kind="ExternalOutput").ap()
    kvp_o = nc.dram_tensor("kvp", [64, D], F32, kind="ExternalOutput").ap()
    ksp_o = nc.dram_tensor("ksp", [1, D], F32, kind="ExternalOutput").ap()

    xn_d = nc.dram_tensor("xn_d", [TOK, D], BF16, kind="Internal").ap()
    w_bf_d = [nc.dram_tensor(f"wbf_d{j}", [D, D], BF16, kind="Internal").ap()
              for j in range(3)]

    ctx = contextlib.ExitStack()
    with ctx:
        xin = ctx.enter_context(nc.sbuf_tensor("xin", [128, 2 * D], F32))
        xm = ctx.enter_context(nc.sbuf_tensor("xm", [128, D], F32))
        sq = ctx.enter_context(nc.sbuf_tensor("sq", [128, D], F32))
        stat = ctx.enter_context(nc.sbuf_tensor("stat", [128, 8], F32))
        xnbf = ctx.enter_context(nc.sbuf_tensor("xnbf", [128, 2 * D], BF16))
        wtile = ctx.enter_context(nc.sbuf_tensor("wtile", [128, 2 * D], BF16))
        wqt = ctx.enter_context(nc.sbuf_tensor("wqt", [128, 8192], BF16))
        wkt = ctx.enter_context(nc.sbuf_tensor("wkt", [128, 8192], BF16))
        wvt = ctx.enter_context(nc.sbuf_tensor("wvt", [128, 8192], BF16))
        xnfm = ctx.enter_context(nc.sbuf_tensor("xnfm", [128, 2 * 4096], BF16))
        qt = ctx.enter_context(nc.sbuf_tensor("qt", [128, 2 * D], BF16))
        kt = ctx.enter_context(nc.sbuf_tensor("kt", [128, 2 * D], BF16))
        vt = ctx.enter_context(nc.sbuf_tensor("vt", [128, 2 * D], BF16))
        ev1 = ctx.enter_context(nc.sbuf_tensor("ev1", [128, 512], F32))
        ev2 = ctx.enter_context(nc.sbuf_tensor("ev2", [128, 512], F32))
        ev3 = ctx.enter_context(nc.sbuf_tensor("ev3", [128, 512], F32))
        onesb = ctx.enter_context(nc.sbuf_tensor("onesb", [128, 1], BF16))
        kvs = ctx.enter_context(nc.sbuf_tensor("kvs", [64, D], F32))
        kss = ctx.enter_context(nc.sbuf_tensor("kss", [1, D], F32))
        pp0 = ctx.enter_context(nc.psum_tensor("pp0", [128, 512], F32))
        pp1 = ctx.enter_context(nc.psum_tensor("pp1", [128, 512], F32))
        kvp0 = ctx.enter_context(nc.psum_tensor("kvps0", [64, D], F32))
        kvp1 = ctx.enter_context(nc.psum_tensor("kvps1", [64, D], F32))
        ksp = ctx.enter_context(nc.psum_tensor("ksps", [1, D], F32))
        s_ld = ring_sems(ctx, nc, "s_ld", 2)
        s_cst = ring_sems(ctx, nc, "s_cst", 2)
        s_wst = ring_sems(ctx, nc, "s_wst", 2)
        s_one = ctx.enter_context(nc.semaphore("s_one"))
        s_wt = ctx.enter_context(nc.semaphore("s_wt"))
        s_xn = ring_sems(ctx, nc, "s_xn", 2)
        s_fm = ring_sems(ctx, nc, "s_fm", 2)
        s_ln = ctx.enter_context(nc.semaphore("s_ln"))
        s_sc = ctx.enter_context(nc.semaphore("s_sc"))
        s_pe = ctx.enter_context(nc.semaphore("s_pe"))
        s_ev = ctx.enter_context(nc.semaphore("s_ev"))
        s_ex = ctx.enter_context(nc.semaphore("s_ex"))
        s_qk = ctx.enter_context(nc.semaphore("s_qk"))
        s_kv = ctx.enter_context(nc.semaphore("s_kv"))
        s_kve = ctx.enter_context(nc.semaphore("s_kve"))
        s_kse = ctx.enter_context(nc.semaphore("s_kse"))
        s_qf = ring_sems(ctx, nc, "s_qf", 2)
        s_fin = ctx.enter_context(nc.semaphore("s_fin"))
        s_out = ctx.enter_context(nc.semaphore("s_out"))
        vsync = SelfSync(ctx, nc, "vsync")
        asns = SelfSync(ctx, nc, "asns")
        block = ctx.enter_context(nc.Block())
        wts = [wqt, wkt, wvt]
        NG = NT // 4

        @block.gpsimd
        def _(g):
            g.dma_start(out=onesb[:, :], in_=ones_bf).then_inc(s_one, 16)
            n = 0
            for j in range(3):
                for r in range(8):
                    ring_wait(g, s_wst, n - 2)
                    sm, per = ring_inc(s_cst, n)
                    g.dma_start(
                        out=wtile[:, (n % 2) * D:(n % 2 + 1) * D],
                        in_=w_in[j][r * 128:(r + 1) * 128, :],
                    ).then_inc(sm, per)
                    n += 1
            for i in range(NT):
                g.wait_ge(s_qk, i * 6 + 2)
                sm, per = ring_inc(s_qf, i)
                g.dma_start(
                    out=qf_o[i * 128:(i + 1) * 128, :],
                    in_=qt[:, (i % 2) * D:(i % 2 + 1) * D],
                ).then_inc(sm, per)
            g.wait_ge(s_kve, NT)
            g.wait_ge(s_kse, NT)
            g.dma_start(out=kvp_o, in_=kvs[:, :]).then_inc(s_out, 16)
            g.dma_start(out=ksp_o, in_=kss[:, :]).then_inc(s_out, 16)
            g.wait_ge(s_out, 32)
            ring_wait(g, s_qf, NT - 1)
            ring_wait(g, s_qf, NT - 2)

        @block.sync
        def _(s):
            n = 0
            for j in range(3):
                for r in range(8):
                    ring_wait(s, s_cst, n)
                    sm, per = ring_inc(s_wst, n)
                    s.dma_start(
                        out=w_bf_d[j][r * 128:(r + 1) * 128, :],
                        in_=wtile[:, (n % 2) * D:(n % 2 + 1) * D],
                    ).then_inc(sm, per)
                    n += 1
            for j in range(3):
                ring_wait(s, s_wst, 8 * (j + 1) - 1)
                ring_wait(s, s_wst, 8 * (j + 1) - 2)
                for k in range(8):
                    for h in range(2):
                        s.dma_start(
                            out=wts[j][:, k * 1024 + h * 512:k * 1024 + h * 512 + 512],
                            in_=w_bf_d[j][h * 512:(h + 1) * 512, k * 128:(k + 1) * 128],
                            transpose=True,
                        ).then_inc(s_wt, 16)
            for i in range(NT):
                if i >= 2:
                    s.wait_ge(s_ln, 3 * (i - 2) + 1)
                sm, per = ring_inc(s_ld, i)
                s.dma_start(
                    out=xin[:, (i % 2) * D:(i % 2 + 1) * D],
                    in_=x0[i * 128:(i + 1) * 128, :],
                ).then_inc(sm, per)
                s.wait_ge(s_ln, 3 * (i + 1))
                sm, per = ring_inc(s_xn, i)
                s.dma_start(
                    out=xn_d[i * 128:(i + 1) * 128, :],
                    in_=xnbf[:, (i % 2) * D:(i % 2 + 1) * D],
                ).then_inc(sm, per)
                if i % 4 == 3:
                    gi = i // 4
                    ring_wait(s, s_xn, i)
                    ring_wait(s, s_xn, i - 1)
                    if gi >= 2:
                        s.wait_ge(s_pe, 6 * 4 * (gi - 1))
                    sm, per = ring_inc(s_fm, gi)
                    for c in range(8):
                        s.dma_start(
                            out=xnfm[:, (gi % 2) * 4096 + c * 512:
                                     (gi % 2) * 4096 + c * 512 + 512],
                            in_=xn_d[gi * 512:(gi + 1) * 512, c * 128:(c + 1) * 128],
                            transpose=True,
                        ).then_inc(sm, per)

        @block.vector
        def _(v):
            v.memset(stat[:, 5:6], EPS_LN)
            v.memset(kvs[:, :], 0.0)
            v.memset(kss[:, :], 0.0)
            for gi in range(NT // 4):
                for ti in range(4):
                    i = gi * 4 + ti
                    xi = xin[:, (i % 2) * D:(i % 2 + 1) * D]
                    ring_wait(v, s_ld, i)
                    ins = v.reduce_sum(stat[:, 0:1], xi, axis=AX.X)
                    vsync.after(v, ins)
                    ins = v.tensor_scalar(stat[:, 1:2], stat[:, 0:1], -RLN,
                                          None, ALU.mult)
                    vsync.after(v, ins)
                    v.tensor_scalar(xm[:, :], xi, stat[:, 1:2], None,
                                    ALU.add).then_inc(s_ln, 1)
                    v.wait_ge(s_sc, 2 * i + 1)
                    v.reduce_sum(stat[:, 2:3], sq[:, :],
                                 axis=AX.X).then_inc(s_ln, 1)
                    v.wait_ge(s_sc, 2 * i + 2)
                    ring_wait(v, s_xn, i - 2)
                    v.tensor_scalar(xnbf[:, (i % 2) * D:(i % 2 + 1) * D],
                                    xm[:, :], stat[:, 4:5], None,
                                    ALU.mult).then_inc(s_ln, 1)
                for ti in range(4):
                    i = gi * 4 + ti
                    for pg in range(6):
                        npg = i * 6 + pg
                        ps = (pp0, pp1)[npg % 2]
                        v.wait_ge(s_pe, npg + 1)
                        hh = pg % 2
                        dofs = (i % 2) * D + hh * 512
                        if pg < 4:
                            if pg < 2:
                                ring_wait(v, s_qf, i - 2)
                            elif i >= 2:
                                v.wait_ge(s_kv, i - 1)
                            dst = qt if pg < 2 else kt
                            v.tensor_scalar_min(ev1[:, :], ps[:, :], 0.0)
                            v.tensor_scalar_max(ev2[:, :], ps[:, :],
                                                0.0).then_inc(s_ev, 1)
                            v.wait_ge(s_ex, i * 4 + pg + 1)
                            v.tensor_tensor(dst[:, dofs:dofs + 512],
                                            ev3[:, :], ev2[:, :],
                                            ALU.add).then_inc(s_qk, 1)
                        else:
                            if i >= 2:
                                v.wait_ge(s_kv, i - 1)
                            v.tensor_copy(vt[:, dofs:dofs + 512],
                                          ps[:, :]).then_inc(s_qk, 1)
                    v.wait_ge(s_kv, i + 1)
                    kvpi = (kvp0, kvp1)[i % 2]
                    v.tensor_tensor(kvs[:, :], kvs[:, :], kvpi[:, :],
                                    ALU.add).then_inc(s_kve, 1)
                    v.tensor_tensor(kss[:, :], kss[:, :], ksp[:, :],
                                    ALU.add).then_inc(s_kse, 1)

        @block.scalar
        def _(a):
            for gi in range(NT // 4):
                for ti in range(4):
                    i = gi * 4 + ti
                    a.wait_ge(s_ln, 3 * i + 1)
                    a.activation(sq[:, :], xm[:, :], AF.Square).then_inc(s_sc, 1)
                    a.wait_ge(s_ln, 3 * i + 2)
                    ins = a.activation(stat[:, 3:4], stat[:, 2:3], AF.Ln,
                                       bias=stat[:, 5:6], scale=RLN)
                    asns.after(a, ins)
                    a.activation(stat[:, 4:5], stat[:, 3:4], AF.Exp,
                                 scale=-0.5).then_inc(s_sc, 1)
                for ti in range(4):
                    i = gi * 4 + ti
                    for eg in range(4):
                        a.wait_ge(s_ev, i * 4 + eg + 1)
                        a.activation(ev3[:, :], ev1[:, :],
                                     AF.Exp).then_inc(s_ex, 1)

        @block.tensor
        def _(t):
            t.wait_ge(s_wt, 16 * 48)
            for gi in range(NG):
                ring_wait(t, s_fm, gi, mult=8)
                for ti in range(4):
                    i = gi * 4 + ti
                    mofs = (gi % 2) * 4096
                    for pg in range(6):
                        npg = i * 6 + pg
                        ps = (pp0, pp1)[npg % 2]
                        j = pg // 2
                        hh = pg % 2
                        if npg >= 2:
                            t.wait_ge(s_qk, npg - 1)
                        for c in range(8):
                            ins = t.matmul(
                                ps[:, :],
                                xnfm[:, mofs + c * 512 + ti * 128:
                                     mofs + c * 512 + ti * 128 + 128],
                                wts[j][:, c * 1024 + hh * 512:
                                       c * 1024 + hh * 512 + 512],
                                start=(c == 0), stop=(c == 7),
                            )
                        ins.then_inc(s_pe, 1)
                    t.wait_ge(s_qk, (i + 1) * 6)
                    if i >= 2:
                        t.wait_ge(s_kve, i - 1)
                    kvpi = (kvp0, kvp1)[i % 2]
                    for h in range(H):
                        t.matmul(
                            kvpi[:, h * 64:(h + 1) * 64],
                            kt[:, (i % 2) * D + h * 64:(i % 2) * D + h * 64 + 64],
                            vt[:, (i % 2) * D + h * 64:(i % 2) * D + h * 64 + 64],
                            start=True, stop=True,
                        )
                    if i >= 1:
                        t.wait_ge(s_kse, i)
                    for hh in range(2):
                        ins = t.matmul(
                            ksp[:, hh * 512:(hh + 1) * 512],
                            onesb[:, :],
                            kt[:, (i % 2) * D + hh * 512:(i % 2) * D + hh * 512 + 512],
                            start=True, stop=True,
                        )
                    ins.then_inc(s_kv, 1)
    return nc



def build_l2():
    nc = bass.Bass("TRN2", target_bir_lowering=False,
                   detect_race_conditions=False)
    x0 = nc.dram_tensor("x0", [TOK, D], F32, kind="ExternalInput").ap()
    qf_i = nc.dram_tensor("qf", [TOK, D], BF16, kind="ExternalInput").ap()
    kv4 = nc.dram_tensor("kv4", [64, 4 * D], F32, kind="ExternalInput").ap()
    ks4 = nc.dram_tensor("ks4", [4, D], F32, kind="ExternalInput").ap()
    wo = nc.dram_tensor("wo", [D, D], F32, kind="ExternalInput").ap()
    bo_bc_i = nc.dram_tensor("bo_bc", [128, D], F32, kind="ExternalInput").ap()
    wg = nc.dram_tensor("wg", [E, D], F32, kind="ExternalInput").ap()
    wcat = nc.dram_tensor("wcat", [3 * E * D // 8, D], F32,
                          kind="ExternalInput").ap()
    c_id = nc.dram_tensor("c_id", [128, 128], BF16, kind="ExternalInput").ap()
    c_us = nc.dram_tensor("c_us", [128, 128], F32, kind="ExternalInput").ap()
    c_io32 = nc.dram_tensor("c_io32", [128, E], F32, kind="ExternalInput").ap()
    c_on128 = nc.dram_tensor("c_on128", [128, 1], F32, kind="ExternalInput").ap()
    c_onr = nc.dram_tensor("c_onr", [1, 128], F32, kind="ExternalInput").ap()
    c_on4 = nc.dram_tensor("c_on4", [4, 128], F32, kind="ExternalInput").ap()

    x1_o = nc.dram_tensor("x1", [TOK, D], F32, kind="ExternalOutput").ap()
    xf_o = nc.dram_tensor("xf", [TOK, D], BF16, kind="ExternalOutput").ap()
    ro_o = nc.dram_tensor("rout", [TOK, 12], F32, kind="ExternalOutput").ap()
    ct_o = nc.dram_tensor("counts", [1, E], F32, kind="ExternalOutput").ap()
    wbfp_o = nc.dram_tensor("wbfp", [3 * E * D // 8, D], BF16,
                            kind="ExternalOutput").ap()

    wo_bf_d = nc.dram_tensor("wo_bf_d", [D, D], BF16, kind="Internal").ap()
    wg_bf_d = nc.dram_tensor("wg_bf_d", [E, D], BF16, kind="Internal").ap()

    ctx = contextlib.ExitStack()
    with ctx:
        sb = lambda n, s, d: ctx.enter_context(nc.sbuf_tensor(n, s, d))
        pt = lambda n, s, d: ctx.enter_context(nc.psum_tensor(n, s, d))
        xin = sb("xin", [128, 2 * D], F32)
        qin = sb("qin", [128, 2 * D], BF16)
        prodb = sb("prodb", [128, D], F32)
        den = sb("den", [128, 2 * H], F32)
        qfs = sb("qfs", [128, D], BF16)
        qfsfme = sb("qfsfme", [64, D], BF16)
        qfsfmo = sb("qfsfmo", [64, D], BF16)
        mrg = sb("mrg", [128, D], BF16)
        mrgfm = sb("mrgfm", [128, D], BF16)
        x1t = sb("x1t", [128, 2 * D], F32)
        xm2 = sb("xm2", [128, D], F32)
        sq2 = sb("sq2", [128, D], F32)
        stat = sb("stat", [128, 8], F32)
        xfb = sb("xfb", [128, 2 * D], BF16)
        xffm = sb("xffm", [128, D], BF16)
        lg = sb("lg", [128, E], F32)
        sc_t = sb("sc_t", [128, E], F32)
        gs = sb("gs", [128, NGRP], F32)
        kg = sb("kg", [128, NGRP], F32)
        kg32 = sb("kg32", [128, E], F32)
        msk = sb("msk", [128, E], F32)
        selJ = sb("selJ", [128, 4 * E], F32)
        tmp32 = sb("tmp32", [128, E], F32)
        wcol = sb("wcol", [128, 4], F32)
        rsm = sb("rsm", [128, 4], F32)
        g8 = sb("g8", [128, 8], F32)
        m8 = sb("m8", [128, 8], F32)
        eidx = sb("eidx", [128, 8], mybir.dt.uint32)
        eidf = sb("eidf", [128, 8], F32)
        post = sb("post", [128, E], F32)
        routt = sb("routt", [128, 2 * 12], F32)
        runr = sb("runr", [1, E], F32)
        wot = sb("wot", [128, 8192], BF16)
        wgt = sb("wgt", [128, 8 * E], BF16)
        wtile = sb("wtile", [128, 2 * D], BF16)
        idsb = sb("idsb", [128, 128], BF16)
        ussb = sb("ussb", [128, 128], F32)
        iosb = sb("iosb", [128, E], F32)
        on1 = sb("on1", [128, 1], F32)
        onr = sb("onr", [1, 128], F32)
        on4 = sb("on4", [4, 128], F32)
        bobc = sb("bobc", [128, D], F32)
        ksbc = sb("ksbc", [128, D], F32)
        ks4s = sb("ks4s", [4, D], F32)
        kv4s = sb("kv4s", [64, 4 * D], F32)
        kvsum = sb("kvsum", [64, D], F32)
        kvte = sb("kvte", [64, 512], BF16)
        kvto = sb("kvto", [64, 512], BF16)
        tp0 = pt("tp0", [128, 128], BF16)
        tp1 = pt("tp1", [128, 128], BF16)
        mg = pt("mg", [128, D], F32)
        op0 = pt("op0", [128, 512], F32)
        op1 = pt("op1", [128, 512], F32)
        smallp = pt("smallp", [128, 128], F32)
        s_ld = ring_sems(ctx, nc, "s_ld", 2)
        s_qi = ring_sems(ctx, nc, "s_qi", 2)
        s_cst = ring_sems(ctx, nc, "s_cst", 2)
        s_wst = ring_sems(ctx, nc, "s_wst", 2)
        s_sx1 = ring_sems(ctx, nc, "s_sx1", 2)
        s_sxf = ring_sems(ctx, nc, "s_sxf", 2)
        s_sro = ring_sems(ctx, nc, "s_sro", 2)
        s_wt = ctx.enter_context(nc.semaphore("s_wt"))
        s_cs2 = ctx.enter_context(nc.semaphore("s_cs2"))
        s_su = ctx.enter_context(nc.semaphore("s_su"))
        s_sup = ctx.enter_context(nc.semaphore("s_sup"))
        s_den = ctx.enter_context(nc.semaphore("s_den"))
        s_tp = ctx.enter_context(nc.semaphore("s_tp"))
        s_tpe = ctx.enter_context(nc.semaphore("s_tpe"))
        s_qkv = ctx.enter_context(nc.semaphore("s_qkv"))
        s_mev = ctx.enter_context(nc.semaphore("s_mev"))
        s_op = ctx.enter_context(nc.semaphore("s_op"))
        s_x1 = ctx.enter_context(nc.semaphore("s_x1"))
        s_ln = ctx.enter_context(nc.semaphore("s_ln"))
        s_sc = ctx.enter_context(nc.semaphore("s_sc"))
        s_xfb = ctx.enter_context(nc.semaphore("s_xfb"))
        s_gt = ctx.enter_context(nc.semaphore("s_gt"))
        s_rt = ctx.enter_context(nc.semaphore("s_rt"))
        s_ps = ctx.enter_context(nc.semaphore("s_ps"))
        s_rr = ctx.enter_context(nc.semaphore("s_rr"))
        s_ro = ctx.enter_context(nc.semaphore("s_ro"))
        s_out = ctx.enter_context(nc.semaphore("s_out"))
        vsync = SelfSync(ctx, nc, "vsync")
        asns = SelfSync(ctx, nc, "asns")
        block = ctx.enter_context(nc.Block())

        @block.gpsimd
        def _(g):
            n = 0
            for r in range(8):
                ring_wait(g, s_wst, n - 2)
                sm, per = ring_inc(s_cst, n)
                g.dma_start(out=wtile[:, (n % 2) * D:(n % 2 + 1) * D],
                            in_=wo[r * 128:(r + 1) * 128, :]).then_inc(sm, per)
                n += 1
            ring_wait(g, s_wst, n - 2)
            sm, per = ring_inc(s_cst, n)
            g.dma_start(out=wtile[0:E, (n % 2) * D:(n % 2 + 1) * D],
                        in_=wg).then_inc(sm, per)
            n += 1
            for r in range(96):
                ring_wait(g, s_wst, n - 2)
                sm, per = ring_inc(s_cst, n)
                g.dma_start(out=wtile[:, (n % 2) * D:(n % 2 + 1) * D],
                            in_=wcat[r * 128:(r + 1) * 128, :]
                            ).then_inc(sm, per)
                n += 1

        @block.sync
        def _(s):
            s.dma_start(out=idsb[:, :], in_=c_id).then_inc(s_cs2, 16)
            s.dma_start(out=ussb[:, :], in_=c_us).then_inc(s_cs2, 16)
            s.dma_start(out=iosb[:, :], in_=c_io32).then_inc(s_cs2, 16)
            s.dma_start(out=on1[:, :], in_=c_on128).then_inc(s_cs2, 16)
            s.dma_start(out=onr[:, :], in_=c_onr).then_inc(s_cs2, 16)
            s.dma_start(out=on4[:, :], in_=c_on4).then_inc(s_cs2, 16)
            s.dma_start(out=bobc[:, :], in_=bo_bc_i).then_inc(s_cs2, 16)
            s.dma_start(out=ks4s[:, :], in_=ks4).then_inc(s_cs2, 16)
            s.dma_start(out=kv4s[:, :], in_=kv4).then_inc(s_cs2, 16)
            n = 0
            for r in range(8):
                ring_wait(s, s_cst, n)
                sm, per = ring_inc(s_wst, n)
                s.dma_start(out=wo_bf_d[r * 128:(r + 1) * 128, :],
                            in_=wtile[:, (n % 2) * D:(n % 2 + 1) * D]
                            ).then_inc(sm, per)
                n += 1
            ring_wait(s, s_cst, n)
            sm, per = ring_inc(s_wst, n)
            s.dma_start(out=wg_bf_d,
                        in_=wtile[0:E, (n % 2) * D:(n % 2 + 1) * D]
                        ).then_inc(sm, per)
            n += 1
            ring_wait(s, s_wst, n - 1)
            ring_wait(s, s_wst, n - 2)
            for k in range(8):
                for h in range(2):
                    s.dma_start(
                        out=wot[:, k * 1024 + h * 512:k * 1024 + h * 512 + 512],
                        in_=wo_bf_d[h * 512:(h + 1) * 512, k * 128:(k + 1) * 128],
                        transpose=True).then_inc(s_wt, 16)
            for k in range(8):
                s.dma_start(out=wgt[:, k * E:(k + 1) * E],
                            in_=wg_bf_d[0:E, k * 128:(k + 1) * 128],
                            transpose=True).then_inc(s_wt, 16)
            for i in range(NT):
                if i >= 2:
                    s.wait_ge(s_x1, i - 1)
                    s.wait_ge(s_den, i - 1)
                sm, per = ring_inc(s_ld, i)
                s.dma_start(out=xin[:, (i % 2) * D:(i % 2 + 1) * D],
                            in_=x0[i * 128:(i + 1) * 128, :]).then_inc(sm, per)
                sm, per = ring_inc(s_qi, i)
                s.dma_start(out=qin[:, (i % 2) * D:(i % 2 + 1) * D],
                            in_=qf_i[i * 128:(i + 1) * 128, :]).then_inc(sm, per)
                s.wait_ge(s_x1, i + 1)
                sm, per = ring_inc(s_sx1, i)
                s.dma_start(out=x1_o[i * 128:(i + 1) * 128, :],
                            in_=x1t[:, (i % 2) * D:(i % 2 + 1) * D]
                            ).then_inc(sm, per)
                s.wait_ge(s_xfb, i + 1)
                sm, per = ring_inc(s_sxf, i)
                s.dma_start(out=xf_o[i * 128:(i + 1) * 128, :],
                            in_=xfb[:, (i % 2) * D:(i % 2 + 1) * D]
                            ).then_inc(sm, per)
                s.wait_ge(s_ro, i + 1)
                sm, per = ring_inc(s_sro, i)
                s.dma_start(out=ro_o[i * 128:(i + 1) * 128, :],
                            in_=routt[:, (i % 2) * 12:(i % 2 + 1) * 12]
                            ).then_inc(sm, per)
            s.wait_ge(s_rr, NT)
            s.dma_start(out=ct_o, in_=runr[:, :]).then_inc(s_out, 16)
            s.wait_ge(s_out, 16)
            ring_wait(s, s_sx1, NT - 1)
            ring_wait(s, s_sx1, NT - 2)
            ring_wait(s, s_sxf, NT - 1)
            ring_wait(s, s_sxf, NT - 2)
            ring_wait(s, s_sro, NT - 1)
            ring_wait(s, s_sro, NT - 2)
            ring_wait(s, s_wst, 96 + 9 - 1)
            ring_wait(s, s_wst, 96 + 9 - 2)
            s.wait_ge(s_out, 16)

        @block.tensor
        def _(t):
            t.wait_ge(s_cs2, 16 * 9)
            for hh in range(2):
                t.matmul((op0, op1)[hh][:, :], on4[:, :],
                         ks4s[:, hh * 512:(hh + 1) * 512],
                         start=True, stop=True).then_inc(s_sup, 1)
            t.wait_ge(s_wt, 16 * 24)
            ntp = 0
            for i in range(NT):
                t.wait_ge(s_den, i + 1)
                for c in range(8):
                    if ntp >= 2:
                        t.wait_ge(s_tpe, ntp - 1)
                    ins = t.transpose((tp0, tp1)[ntp % 2][:, :],
                                      qfs[:, c * 128:(c + 1) * 128], idsb[:, :])
                    ins.then_inc(s_tp, 1)
                    ntp += 1
                t.wait_ge(s_tpe, ntp)
                if i >= 1:
                    t.wait_ge(s_mev, i)
                for h in range(H):
                    qsrc = qfsfme if h % 2 == 0 else qfsfmo
                    ksrc = kvte if h % 2 == 0 else kvto
                    ins = t.matmul(
                        mg[:, h * 64:(h + 1) * 64],
                        qsrc[:, (h // 2) * 128:(h // 2) * 128 + 128],
                        ksrc[:, (h // 2) * 64:(h // 2) * 64 + 64],
                        start=True, stop=True)
                ins.then_inc(s_qkv, 1)
                t.wait_ge(s_mev, i + 1)
                for c in range(8):
                    if ntp >= 2:
                        t.wait_ge(s_tpe, ntp - 1)
                    ins = t.transpose((tp0, tp1)[ntp % 2][:, :],
                                      mrg[:, c * 128:(c + 1) * 128], idsb[:, :])
                    ins.then_inc(s_tp, 1)
                    ntp += 1
                t.wait_ge(s_tpe, ntp)
                if i >= 1:
                    t.wait_ge(s_x1, i)
                else:
                    t.wait_ge(s_su, 1)
                for hh in range(2):
                    for c in range(8):
                        ins = t.matmul(
                            (op0, op1)[hh][:, :],
                            mrgfm[:, c * 128:(c + 1) * 128],
                            wot[:, c * 1024 + hh * 512:c * 1024 + hh * 512 + 512],
                            start=(c == 0), stop=(c == 7))
                ins.then_inc(s_op, 1)
                t.wait_ge(s_xfb, i + 1)
                for c in range(8):
                    if ntp >= 2:
                        t.wait_ge(s_tpe, ntp - 1)
                    ins = t.transpose((tp0, tp1)[ntp % 2][:, :],
                                      xfb[:, (i % 2) * D + c * 128:
                                          (i % 2) * D + c * 128 + 128],
                                      idsb[:, :])
                    ins.then_inc(s_tp, 1)
                    ntp += 1
                t.wait_ge(s_tpe, ntp)
                if i >= 1:
                    t.wait_ge(s_rr, i)
                for c in range(8):
                    ins = t.matmul(smallp[:, 0:E],
                                   xffm[:, c * 128:(c + 1) * 128],
                                   wgt[:, c * E:(c + 1) * E],
                                   start=(c == 0), stop=(c == 7))
                ins.then_inc(s_gt, 1)
                t.wait_ge(s_rt, i + 1)
                t.wait_ge(s_rr, i)
                t.matmul(smallp[:, 32:32 + E], ussb[:, :], kg32[:, :],
                         start=True, stop=False)
                t.matmul(smallp[:, 32:32 + E], onr[:, :], runr[:, :],
                         start=False, stop=True)
                ins2 = t.matmul(smallp[0:1, 64:64 + E], on1[:, :],
                                kg32[:, :], start=True, stop=True)
                ins2.then_inc(s_ps, 1)

        @block.vector
        def _(v):
            v.memset(stat[:, 5:6], EPS_LN)
            v.memset(runr[:, :], 0.0)
            v.wait_ge(s_cs2, 16 * 9)
            v.tensor_tensor(kvsum[:, :], kv4s[:, 0:D], kv4s[:, D:2 * D], ALU.add)
            v.tensor_tensor(kvsum[:, :], kvsum[:, :], kv4s[:, 2 * D:3 * D], ALU.add)
            ins = v.tensor_tensor(kvsum[:, :], kvsum[:, :],
                                  kv4s[:, 3 * D:4 * D], ALU.add)
            vsync.after(v, ins)
            v.tensor_copy(
                kvte[:, :].rearrange("p (g d) -> p g d", d=64),
                kvsum[:, :].rearrange("p (g two d) -> p g two d", two=2, d=64)[:, :, 0, :])
            v.tensor_copy(
                kvto[:, :].rearrange("p (g d) -> p g d", d=64),
                kvsum[:, :].rearrange("p (g two d) -> p g two d", two=2, d=64)[:, :, 1, :])
            v.wait_ge(s_sup, 2)
            v.tensor_copy(ksbc[:, 0:512], op0[:, :])
            v.tensor_copy(ksbc[:, 512:1024], op1[:, :]).then_inc(s_su, 1)
            ntp = 0
            for i in range(NT):
                qi = qin[:, (i % 2) * D:(i % 2 + 1) * D]
                xi = xin[:, (i % 2) * D:(i % 2 + 1) * D]
                ring_wait(v, s_qi, i)
                ins = v.tensor_tensor(prodb[:, :], ksbc[:, :], qi, ALU.mult)
                vsync.after(v, ins)
                ins = v.reduce_sum(den[:, 0:H],
                                   prodb[:, :].rearrange("p (h d) -> p h d",
                                                         d=DH),
                                   axis=AX.X)
                vsync.after(v, ins)
                ins = v.tensor_scalar(den[:, 0:H], den[:, 0:H], EPS_AT, None,
                                      ALU.add)
                vsync.after(v, ins)
                ins = v.reciprocal(den[:, H:2 * H], den[:, 0:H])
                vsync.after(v, ins)
                for h in range(H):
                    ins = v.tensor_scalar(qfs[:, h * 64:(h + 1) * 64],
                                          qi[:, h * 64:(h + 1) * 64],
                                          den[:, H + h:H + h + 1], None,
                                          ALU.mult)
                ins.then_inc(s_den, 1)
                for c in range(8):
                    v.wait_ge(s_tp, ntp + 1)
                    tpp = (tp0, tp1)[ntp % 2]
                    v.tensor_copy(qfsfme[:, c * 128:(c + 1) * 128],
                                  tpp[0:64, :])
                    v.tensor_copy(qfsfmo[:, c * 128:(c + 1) * 128],
                                  tpp[64:128, :]).then_inc(s_tpe, 1)
                    ntp += 1
                v.wait_ge(s_qkv, i + 1)
                v.tensor_copy(mrg[:, :], mg[:, :]).then_inc(s_mev, 1)
                for c in range(8):
                    v.wait_ge(s_tp, ntp + 1)
                    v.tensor_copy(mrgfm[:, c * 128:(c + 1) * 128],
                                  (tp0, tp1)[ntp % 2][:, :]).then_inc(s_tpe, 1)
                    ntp += 1
                v.wait_ge(s_op, i + 1)
                ring_wait(v, s_ld, i)
                ring_wait(v, s_sx1, i - 2)
                xo = x1t[:, (i % 2) * D:(i % 2 + 1) * D]
                for hh in range(2):
                    ps = (op0, op1)[hh]
                    sl = slice(hh * 512, (hh + 1) * 512)
                    v.tensor_tensor(xo[:, sl], ps[:, :], bobc[:, sl], ALU.add)
                    ins = v.tensor_tensor(xo[:, sl], xo[:, sl], xi[:, sl],
                                          ALU.add)
                ins.then_inc(s_x1, 1)
                v.wait_ge(s_x1, i + 1)
                ins = v.reduce_sum(stat[:, 0:1], xo, axis=AX.X)
                vsync.after(v, ins)
                ins = v.tensor_scalar(stat[:, 1:2], stat[:, 0:1], -RLN, None,
                                      ALU.mult)
                vsync.after(v, ins)
                v.tensor_scalar(xm2[:, :], xo, stat[:, 1:2], None,
                                ALU.add).then_inc(s_ln, 1)
                v.wait_ge(s_sc, 3 * i + 1)
                v.reduce_sum(stat[:, 2:3], sq2[:, :], axis=AX.X).then_inc(s_ln, 1)
                v.wait_ge(s_sc, 3 * i + 2)
                ring_wait(v, s_sxf, i - 2)
                v.tensor_scalar(xfb[:, (i % 2) * D:(i % 2 + 1) * D],
                                xm2[:, :], stat[:, 4:5], None,
                                ALU.mult).then_inc(s_xfb, 1)
                for c in range(8):
                    v.wait_ge(s_tp, ntp + 1)
                    v.tensor_copy(xffm[:, c * 128:(c + 1) * 128],
                                  (tp0, tp1)[ntp % 2][:, :]).then_inc(s_tpe, 1)
                    ntp += 1
                v.wait_ge(s_gt, i + 1)
                ins = v.tensor_copy(lg[:, :], smallp[:, 0:E])
                vsync.after(v, ins)
                v.reduce_max(stat[:, 6:7], lg[:, :], axis=AX.X,
                             negate=True).then_inc(s_ln, 1)
                v.wait_ge(s_sc, 3 * i + 3)
                ins = v.reduce_sum(stat[:, 7:8], sc_t[:, :], axis=AX.X)
                vsync.after(v, ins)
                ins = v.reciprocal(stat[:, 6:7], stat[:, 7:8])
                vsync.after(v, ins)
                ins = v.tensor_scalar(sc_t[:, :], sc_t[:, :], stat[:, 6:7],
                                      None, ALU.mult)
                vsync.after(v, ins)
                ins = v.reduce_max(gs[:, :],
                                   sc_t[:, :].rearrange("p (g k) -> p g k",
                                                        k=4),
                                   axis=AX.X)
                vsync.after(v, ins)
                ins = v.max(g8[:, :], gs[:, :])
                vsync.after(v, ins)
                ins = v.tensor_scalar(kg[:, :], gs[:, :], g8[:, LG - 1:LG],
                                      None, ALU.is_ge)
                vsync.after(v, ins)
                for jj in range(4):
                    ins = v.tensor_copy(
                        kg32[:, :].rearrange("p (g k) -> p g k", k=4)[:, :, jj],
                        kg[:, :])
                vsync.after(v, ins)
                ins = v.tensor_scalar(msk[:, :], kg32[:, :], -1.0, None,
                                      ALU.add)
                vsync.after(v, ins)
                ins = v.scalar_tensor_tensor(msk[:, :], msk[:, :], 1e8,
                                             sc_t[:, :], ALU.mult, ALU.add)
                vsync.after(v, ins)
                ins = v.max(m8[:, :], msk[:, :])
                vsync.after(v, ins)
                ins = v.max_index(eidx[:, :], m8[:, :], msk[:, :])
                vsync.after(v, ins)
                ins = v.tensor_copy(eidf[:, :], eidx[:, :])
                vsync.after(v, ins)
                v.tensor_scalar(kg32[:, :], msk[:, :], m8[:, TOPK - 1:TOPK],
                                None, ALU.is_ge).then_inc(s_rt, 1)
                ins = v.reduce_sum(rsm[:, 2:3], m8[:, 0:TOPK], axis=AX.X)
                vsync.after(v, ins)
                ins = v.tensor_scalar(rsm[:, 2:3], rsm[:, 2:3], 1e-9, None,
                                      ALU.add)
                vsync.after(v, ins)
                ins = v.reciprocal(rsm[:, 3:4], rsm[:, 2:3])
                vsync.after(v, ins)
                ring_wait(v, s_sro, i - 2)
                v.tensor_scalar(routt[:, (i % 2) * 12 + 8:(i % 2) * 12 + 12],
                                m8[:, 0:TOPK], rsm[:, 3:4], None, ALU.mult)
                v.tensor_copy(routt[:, (i % 2) * 12:(i % 2) * 12 + TOPK],
                              eidf[:, 0:TOPK])
                for j in range(TOPK):
                    v.tensor_scalar(selJ[:, j * E:(j + 1) * E], iosb[:, :],
                                    eidf[:, j:j + 1], None, ALU.is_equal)
                v.wait_ge(s_ps, i + 1)
                ins = v.tensor_copy(post[:, :], smallp[:, 32:32 + E])
                vsync.after(v, ins)
                for j in range(TOPK):
                    ins = v.tensor_tensor(tmp32[:, :],
                                          selJ[:, j * E:(j + 1) * E],
                                          post[:, :], ALU.mult)
                    vsync.after(v, ins)
                    ins = v.reduce_sum(routt[:, (i % 2) * 12 + 4 + j:
                                             (i % 2) * 12 + 4 + j + 1],
                                       tmp32[:, :], axis=AX.X)
                    vsync.after(v, ins)
                v.engine_nop().then_inc(s_ro, 1)
                v.tensor_tensor(runr[:, :], runr[:, :], smallp[0:1, 64:64 + E],
                                ALU.add).then_inc(s_rr, 1)

        @block.scalar
        def _(a):
            for i in range(NT):
                a.wait_ge(s_ln, 3 * i + 1)
                a.activation(sq2[:, :], xm2[:, :], AF.Square).then_inc(s_sc, 1)
                a.wait_ge(s_ln, 3 * i + 2)
                a.activation(stat[:, 3:4], stat[:, 2:3], AF.Ln,
                             bias=stat[:, 5:6], scale=RLN)
                asns(a)
                a.activation(stat[:, 4:5], stat[:, 3:4], AF.Exp,
                             scale=-0.5).then_inc(s_sc, 1)
                a.wait_ge(s_ln, 3 * i + 3)
                a.activation(sc_t[:, :], lg[:, :], AF.Exp,
                             bias=stat[:, 6:7]).then_inc(s_sc, 1)
                for r3_ in range(3):
                    n_ = 9 + i * 3 + r3_
                    ring_wait(a, s_cst, n_)
                    sm, per = ring_inc(s_wst, n_)
                    a.dma_start(out=wbfp_o[(n_ - 9) * 128:(n_ - 8) * 128, :],
                                in_=wtile[:, (n_ % 2) * D:(n_ % 2 + 1) * D]
                                ).then_inc(sm, per)
    return nc




def l3_sched():
    gl = []
    for sgrp in range(8):
        gl.append((0, sgrp * 512, 512, -1))
    for e in range(E):
        gl.append((1, e * BASE_C, 512, e))
        gl.append((1, e * BASE_C + 512, 128, e))
    for og in range(OV_N):
        gl.append((2, OV_BASE + og * OV_GRAN, OV_GRAN, E + og))
    return gl


def build_l3(second=False):
    nc = bass.Bass("TRN2", target_bir_lowering=False,
                   detect_race_conditions=False)
    x1 = nc.dram_tensor("x1", [TOK, D], F32, kind="ExternalInput").ap()
    xf_i = nc.dram_tensor("xf", [TOK, D], BF16, kind="ExternalInput").ap()
    ro_i = nc.dram_tensor("rout", [TOK, 12], F32, kind="ExternalInput").ap()
    cts = nc.dram_tensor("counts_all", [8, E], F32, kind="ExternalInput").ap()
    myct = nc.dram_tensor("mycounts", [1, E], F32, kind="ExternalInput").ap()
    pmask = nc.dram_tensor("prefmask", [8, 1], F32, kind="ExternalInput").ap()
    sw1 = nc.dram_tensor("sw1", [D, D], F32, kind="ExternalInput").ap()
    sw2 = nc.dram_tensor("sw2", [D, D], F32, kind="ExternalInput").ap()
    sw3 = nc.dram_tensor("sw3", [D, D], F32, kind="ExternalInput").ap()
    if not second:
        wat_i = nc.dram_tensor("w_attn", [D, 3 * D], F32,
                               kind="ExternalInput").ap()
    c_io32 = nc.dram_tensor("c_io32", [128, E], F32, kind="ExternalInput").ap()
    c_io28 = nc.dram_tensor("c_io28", [1, OV_N], F32, kind="ExternalInput").ap()
    c_onr = nc.dram_tensor("c_onr", [1, 128], F32, kind="ExternalInput").ap()
    c_cb = nc.dram_tensor("c_colbase", [128, 24], F32, kind="ExternalInput").ap()
    c_onb = nc.dram_tensor("c_ones_bf", [128, 1], BF16, kind="ExternalInput").ap()

    x2_o = nc.dram_tensor("x2", [TOK, D], F32, kind="ExternalOutput").ap()
    wbf_o = nc.dram_tensor("wbf", [3 * E * D, D], BF16,
                           kind="ExternalInput").ap()
    if not second:
        qb_o = nc.dram_tensor("qb", [TOK, D], BF16,
                              kind="ExternalOutput").ap()
        kb_o = nc.dram_tensor("kb", [TOK, D], BF16,
                              kind="ExternalOutput").ap()
        vb_o = nc.dram_tensor("vb", [TOK, D], BF16,
                              kind="ExternalOutput").ap()
        sp_o = nc.dram_tensor("Sp", [64, D], F32, kind="ExternalOutput").ap()
        zp_o = nc.dram_tensor("zp", [1, D], F32, kind="ExternalOutput").ap()

    _l3kind = "ExternalOutput" if globals().get("L3_DEBUG") else "Internal"
    xe_d = nc.dram_tensor("xe_d", [NROW, D], BF16, kind=_l3kind).ap()
    so_d = nc.dram_tensor("so_d", [NROW, D], F32, kind=_l3kind).ap()
    z_d = nc.dram_tensor("z_d", [TOK, D], F32, kind=_l3kind).ap()
    if not second:
        xb_d = nc.dram_tensor("xb_d", [TOK, D], BF16, kind="Internal").ap()
    tab_d = nc.dram_tensor(
        "tab_d", [1, 4 * E], F32,
        kind="ExternalOutput" if globals().get("L3_DEBUG") else "Internal"
    ).ap()
    if globals().get("L3_DEBUG"):
        dst_dbg = nc.dram_tensor("dst_dbg", [TOK, 4], I32,
                                 kind="ExternalOutput").ap()
        tbt_dbg = nc.dram_tensor("tbt_dbg", [128, 16 * NT], F32,
                                 kind="ExternalOutput").ap()

    GL = l3_sched()
    NGR = len(GL)
    HB, SB_, nh, ns = [], [], 0, 0
    for (kind, base, n, u) in GL:
        HB.append(nh)
        SB_.append(ns)
        nh += 8
        ns += n // 128
    NHC, NST = nh, ns
    unit_last_group = {}
    for gi, (kind, base, n, u) in enumerate(GL):
        if u >= 0:
            unit_last_group[u] = gi
    NUNIT = E + OV_N

    ctx = contextlib.ExitStack()
    with ctx:
        sb = lambda n_, s, d: ctx.enter_context(nc.sbuf_tensor(n_, s, d))
        pt = lambda n_, s, d: ctx.enter_context(nc.psum_tensor(n_, s, d))
        wslab = sb("wslab", [128, 49152], BF16)
        xefm = sb("xefm", [128, 2 * 4096], BF16)
        hfm = sb("hfm", [128, 4096], BF16)
        hs_t = sb("hs_t", [128, 1024], F32)
        outb = sb("outb", [128, 2 * D], F32)
        rot = sb("rot", [128, 2 * 12], F32)
        xft = sb("xft", [128, 2 * D], BF16)
        tbt = sb("tbt", [128, 16], F32)
        ei32 = sb("ei32", [128, 4], I32)
        di32 = sb("di32", [128, 8 * NT], I32)
        dsc = sb("dsc", [128, 160], F32)
        row32 = sb("row32", [1, 256], F32)
        ov28 = sb("ov28", [1, 96], F32)
        ebc = sb("ebc", [128, OV_N], F32)
        bc96 = sb("bc96", [128, 96], F32)
        widxf = sb("widxf", [128, 24], F32)
        widx = sb("widx", [128, 2 * 24], I32)
        gt = sb("gt", [128, 2 * D], F32)
        x1t = sb("x1t", [128, D], F32)
        x2t = sb("x2t", [128, 2 * D], F32)
        xm3 = sb("xm3", [128, D], F32)
        sq3 = sb("sq3", [128, D], F32)
        stat = sb("stat", [128, 8], F32)
        xbt = sb("xbt", [128, 2 * D], BF16)
        xbfm = sb("xbfm", [128, 4096], BF16)
        qbt = sb("qbt", [128, 2 * D], BF16)
        kbt = sb("kbt", [128, 2 * D], BF16)
        vbt = sb("vbt", [128, 2 * D], BF16)
        ev1 = sb("ev1", [128, 512], F32)
        ev2 = sb("ev2", [128, 512], F32)
        ev3 = sb("ev3", [128, 512], F32)
        onesb = sb("onesb", [128, 1], BF16)
        io32s = sb("io32s", [128, E], F32)
        cbs = sb("cbs", [128, 24], F32)
        onrs = sb("onrs", [1, 128], F32)
        pms = sb("pms", [8, 128], F32)
        ctss = sb("ctss", [8, E], F32)
        mycs = sb("mycs", [1, E], F32)
        io28s = sb("io28s", [1, OV_N], F32)
        spac = sb("spac", [64, D], F32)
        zpac = sb("zpac", [1, D], F32)
        h1a = pt("h1a", [128, 512], F32)
        h1b = pt("h1b", [128, 512], F32)
        h3a = pt("h3a", [128, 512], F32)
        h3b = pt("h3b", [128, 512], F32)
        outp = pt("outp", [128, D], F32)
        smallp = pt("smallp", [128, 128], F32)
        s_cs = ctx.enter_context(nc.semaphore("s_cs"))
        s_onb = ctx.enter_context(nc.semaphore("s_onb"))
        s_zro = ctx.enter_context(nc.semaphore("s_zro"))
        s_xez = ctx.enter_context(nc.semaphore("s_xez"))
        s_swl = ctx.enter_context(nc.semaphore("s_swl"))
        s_sup = ctx.enter_context(nc.semaphore("s_sup"))
        s_su = ctx.enter_context(nc.semaphore("s_su"))
        s_tab = ctx.enter_context(nc.semaphore("s_tab"))
        if globals().get("L3_DEBUG"):
            s_tbd = ctx.enter_context(nc.semaphore("s_tbd"))
        s_rol = ring_sems(ctx, nc, "s_rol", 2)
        s_xfl = ring_sems(ctx, nc, "s_xfl", 2)
        s_ei = ctx.enter_context(nc.semaphore("s_ei"))
        s_tbg = ctx.enter_context(nc.semaphore("s_tbg"))
        s_dst = ctx.enter_context(nc.semaphore("s_dst"))
        s_sct = ring_sems(ctx, nc, "s_sct", 2)
        s_hm = ctx.enter_context(nc.semaphore("s_hm"))
        s_sil = ctx.enter_context(nc.semaphore("s_sil"))
        s_hev = ctx.enter_context(nc.semaphore("s_hev"))
        s_hfb = ctx.enter_context(nc.semaphore("s_hfb"))
        s_og = ctx.enter_context(nc.semaphore("s_og"))
        s_oev = ctx.enter_context(nc.semaphore("s_oev"))
        s_sod = ring_sems(ctx, nc, "s_sod", 2)
        s_wld = ring_sems(ctx, nc, "s_wld", 2)
        s_wldh = ring_sems(ctx, nc, "s_wldh", 2)
        s_wlda = ring_sems(ctx, nc, "s_wlda", 2)
        s_wldo = ring_sems(ctx, nc, "s_wldo", 2)
        s_wbs = ring_sems(ctx, nc, "s_wbs", 2)
        s_wix = ctx.enter_context(nc.semaphore("s_wix"))
        s_xel = ring_sems(ctx, nc, "s_xel", 2)
        s_cg2 = ctx.enter_context(nc.semaphore("s_cg2"))
        s_cgd = ctx.enter_context(nc.semaphore("s_cgd"))
        s_czl = ring_sems(ctx, nc, "s_czl", 2)
        s_x2 = ctx.enter_context(nc.semaphore("s_x2"))
        s_sx2 = ring_sems(ctx, nc, "s_sx2", 2)
        s_lnb = ctx.enter_context(nc.semaphore("s_lnb"))
        s_scb = ctx.enter_context(nc.semaphore("s_scb"))
        s_xbs = ring_sems(ctx, nc, "s_xbs", 2)
        s_bfm = ring_sems(ctx, nc, "s_bfm", 2)
        s_pb = ctx.enter_context(nc.semaphore("s_pb"))
        s_evb = ctx.enter_context(nc.semaphore("s_evb"))
        s_exb = ctx.enter_context(nc.semaphore("s_exb"))
        s_qkb = ctx.enter_context(nc.semaphore("s_qkb"))
        s_kvb = ctx.enter_context(nc.semaphore("s_kvb"))
        s_kve = ctx.enter_context(nc.semaphore("s_kve"))
        s_kse = ctx.enter_context(nc.semaphore("s_kse"))
        s_qs = ring_sems(ctx, nc, "s_qs", 2)
        s_ks = ring_sems(ctx, nc, "s_ks", 2)
        s_vs = ring_sems(ctx, nc, "s_vs", 2)
        s_watl = ctx.enter_context(nc.semaphore("s_watl"))
        s_fin = ctx.enter_context(nc.semaphore("s_fin"))
        s_out = ctx.enter_context(nc.semaphore("s_out"))
        vsync = SelfSync(ctx, nc, "vsync")
        asns = SelfSync(ctx, nc, "asns")
        block = ctx.enter_context(nc.Block())

        def wv(kind, p, k):
            return wslab[:, kind * 16384 + p * 8192 + k * 1024:
                         kind * 16384 + p * 8192 + (k + 1) * 1024]

        def watv(k, hh):
            return wslab[:, k * 3072 + hh * 512:k * 3072 + (hh + 1) * 512]

        NM = 8

        @block.gpsimd
        def _(g):
            g.dma_start(out=onesb[:, :], in_=c_onb).then_inc(s_onb, 16)
            for j, w in enumerate((sw1, sw3, sw2)):
                for k in range(8):
                    g.dma_start(out=wv(j, 0, k),
                                in_=w[k * 128:(k + 1) * 128, :]
                                ).then_inc(s_swl, 16)
            g.wait_ge(s_tab, 64)
            for i in range(NT):
                g.wait_ge(s_ei, i + 1)
                for j in range(4):
                    g.indirect_dma_start(
                        out=tbt[:, j * 4:(j + 1) * 4], out_offset=None,
                        in_=tab_d.rearrange("o (a b) -> (o a) b", b=4),
                        in_offset=IndirectOffsetOnAxis(ei32[:, j:j + 1], 0),
                    ).then_inc(s_tbg, 16)
                if globals().get("L3_DEBUG"):
                    g.dma_start(out=tbt_dbg[:, i * 16:(i + 1) * 16],
                                in_=tbt[:, 0:16]).then_inc(s_tbd, 16)
                g.wait_ge(s_dst, i + 1)
                ring_wait(g, s_xfl, i)
                if i == 0:
                    g.wait_ge(s_xez, 16 * (NROW // 128 + 1))
                ring_wait(g, s_sct, i - 2, mult=4)
                for j in range(4):
                    g.indirect_dma_start(
                        out=xe_d,
                        out_offset=IndirectOffsetOnAxis(
                            di32[:, i * 8 + j:i * 8 + j + 1], 0),
                        in_=xft[:, (i % 2) * D:(i % 2 + 1) * D],
                        in_offset=None,
                    ).then_inc(s_sct[i % 2], 16)
            for u in range(NUNIT):
                p = u % 2
                if u >= 2:
                    gl_ = unit_last_group[u - 2]
                    g.wait_ge(s_og, SB_[gl_] + GL[gl_][2] // 128)
                if u < 2:
                    g.wait_ge(s_og, 32)
                if u < E:
                    for j in range(3):
                        for k in range(8):
                            g.dma_start(
                                out=wv(j, p, k),
                                in_=wbf_o[j * E * D + u * D + k * 128:
                                          j * E * D + u * D + (k + 1) * 128,
                                          :],
                            ).then_inc(s_wld[u % 2], 16)
                else:
                    og = u - E
                    g.wait_ge(s_wix, og + 1)
                    for cm in range(24):
                        j, k = cm // 8, cm % 8
                        g.indirect_dma_start(
                            out=wv(j, p, k), out_offset=None,
                            in_=wbf_o,
                            in_offset=IndirectOffsetOnAxis(
                                widx[:, (og % 2) * 24 + cm:
                                     (og % 2) * 24 + cm + 1], 0),
                        ).then_inc(s_wldo[u % 2], 16)
            if not second:
                g.wait_ge(s_og, NST)
                for k in range(8):
                    for hh in range(3):
                        g.dma_start(
                            out=wslab[:, k * 3072 + hh * 1024:
                                      k * 3072 + (hh + 1) * 1024],
                            in_=wat_i[k * 128:(k + 1) * 128,
                                      hh * 1024:(hh + 1) * 1024],
                        ).then_inc(s_watl, 16)
            g.wait_ge(s_oev, NST)
            ring_wait(g, s_sod, NST - 1)
            ring_wait(g, s_sod, NST - 2)
            for i in range(NT):
                if i >= 1:
                    g.wait_ge(s_cg2, 2 * i)
                for j in range(2):
                    g.indirect_dma_start(
                        out=gt[:, j * D:(j + 1) * D], out_offset=None,
                        in_=so_d,
                        in_offset=IndirectOffsetOnAxis(
                            di32[:, i * 8 + j:i * 8 + j + 1], 0),
                    ).then_inc(s_cgd, 16)
                g.wait_ge(s_cg2, 2 * i + 1)
                for j in range(2, 4):
                    g.indirect_dma_start(
                        out=gt[:, (j - 2) * D:(j - 1) * D], out_offset=None,
                        in_=so_d,
                        in_offset=IndirectOffsetOnAxis(
                            di32[:, i * 8 + j:i * 8 + j + 1], 0),
                    ).then_inc(s_cgd, 16)
                if i % 4 == 3 and not second:
                    for jj in range(i - 3, i + 1):
                        g.wait_ge(s_qkb, jj * 6 + 2)
                        sm, per = ring_inc(s_qs, jj)
                        g.dma_start(out=qb_o[jj * 128:(jj + 1) * 128, :],
                                    in_=qbt[:, (jj % 2) * D:(jj % 2 + 1) * D]
                                    ).then_inc(sm, per)
                        g.wait_ge(s_qkb, jj * 6 + 4)
                        sm, per = ring_inc(s_ks, jj)
                        g.dma_start(out=kb_o[jj * 128:(jj + 1) * 128, :],
                                    in_=kbt[:, (jj % 2) * D:(jj % 2 + 1) * D]
                                    ).then_inc(sm, per)
                        g.wait_ge(s_qkb, jj * 6 + 6)
                        sm, per = ring_inc(s_vs, jj)
                        g.dma_start(out=vb_o[jj * 128:(jj + 1) * 128, :],
                                    in_=vbt[:, (jj % 2) * D:(jj % 2 + 1) * D]
                                    ).then_inc(sm, per)
            if globals().get("L3_DEBUG"):
                g.wait_ge(s_tbd, 16 * NT)

        @block.sync
        def _(s):
            for ap_in, dst in ((c_io32, io32s[:, :]), (c_io28, io28s[:, :]),
                               (c_onr, onrs[:, :]), (pmask, pms[:, 0:1]),
                               (cts, ctss[:, :]), (myct, mycs[:, :]),
                               (c_cb, cbs[:, :])):
                s.dma_start(out=dst, in_=ap_in).then_inc(s_cs, 16)
            s.wait_ge(s_zro, 2)
            s.dma_start(out=so_d[SINK:SINK + 128, :],
                        in_=x2t[:, 0:D]).then_inc(s_xez, 16)
            s.wait_ge(s_su, 1)
            with nc.allow_non_contiguous_dma(reason="32x4B strided table"):
                for c3 in range(4):
                    src = row32[0:1, c3 * 32:(c3 + 1) * 32]
                    s.dma_start(
                        out=tab_d.rearrange("o (a b) -> o a b", b=4)[:, :, c3],
                        in_=src).then_inc(s_tab, 16)
            for i in range(NT):
                if i % 4 == 0:
                    gi = i // 4
                    kind, base, n, u = GL[gi]
                    if gi >= 2:
                        s.wait_ge(s_og, SB_[gi - 1])
                    sm, per = ring_inc(s_xel, gi)
                    for c in range(8):
                        s.dma_start(
                            out=xefm[:, (gi % 2) * 4096 + c * n:
                                     (gi % 2) * 4096 + (c + 1) * n],
                            in_=xf_i[base:base + n, c * 128:(c + 1) * 128],
                            transpose=True).then_inc(sm, per)
                if i >= 2:
                    s.wait_ge(s_dst, i - 1)
                    ring_wait(s, s_sct, i - 2, mult=4)
                sm, per = ring_inc(s_rol, i)
                s.dma_start(out=rot[:, (i % 2) * 12:(i % 2 + 1) * 12],
                            in_=ro_i[i * 128:(i + 1) * 128, :]).then_inc(sm, per)
                sm, per = ring_inc(s_xfl, i)
                s.dma_start(out=xft[:, (i % 2) * D:(i % 2 + 1) * D],
                            in_=xf_i[i * 128:(i + 1) * 128, :]).then_inc(sm, per)
            for gi, (kind, base, n, u) in enumerate(GL):
                if kind == 0:
                    continue
                if gi == 8:
                    ring_wait(s, s_sct, NT - 1, mult=4)
                    ring_wait(s, s_sct, NT - 2, mult=4)
                if gi >= 2:
                    s.wait_ge(s_og, SB_[gi - 1])
                sm, per = ring_inc(s_xel, gi)
                for c in range(8):
                    s.dma_start(
                        out=xefm[:, (gi % 2) * 4096 + c * n:
                                 (gi % 2) * 4096 + (c + 1) * n],
                        in_=xe_d[base:base + n, c * 128:(c + 1) * 128],
                        transpose=True).then_inc(sm, per)
            s.wait_ge(s_oev, NST)
            ring_wait(s, s_sod, NST - 1)
            ring_wait(s, s_sod, NST - 2)
            for i in range(NT):
                if i >= 1:
                    s.wait_ge(s_x2, i)
                ring_wait(s, s_sx2, i - 2)
                sm, per = ring_inc(s_czl, i)
                s.dma_start(out=x2t[:, (i % 2) * D:(i % 2 + 1) * D],
                            in_=z_d[i * 128:(i + 1) * 128, :]).then_inc(sm, per)
                s.dma_start(out=x1t[:, :], in_=x1[i * 128:(i + 1) * 128, :]
                            ).then_inc(sm, per)
                sm, per = ring_inc(s_rol, NT + i)
                s.dma_start(out=rot[:, (i % 2) * 12:(i % 2 + 1) * 12],
                            in_=ro_i[i * 128:(i + 1) * 128, :]).then_inc(sm, per)
                s.wait_ge(s_x2, i + 1)
                sm, per = ring_inc(s_sx2, i)
                s.dma_start(out=x2_o[i * 128:(i + 1) * 128, :],
                            in_=x2t[:, (i % 2) * D:(i % 2 + 1) * D]
                            ).then_inc(sm, per)
                if not second:
                    s.wait_ge(s_lnb, 3 * (i + 1))
                    sm, per = ring_inc(s_xbs, i)
                    s.dma_start(out=xb_d[i * 128:(i + 1) * 128, :],
                                in_=xbt[:, (i % 2) * D:(i % 2 + 1) * D]
                                ).then_inc(sm, per)
                if i % 4 == 3 and not second:
                    gi4 = i // 4
                    ring_wait(s, s_xbs, i)
                    ring_wait(s, s_xbs, i - 1)
                    if gi4 >= 1:
                        s.wait_ge(s_pb, 6 * 4 * gi4)
                    sm, per = ring_inc(s_bfm, gi4)
                    for c in range(8):
                        s.dma_start(
                            out=xbfm[:, c * 512:(c + 1) * 512],
                            in_=xb_d[gi4 * 512:(gi4 + 1) * 512,
                                     c * 128:(c + 1) * 128],
                            transpose=True).then_inc(sm, per)
            if globals().get("L3_DEBUG"):
                s.wait_ge(s_dst, NT)
                for i_ in range(NT):
                    s.dma_start(out=dst_dbg[i_ * 128:(i_ + 1) * 128, :],
                                in_=di32[:, i_ * 8:i_ * 8 + 4]
                                ).then_inc(s_out, 16)
            if not second:
                s.wait_ge(s_kve, NT)
                s.wait_ge(s_kse, NT)
                s.dma_start(out=sp_o, in_=spac[:, :]).then_inc(s_out, 16)
                s.dma_start(out=zp_o, in_=zpac[:, :]).then_inc(s_out, 16)
                s.wait_ge(s_out, 32)
            for r in ((s_sx2, s_qs, s_ks, s_vs) if not second
                      else (s_sx2,)):
                ring_wait(s, r, NT - 1)
                ring_wait(s, r, NT - 2)

        @block.tensor
        def _(t):
            t.wait_ge(s_cs, 16 * 7)
            t.wait_ge(s_onb, 16)
            t.matmul(smallp[0:1, 0:E], pms[:, 0:1], ctss[:, :],
                     start=True, stop=True).then_inc(s_sup, 1)
            t.wait_ge(s_su, 1)
            t.matmul(smallp[:, 0:96], onrs[:, :], row32[0:1, 0:96],
                     start=True, stop=True).then_inc(s_sup, 1)
            t.matmul(smallp[:, 96:96 + OV_N], onrs[:, :], ov28[0:1, 0:OV_N],
                     start=True, stop=True).then_inc(s_sup, 1)
            for gi, (kind, base, n, u) in enumerate(GL):
                ring_wait(t, s_xel, gi, mult=8)
                if kind == 0:
                    t.wait_ge(s_swl, 16 * 24)
                    p = 0
                elif u < E:
                    ring_wait(t, s_wld, u, mult=24)
                    p = u % 2
                else:
                    ring_wait(t, s_wldo, u - E, mult=24)
                    p = u % 2
                for m in range(NM):
                    nglob = HB[gi] + m
                    pr = nglob % 2
                    if nglob >= 2:
                        t.wait_ge(s_hev, nglob - 1)
                    h1p = (h1a, h1b)[pr]
                    h3p = (h3a, h3b)[pr]
                    for k in range(8):
                        t.matmul(h1p[:, 0:n],
                                 wv(0, p, k)[:, m * 128:(m + 1) * 128],
                                 xefm[:, (gi % 2) * 4096 + k * n:
                                      (gi % 2) * 4096 + (k + 1) * n],
                                 start=(k == 0), stop=(k == 7))
                    for k in range(8):
                        ins = t.matmul(h3p[:, 0:n],
                                       wv(1, p, k)[:, m * 128:(m + 1) * 128],
                                       xefm[:, (gi % 2) * 4096 + k * n:
                                            (gi % 2) * 4096 + (k + 1) * n],
                                       start=(k == 0), stop=(k == 7))
                    ins.then_inc(s_hm, 1)
                t.wait_ge(s_hev, HB[gi] + 8)
                for st in range(n // 128):
                    nst = SB_[gi] + st
                    if nst >= 1:
                        t.wait_ge(s_oev, nst)
                    for hh in range(2):
                        for k in range(8):
                            ins = t.matmul(
                                outp[:, hh * 512:(hh + 1) * 512],
                                hfm[:, k * n + st * 128:k * n + st * 128 + 128],
                                wv(2, p, k)[:, hh * 512:(hh + 1) * 512],
                                start=(k == 0), stop=(k == 7))
                    ins.then_inc(s_og, 1)
            if not second:
                t.wait_ge(s_watl, 16 * 24)
                for gi4 in range(NT // 4):
                    ring_wait(t, s_bfm, gi4, mult=8)
                    for ti in range(4):
                        i = gi4 * 4 + ti
                        for pg in range(6):
                            npg = i * 6 + pg
                            ps = (h1a, h1b)[npg % 2]
                            if npg >= 2:
                                t.wait_ge(s_qkb, npg - 1)
                            for c in range(8):
                                ins = t.matmul(
                                    ps[:, :],
                                    xbfm[:, c * 512 + ti * 128:
                                         c * 512 + ti * 128 + 128],
                                    watv(c, pg),
                                    start=(c == 0), stop=(c == 7))
                            ins.then_inc(s_pb, 1)
                        t.wait_ge(s_qkb, (i + 1) * 6)
                        t.wait_ge(s_kve, i)
                        for h in range(H):
                            t.matmul(
                                outp[0:64, h * 64:(h + 1) * 64],
                                kbt[:, (i % 2) * D + h * 64:(i % 2) * D + h * 64 + 64],
                                vbt[:, (i % 2) * D + h * 64:(i % 2) * D + h * 64 + 64],
                                start=True, stop=True)
                        t.wait_ge(s_kse, i)
                        for hh in range(2):
                            zpp = (h3a, h3b)[hh]
                            ins = t.matmul(
                                zpp[0:1, 0:512], onesb[:, :],
                                kbt[:, (i % 2) * D + hh * 512:
                                    (i % 2) * D + hh * 512 + 512],
                                start=True, stop=True)
                        ins.then_inc(s_kvb, 1)

        @block.vector
        def _(v):
            v.memset(stat[:, 5:6], EPS_LN)
            if not second:
                v.memset(spac[:, :], 0.0)
                v.memset(zpac[:, :], 0.0)
            v.memset(xbt[:, 0:D], 0.0).then_inc(s_zro, 1)
            v.memset(x2t[:, 0:D], 0.0).then_inc(s_zro, 1)
            v.wait_ge(s_cs, 16 * 7)
            v.wait_ge(s_sup, 1)
            ins = v.tensor_copy(row32[0:1, 0:E], smallp[0:1, 0:E])
            vsync.after(v, ins)
            ins = v.tensor_scalar(row32[0:1, 96:128], mycs[0:1, :],
                                  -float(BASE_C), None, ALU.add)
            vsync.after(v, ins)
            ins = v.tensor_scalar(row32[0:1, 96:128], row32[0:1, 96:128], 0.0,
                                  None, ALU.max)
            vsync.after(v, ins)
            ins = v.memset(row32[0:1, 128:160], 0.0)
            vsync.after(v, ins)
            for m_ in range(OV_N):
                ins = v.tensor_scalar(row32[0:1, 192:224], row32[0:1, 96:128],
                                      float(m_ * OV_GRAN) + 0.5, None,
                                      ALU.is_ge)
                vsync.after(v, ins)
                ins = v.tensor_tensor(row32[0:1, 128:160],
                                      row32[0:1, 128:160],
                                      row32[0:1, 192:224], ALU.add)
                vsync.after(v, ins)
            ins = v.tensor_scalar(row32[0:1, 32:64], row32[0:1, 128:160],
                                  float(OV_GRAN), None, ALU.mult)
            vsync.after(v, ins)
            ins = v.tensor_copy(row32[0:1, 160:192], row32[0:1, 32:64])
            vsync.after(v, ins)
            a_, b_ = 160, 192
            for dstep in (1, 2, 4, 8, 16):
                ins = v.tensor_copy(row32[0:1, b_:b_ + dstep],
                                    row32[0:1, a_:a_ + dstep])
                vsync.after(v, ins)
                ins = v.tensor_tensor(row32[0:1, b_ + dstep:b_ + 32],
                                      row32[0:1, a_ + dstep:a_ + 32],
                                      row32[0:1, a_:a_ + 32 - dstep], ALU.add)
                vsync.after(v, ins)
                a_, b_ = b_, a_
            ins = v.memset(row32[0:1, 64:65], 0.0)
            vsync.after(v, ins)
            ins = v.tensor_copy(row32[0:1, 65:96], row32[0:1, a_:a_ + 31])
            vsync.after(v, ins)
            ins = v.tensor_scalar(row32[0:1, 224:256], row32[0:1, 64:96],
                                  1.0 / OV_GRAN, None, ALU.mult)
            vsync.after(v, ins)
            ins = v.tensor_tensor(row32[0:1, 128:160], row32[0:1, 64:96],
                                  row32[0:1, 32:64], ALU.add)
            vsync.after(v, ins)
            ins = v.tensor_scalar(row32[0:1, 128:160], row32[0:1, 128:160],
                                  1.0 / OV_GRAN, None, ALU.mult)
            vsync.after(v, ins)
            ins = v.memset(ov28[0:1, 0:OV_N], 0.0)
            vsync.after(v, ins)
            for e in range(E):
                ins = v.tensor_scalar(ov28[0:1, 32:32 + OV_N], io28s[0:1, :],
                                      row32[0:1, 224 + e:225 + e], None,
                                      ALU.is_ge)
                vsync.after(v, ins)
                ins = v.tensor_scalar(ov28[0:1, 64:64 + OV_N], io28s[0:1, :],
                                      row32[0:1, 128 + e:129 + e], None,
                                      ALU.is_lt)
                vsync.after(v, ins)
                ins = v.tensor_tensor(ov28[0:1, 32:32 + OV_N],
                                      ov28[0:1, 32:32 + OV_N],
                                      ov28[0:1, 64:64 + OV_N], ALU.mult)
                vsync.after(v, ins)
                ins = v.scalar_tensor_tensor(ov28[0:1, 0:OV_N],
                                             ov28[0:1, 32:32 + OV_N],
                                             float(e), ov28[0:1, 0:OV_N],
                                             ALU.mult, ALU.add)
                vsync.after(v, ins)
            ins = v.memset(row32[0:1, 96:128], 0.0)
            ins.then_inc(s_su, 1)
            v.wait_ge(s_sup, 3)
            v.tensor_copy(bc96[:, :], smallp[:, 0:96])
            ins = v.tensor_scalar(ebc[:, :], smallp[:, 96:96 + OV_N], 1024.0,
                                  None, ALU.mult)
            vsync.after(v, ins)
            for og in range(2):
                ins = v.tensor_scalar(widxf[:, :], cbs[:, :],
                                      ebc[:, og:og + 1], None, ALU.add)
                vsync.after(v, ins)
                ins = v.tensor_copy(widx[:, (og % 2) * 24:(og % 2 + 1) * 24],
                                    widxf[:, :])
                ins.then_inc(s_wix, 1)
            nself = 0
            for i in range(NT):
                ring_wait(v, s_rol, i)
                ri = rot[:, (i % 2) * 12:(i % 2 + 1) * 12]
                v.tensor_copy(ei32[:, :], ri[:, 0:4]).then_inc(s_ei, 1)
                v.wait_ge(s_tbg, 16 * 4 * (i + 1))
                eidc = [ri[:, j:j + 1] for j in range(4)]
                posc = [ri[:, 4 + j:5 + j] for j in range(4)]
                tbo = [tbt[:, j * 4:j * 4 + 1] for j in range(4)]
                tbg = [tbt[:, j * 4 + 1:j * 4 + 2] for j in range(4)]
                tbs = [tbt[:, j * 4 + 2:j * 4 + 3] for j in range(4)]
                A = [dsc[:, 32 + j * 32:33 + j * 32] for j in range(4)]
                B = [dsc[:, 40 + j * 32:41 + j * 32] for j in range(4)]
                C = [dsc[:, 48 + j * 32:49 + j * 32] for j in range(4)]
                Dd = [dsc[:, 56 + j * 32:57 + j * 32] for j in range(4)]
                dst = [dsc[:, j * 8:j * 8 + 1] for j in range(4)]
                J = range(4)
                for j in J:
                    v.tensor_scalar(A[j], posc[j], float(OV_BASE - BASE_C),
                                    None, ALU.add)
                for j in J:
                    v.tensor_tensor(A[j], A[j], tbs[j], ALU.add)
                for j in J:
                    v.scalar_tensor_tensor(Dd[j], eidc[j], float(BASE_C),
                                           posc[j], ALU.mult, ALU.add)
                for j in J:
                    v.tensor_tensor(C[j], posc[j], tbg[j], ALU.subtract)
                for j in J:
                    v.tensor_scalar(B[j], A[j], float(NSLOT), None, ALU.is_lt)
                for j in J:
                    v.tensor_scalar(C[j], C[j], float(BASE_C), None, ALU.is_lt)
                for j in J:
                    v.tensor_tensor(B[j], B[j], C[j], ALU.mult)
                for j in J:
                    v.tensor_scalar(A[j], A[j], -float(SINK), None, ALU.add)
                for j in J:
                    v.tensor_tensor(A[j], A[j], B[j], ALU.mult)
                for j in J:
                    v.tensor_scalar(A[j], A[j], float(SINK), None, ALU.add)
                for j in J:
                    v.tensor_scalar(C[j], posc[j], float(BASE_C), None,
                                    ALU.is_lt)
                for j in J:
                    v.tensor_tensor(B[j], posc[j], tbo[j], ALU.add)
                for j in J:
                    v.tensor_scalar(B[j], B[j], float(CAP), None, ALU.is_lt)
                for j in J:
                    v.tensor_tensor(Dd[j], Dd[j], A[j], ALU.subtract)
                for j in J:
                    v.tensor_tensor(Dd[j], Dd[j], C[j], ALU.mult)
                for j in J:
                    v.tensor_tensor(Dd[j], Dd[j], A[j], ALU.add)
                for j in J:
                    v.tensor_scalar(Dd[j], Dd[j], -float(SINK), None, ALU.add)
                for j in J:
                    v.tensor_tensor(Dd[j], Dd[j], B[j], ALU.mult)
                for j in J:
                    v.tensor_scalar(Dd[j], Dd[j], float(SINK), None, ALU.add)
                for j in J:
                    v.tensor_scalar(Dd[j], Dd[j], float(SINK), None, ALU.min)
                for j in J:
                    ins = v.tensor_scalar(dst[j], Dd[j], 0.0, None, ALU.max)
                vsync.after(v, ins)
                ins = v.tensor_copy(
                    di32[:, i * 8:i * 8 + 4],
                    dsc[:, 0:32].rearrange("p (a b) -> p a b", b=8)[:, :, 0])
                ins.then_inc(s_dst, 1)
                if i % 4 == 3:
                    sg = i // 4
                    for m in range(NM):
                        nglob = sg * 8 + m
                        v.wait_ge(s_sil, nglob + 1)
                        ins = v.tensor_tensor(
                            hfm[:, m * 512:(m + 1) * 512],
                            hs_t[:, (nglob % 2) * 512:(nglob % 2 + 1) * 512],
                            ((h3a, h3b)[nglob % 2])[:, 0:512],
                            ALU.mult)
                        ins.then_inc(s_hev, 1)
                    for st in range(4):
                        nst = sg * 4 + st
                        v.wait_ge(s_og, nst + 1)
                        ring_wait(v, s_sod, nst - 2)
                        ins = v.tensor_copy(
                            outb[:, (nst % 2) * D:(nst % 2 + 1) * D],
                            outp[:, :])
                        ins.then_inc(s_oev, 1)
            for gi, (kind, base, n, u) in enumerate(GL):
                if kind == 0:
                    continue
                if kind == 2 and (u - E) + 2 < OV_N:
                    og = (u - E) + 2
                    ring_wait(v, s_wldo, og - 2, mult=24)
                    ins = v.tensor_scalar(widxf[:, :], cbs[:, :],
                                          ebc[:, og:og + 1], None, ALU.add)
                    vsync.after(v, ins)
                    ins = v.tensor_copy(
                        widx[:, (og % 2) * 24:(og % 2 + 1) * 24], widxf[:, :])
                    ins.then_inc(s_wix, 1)
                for m in range(NM):
                    nglob = HB[gi] + m
                    v.wait_ge(s_sil, nglob + 1)
                    ins = v.tensor_tensor(
                        hfm[:, m * n:(m + 1) * n],
                        hs_t[:, (nglob % 2) * 512:(nglob % 2) * 512 + n],
                        ((h3a, h3b)[nglob % 2])[:, 0:n],
                        ALU.mult)
                    ins.then_inc(s_hev, 1)
                for st in range(n // 128):
                    nst = SB_[gi] + st
                    v.wait_ge(s_og, nst + 1)
                    ring_wait(v, s_sod, nst - 2)
                    ins = v.tensor_copy(
                        outb[:, (nst % 2) * D:(nst % 2 + 1) * D], outp[:, :])
                    ins.then_inc(s_oev, 1)
            for i in range(NT):
                v.wait_ge(s_cgd, 16 * (4 * i + 2))
                ring_wait(v, s_czl, i, mult=2)
                ring_wait(v, s_rol, NT + i)
                ri = rot[:, (i % 2) * 12:(i % 2 + 1) * 12]
                xo = x2t[:, (i % 2) * D:(i % 2 + 1) * D]
                v.scalar_tensor_tensor(xo, gt[:, 0:D], ri[:, 8:9],
                                       xo, ALU.mult, ALU.add)
                ins = v.scalar_tensor_tensor(
                    xo, gt[:, D:2 * D], ri[:, 9:10], xo,
                    ALU.mult, ALU.add)
                ins.then_inc(s_cg2, 1)
                v.wait_ge(s_cgd, 16 * (4 * i + 4))
                for j in range(2, 4):
                    ins = v.scalar_tensor_tensor(
                        xo, gt[:, (j - 2) * D:(j - 1) * D],
                        ri[:, 8 + j:9 + j], xo,
                        ALU.mult, ALU.add)
                ins.then_inc(s_cg2, 1)
                v.tensor_tensor(xo, xo, x1t[:, :], ALU.add).then_inc(s_x2, 1)
                if not second:
                    v.wait_ge(s_x2, i + 1)
                    ins = v.reduce_sum(stat[:, 0:1], xo, axis=AX.X)
                    vsync.after(v, ins)
                    ins = v.tensor_scalar(stat[:, 1:2], stat[:, 0:1], -RLN, None,
                                          ALU.mult)
                    vsync.after(v, ins)
                    v.tensor_scalar(xm3[:, :], xo, stat[:, 1:2], None,
                                    ALU.add).then_inc(s_lnb, 1)
                    v.wait_ge(s_scb, 2 * i + 1)
                    v.reduce_sum(stat[:, 2:3], sq3[:, :], axis=AX.X).then_inc(s_lnb, 1)
                    v.wait_ge(s_scb, 2 * i + 2)
                    ring_wait(v, s_xbs, i - 2)
                    v.tensor_scalar(xbt[:, (i % 2) * D:(i % 2 + 1) * D],
                                    xm3[:, :], stat[:, 4:5], None,
                                    ALU.mult).then_inc(s_lnb, 1)
                    if i % 4 == 3:
                        for tj in range(4):
                            jj = (i // 4) * 4 + tj
                            for pg in range(6):
                                npg = jj * 6 + pg
                                ps = (h1a, h1b)[npg % 2]
                                v.wait_ge(s_pb, npg + 1)
                                hh = pg % 2
                                dofs = (jj % 2) * D + hh * 512
                                if pg < 4:
                                    if pg < 2:
                                        ring_wait(v, s_qs, jj - 2)
                                    else:
                                        ring_wait(v, s_ks, jj - 2)
                                    dst = qbt if pg < 2 else kbt
                                    v.tensor_scalar_min(ev1[:, :], ps[:, :], 0.0)
                                    v.tensor_scalar_max(ev2[:, :], ps[:, :],
                                                        0.0).then_inc(s_evb, 1)
                                    v.wait_ge(s_exb, jj * 4 + pg + 1)
                                    v.tensor_tensor(dst[:, dofs:dofs + 512],
                                                    ev3[:, :], ev2[:, :],
                                                    ALU.add).then_inc(s_qkb, 1)
                                else:
                                    ring_wait(v, s_vs, jj - 2)
                                    v.tensor_copy(vbt[:, dofs:dofs + 512],
                                                  ps[:, :]).then_inc(s_qkb, 1)
                            v.wait_ge(s_kvb, jj + 1)
                            v.tensor_tensor(spac[:, :], spac[:, :],
                                            outp[0:64, :], ALU.add).then_inc(s_kve, 1)
                            v.tensor_tensor(zpac[:, 0:512], zpac[:, 0:512],
                                            h3a[0:1, 0:512], ALU.add)
                            v.tensor_tensor(zpac[:, 512:1024], zpac[:, 512:1024],
                                            h3b[0:1, 0:512],
                                            ALU.add).then_inc(s_kse, 1)

        @block.scalar
        def _(a):
            a.wait_ge(s_zro, 1)
            for r in range(NROW // 128):
                a.dma_start(out=xe_d[r * 128:(r + 1) * 128, :],
                            in_=xbt[:, 0:D]).then_inc(s_xez, 16)
            for gi, (kind, base, n, u) in enumerate(GL):
                if gi >= 1:
                    pk, pb_, pn, pu = GL[gi - 1]
                    for st in range(pn // 128):
                        nst = SB_[gi - 1] + st
                        a.wait_ge(s_oev, nst + 1)
                        dst_d = z_d if pk == 0 else so_d
                        dofs = pb_ + st * 128
                        sm, per = ring_inc(s_sod, nst)
                        a.dma_start(
                            out=dst_d[dofs:dofs + 128, :],
                            in_=outb[:, (nst % 2) * D:(nst % 2 + 1) * D]
                            ).then_inc(sm, per)
                for m in range(NM):
                    nglob = HB[gi] + m
                    a.wait_ge(s_hm, nglob + 1)
                    if nglob >= 2:
                        a.wait_ge(s_hev, nglob - 1)
                    a.activation(
                        hs_t[:, (nglob % 2) * 512:(nglob % 2) * 512 + n],
                        ((h1a, h1b)[nglob % 2])[:, 0:n],
                        AF.Silu).then_inc(s_sil, 1)
            gi = NGR - 1
            pk, pb_, pn, pu = GL[gi]
            for st in range(pn // 128):
                nst = SB_[gi] + st
                a.wait_ge(s_oev, nst + 1)
                sm, per = ring_inc(s_sod, nst)
                a.dma_start(out=so_d[pb_ + st * 128:pb_ + st * 128 + 128, :],
                            in_=outb[:, (nst % 2) * D:(nst % 2 + 1) * D]
                            ).then_inc(sm, per)
            if not second:
                for i in range(NT):
                    a.wait_ge(s_lnb, 3 * i + 1)
                    a.activation(sq3[:, :], xm3[:, :], AF.Square).then_inc(s_scb, 1)
                    a.wait_ge(s_lnb, 3 * i + 2)
                    ins = a.activation(stat[:, 3:4], stat[:, 2:3], AF.Ln,
                                       bias=stat[:, 5:6], scale=RLN)
                    asns.after(a, ins)
                    a.activation(stat[:, 4:5], stat[:, 3:4], AF.Exp,
                                 scale=-0.5).then_inc(s_scb, 1)
                    if i % 4 == 3:
                        for tj in range(4):
                            jj = (i // 4) * 4 + tj
                            for eg in range(4):
                                a.wait_ge(s_evb, jj * 4 + eg + 1)
                                a.activation(ev3[:, :], ev1[:, :],
                                             AF.Exp).then_inc(s_exb, 1)
    return nc




def build_l4():
    nc = bass.Bass("TRN2", target_bir_lowering=False,
                   detect_race_conditions=False)
    x2_i = nc.dram_tensor("x2", [TOK, D], F32, kind="ExternalInput").ap()
    qb_i = nc.dram_tensor("qb", [TOK, D], BF16, kind="ExternalInput").ap()
    kb_i = nc.dram_tensor("kb", [TOK, D], BF16, kind="ExternalInput").ap()
    vb_i = nc.dram_tensor("vb", [TOK, D], BF16, kind="ExternalInput").ap()
    sp3_i = nc.dram_tensor("sp3", [64, 3 * D], F32, kind="ExternalInput").ap()
    zp3_i = nc.dram_tensor("zp3", [1, 3 * D], F32, kind="ExternalInput").ap()
    wp_i = nc.dram_tensor("wp", [D, D], F32, kind="ExternalInput").ap()
    wg = nc.dram_tensor("wg", [E, D], F32, kind="ExternalInput").ap()
    c_id = nc.dram_tensor("c_id", [128, 128], BF16, kind="ExternalInput").ap()
    c_tru = nc.dram_tensor("c_tru", [128, 128], F32, kind="ExternalInput").ap()
    c_trub = nc.dram_tensor("c_trub", [128, 128], BF16,
                            kind="ExternalInput").ap()
    c_onrb = nc.dram_tensor("c_onrb", [1, 128], BF16,
                            kind="ExternalInput").ap()
    c_us = nc.dram_tensor("c_us", [128, 128], F32, kind="ExternalInput").ap()
    c_io32 = nc.dram_tensor("c_io32", [128, E], F32, kind="ExternalInput").ap()
    c_on128 = nc.dram_tensor("c_on128", [128, 1], F32,
                             kind="ExternalInput").ap()
    c_onr = nc.dram_tensor("c_onr", [1, 128], F32, kind="ExternalInput").ap()
    c_onb = nc.dram_tensor("c_onb", [128, 1], BF16, kind="ExternalInput").ap()

    x3_o = nc.dram_tensor("x3", [TOK, D], F32, kind="ExternalOutput").ap()
    xf2_o = nc.dram_tensor("xf2", [TOK, D], BF16, kind="ExternalOutput").ap()
    ro2_o = nc.dram_tensor("rout2", [TOK, 12], F32, kind="ExternalOutput").ap()
    ct2_o = nc.dram_tensor("counts2", [1, E], F32,
                           kind="ExternalOutput").ap()

    wg_bf_d = nc.dram_tensor("wg_bf_d4", [E, D], BF16, kind="Internal").ap()

    ctx = contextlib.ExitStack()
    with ctx:
        sb = lambda n, s, d: ctx.enter_context(nc.sbuf_tensor(n, s, d))
        pt = lambda n, s, d: ctx.enter_context(nc.psum_tensor(n, s, d))
        qbt = sb("qbt", [128, 2 * D], BF16)
        kbt = sb("kbt", [128, 2 * D], BF16)
        vbt = sb("vbt", [128, 2 * D], BF16)
        qte = sb("qte", [64, 8 * 128], BF16)
        qto = sb("qto", [64, 8 * 128], BF16)
        kte = sb("kte", [64, 8 * 128], BF16)
        kto = sb("kto", [64, 8 * 128], BF16)
        x2t = sb("x2t", [128, 2 * D], F32)
        wpsb = sb("wpsb", [128, 8 * D], BF16)
        wgt = sb("wgt", [128, 8 * E], BF16)
        wtile = sb("wtile", [128, D], BF16)
        ssb = sb("ssb", [64, D], F32)
        sbf = sb("sbf", [64, D], BF16)
        zrow = sb("zrow", [1, D], F32)
        zrbf = sb("zrbf", [1, D], BF16)
        sp3s = sb("sp3s", [64, 3 * D], F32)
        zp3s = sb("zp3s", [1, 3 * D], F32)
        atm = sb("atm", [128, 16 * 128], BF16)
        prodb = sb("prodb", [128, D], F32)
        dent = sb("dent", [128, 32], F32)
        ysb = sb("ysb", [128, D], BF16)
        ytsb = sb("ytsb", [128, D], BF16)
        x3t = sb("x3t", [128, 2 * D], F32)
        xm2 = sb("xm2", [128, D], F32)
        sq2 = sb("sq2", [128, D], F32)
        stat = sb("stat", [128, 8], F32)
        xfb = sb("xfb", [128, 2 * D], BF16)
        xffm = sb("xffm", [128, D], BF16)
        lg = sb("lg", [128, E], F32)
        sc_t = sb("sc_t", [128, E], F32)
        gs = sb("gs", [128, NGRP], F32)
        kg = sb("kg", [128, NGRP], F32)
        kg32 = sb("kg32", [128, E], F32)
        msk = sb("msk", [128, E], F32)
        selJ = sb("selJ", [128, 4 * E], F32)
        tmp32 = sb("tmp32", [128, E], F32)
        rsm = sb("rsm", [128, 4], F32)
        g8 = sb("g8", [128, 8], F32)
        m8 = sb("m8", [128, 8], F32)
        eidx = sb("eidx", [128, 8], mybir.dt.uint32)
        eidf = sb("eidf", [128, 8], F32)
        post = sb("post", [128, E], F32)
        routt = sb("routt", [128, 2 * 12], F32)
        runr = sb("runr", [1, E], F32)
        idsb = sb("idsb", [128, 128], BF16)
        trusb = sb("trusb", [128, 128], F32)
        trubb = sb("trubb", [128, 128], BF16)
        onrb = sb("onrb", [1, 128], BF16)
        ussb = sb("ussb", [128, 128], F32)
        iosb = sb("iosb", [128, E], F32)
        on1 = sb("on1", [128, 1], F32)
        onr = sb("onr", [1, 128], F32)
        onb = sb("onb", [128, 1], BF16)
        pb1 = pt("pb1", [128, 512], F32)
        pb2 = pt("pb2", [128, 512], F32)
        pk1 = pt("pk1", [128, 512], F32)
        pk2 = pt("pk2", [128, 512], F32)
        pb4 = pt("pb4", [128, 128], F32)
        pb5 = pt("pb5", [128, 128], F32)
        tp0 = pt("tp0", [128, 128], BF16)
        tp1 = pt("tp1", [128, 128], BF16)
        smallp = pk1[:, 0:128]

        s_cs = ctx.enter_context(nc.semaphore("s_cs"))
        s_wp = ctx.enter_context(nc.semaphore("s_wp"))
        s_cst = ring_sems(ctx, nc, "s_cst", 2)
        s_wst = ctx.enter_context(nc.semaphore("s_wst"))
        s_wt = ctx.enter_context(nc.semaphore("s_wt"))
        s_qb = ring_sems(ctx, nc, "s_qb", 2)
        s_kb = ring_sems(ctx, nc, "s_kb", 2)
        s_vb = ring_sems(ctx, nc, "s_vb", 2)
        s_x2 = ring_sems(ctx, nc, "s_x2", 2)
        s_at = ctx.enter_context(nc.semaphore("s_at"))
        s_ate = ctx.enter_context(nc.semaphore("s_ate"))
        s_kc = ctx.enter_context(nc.semaphore("s_kc"))
        s_dche = ctx.enter_context(nc.semaphore("s_dche"))
        s_zu = ctx.enter_context(nc.semaphore("s_zu"))
        s_zue = ctx.enter_context(nc.semaphore("s_zue"))
        s_num = ctx.enter_context(nc.semaphore("s_num"))
        s_ye = ctx.enter_context(nc.semaphore("s_ye"))
        s_su1 = ctx.enter_context(nc.semaphore("s_su1"))
        s_se = ctx.enter_context(nc.semaphore("s_se"))
        s_sbf = ctx.enter_context(nc.semaphore("s_sbf"))
        s_tp = ctx.enter_context(nc.semaphore("s_tp"))
        s_tpe = ctx.enter_context(nc.semaphore("s_tpe"))
        s_proj = ctx.enter_context(nc.semaphore("s_proj"))
        s_x3 = ctx.enter_context(nc.semaphore("s_x3"))
        s_x3e = ctx.enter_context(nc.semaphore("s_x3e"))
        s_ln = ctx.enter_context(nc.semaphore("s_ln"))
        s_sc = ctx.enter_context(nc.semaphore("s_sc"))
        s_xfb = ctx.enter_context(nc.semaphore("s_xfb"))
        s_gt = ctx.enter_context(nc.semaphore("s_gt"))
        s_rt = ctx.enter_context(nc.semaphore("s_rt"))
        s_ps = ctx.enter_context(nc.semaphore("s_ps"))
        s_rr = ctx.enter_context(nc.semaphore("s_rr"))
        s_ro = ctx.enter_context(nc.semaphore("s_ro"))
        s_sx3 = ring_sems(ctx, nc, "s_sx3", 2)
        s_sxf = ring_sems(ctx, nc, "s_sxf", 2)
        s_sro = ring_sems(ctx, nc, "s_sro", 2)
        s_out = ctx.enter_context(nc.semaphore("s_out"))
        vsync = SelfSync(ctx, nc, "vsync")
        asns = SelfSync(ctx, nc, "asns")
        block = ctx.enter_context(nc.Block())

        @block.gpsimd
        def _(g):
            for r in range(8):
                g.dma_start(out=wpsb[:, r * D:(r + 1) * D],
                            in_=wp_i[r * 128:(r + 1) * 128, :]
                            ).then_inc(s_wp, 16)
            g.dma_start(out=wtile[0:E, 0:D], in_=wg).then_inc(s_cst[0], 16)

        @block.sync
        def _(s):
            for dst, src in ((idsb[:, :], c_id), (trubb[:, :], c_trub),
                             (onrb[:, :], c_onrb),
                             (trusb[:, :], c_tru), (ussb[:, :], c_us),
                             (iosb[:, :], c_io32), (on1[:, :], c_on128),
                             (onr[:, :], c_onr), (onb[:, :], c_onb),
                             (sp3s[:, :], sp3_i), (zp3s[:, :], zp3_i)):
                s.dma_start(out=dst, in_=src).then_inc(s_cs, 16)
            ring_wait(s, s_cst, 0)
            s.dma_start(out=wg_bf_d, in_=wtile[0:E, 0:D]).then_inc(s_wst, 16)
            s.wait_ge(s_wst, 16)
            for k in range(8):
                s.dma_start(out=wgt[:, k * E:(k + 1) * E],
                            in_=wg_bf_d[0:E, k * 128:(k + 1) * 128],
                            transpose=True).then_inc(s_wt, 16)
            for i in range(NT):
                if i >= 2:
                    s.wait_ge(s_dche, i - 1)
                    s.wait_ge(s_num, 16 * (i - 1))
                    s.wait_ge(s_zu, i - 1)
                    s.wait_ge(s_x3, i - 1)
                sm, per = ring_inc(s_qb, i)
                s.dma_start(out=qbt[:, (i % 2) * D:(i % 2 + 1) * D],
                            in_=qb_i[i * 128:(i + 1) * 128, :]
                            ).then_inc(sm, per)
                sm, per = ring_inc(s_kb, i)
                s.dma_start(out=kbt[:, (i % 2) * D:(i % 2 + 1) * D],
                            in_=kb_i[i * 128:(i + 1) * 128, :]
                            ).then_inc(sm, per)
                sm, per = ring_inc(s_vb, i)
                s.dma_start(out=vbt[:, (i % 2) * D:(i % 2 + 1) * D],
                            in_=vb_i[i * 128:(i + 1) * 128, :]
                            ).then_inc(sm, per)
                sm, per = ring_inc(s_x2, i)
                s.dma_start(out=x2t[:, (i % 2) * D:(i % 2 + 1) * D],
                            in_=x2_i[i * 128:(i + 1) * 128, :]
                            ).then_inc(sm, per)
            s.wait_ge(s_rr, NT)
            s.dma_start(out=ct2_o, in_=runr[:, :]).then_inc(s_out, 16)
            s.wait_ge(s_out, 16)
            ring_wait(s, s_sx3, NT - 1)
            ring_wait(s, s_sx3, NT - 2)
            ring_wait(s, s_sxf, NT - 1)
            ring_wait(s, s_sxf, NT - 2)
            ring_wait(s, s_sro, NT - 1)
            ring_wait(s, s_sro, NT - 2)

        @block.tensor
        def _(t):
            t.wait_ge(s_cs, 16 * 11)
            t.wait_ge(s_wp, 16 * 8)
            t.wait_ge(s_wt, 16 * 8)
            ntp = 0
            for i in range(NT):
                ring_wait(t, s_qb, i)
                ring_wait(t, s_kb, i)
                ring_wait(t, s_vb, i)
                qh = lambda h: (qte, qto)[h % 2][:, (h // 2) * 128:
                                                 (h // 2) * 128 + 128]
                kh = lambda h: (kte, kto)[h % 2][:, (h // 2) * 128:
                                                 (h // 2) * 128 + 128]
                kn = lambda h: kbt[:, (i % 2) * D + h * 64:
                                   (i % 2) * D + (h + 1) * 64]
                vn = lambda h: vbt[:, (i % 2) * D + h * 64:
                                   (i % 2) * D + (h + 1) * 64]
                for b in range(16):
                    src = (qbt if b < 8 else kbt)[
                        :, (i % 2) * D + (b % 8) * 128:
                        (i % 2) * D + (b % 8) * 128 + 128]
                    if ntp >= 2:
                        t.wait_ge(s_tpe, ntp - 1)
                    ins = t.transpose((tp0, tp1)[ntp % 2][:, :], src,
                                      idsb[:, :])
                    ins.then_inc(s_tp, 1)
                    ntp += 1
                t.wait_ge(s_tpe, ntp)
                if i >= 1:
                    t.wait_ge(s_zue, i)
                    t.wait_ge(s_rr, i)
                t.wait_ge(s_sbf, i + 1)
                for hh in range(2):
                    pk = (pk1, pk2)[hh]
                    sl = slice((i % 2) * D + hh * 512,
                               (i % 2) * D + hh * 512 + 512)
                    t.matmul(pk[:, :], trubb[:, :], kbt[:, sl],
                             start=True, stop=False)
                    ins = t.matmul(pk[:, :], onrb[:, :],
                                   zrbf[:, hh * 512:hh * 512 + 512],
                                   start=False, stop=True)
                ins.then_inc(s_kc, 1)
                if i >= 1:
                    t.wait_ge(s_x3e, i)
                for h in range(H):
                    if 16 * i + h >= 2:
                        t.wait_ge(s_ate, 16 * i + h - 1)
                    ins = t.matmul((pb1, pb2)[h % 2][:, 0:128],
                                   kh(h), qh(h), start=True, stop=True)
                    ins.then_inc(s_at, 1)
                t.wait_ge(s_ate, 16 * (i + 1))
                for h in range(H):
                    if 16 * i + h >= 2:
                        t.wait_ge(s_ye, 16 * i + h - 1)
                        t.wait_ge(s_se, 16 * i + h - 1)
                    pY = (pb4, pb5)[h % 2][:, 0:64]
                    pS = (pb4, pb5)[h % 2][0:64, 64:128]
                    t.matmul(pY, atm[:, h * 128:(h + 1) * 128], vn(h),
                             start=True, stop=False)
                    t.matmul(pY, qh(h), sbf[:, h * 64:(h + 1) * 64],
                             start=False, stop=True).then_inc(s_num, 1)
                    ins = t.matmul(pS, kn(h), vn(h), start=True,
                                   stop=True)
                    ins.then_inc(s_su1, 1)
                t.wait_ge(s_dche, i + 1)
                for hh in range(2):
                    sl = slice((i % 2) * D + hh * 512,
                               (i % 2) * D + hh * 512 + 512)
                    ins = t.matmul((pk1, pk2)[hh][0:1, :], onb[:, :],
                                   kbt[:, sl], start=True, stop=True)
                ins.then_inc(s_zu, 1)
                for c in range(8):
                    if ntp >= 2:
                        t.wait_ge(s_tpe, ntp - 1)
                    t.wait_ge(s_ye, 16 * i + 2 * c + 2)
                    ins = t.transpose((tp0, tp1)[ntp % 2][:, :],
                                      ysb[:, c * 128:(c + 1) * 128],
                                      idsb[:, :])
                    ins.then_inc(s_tp, 1)
                    ntp += 1
                t.wait_ge(s_tpe, ntp)
                for hh in range(2):
                    for c in range(8):
                        ins = t.matmul(
                            (pb1, pb2)[hh][:, :],
                            ytsb[:, c * 128:(c + 1) * 128],
                            wpsb[:, c * D + hh * 512:c * D + hh * 512 + 512],
                            start=(c == 0), stop=(c == 7))
                ins.then_inc(s_proj, 1)
                t.wait_ge(s_xfb, i + 1)
                for c in range(8):
                    if ntp >= 2:
                        t.wait_ge(s_tpe, ntp - 1)
                    ins = t.transpose((tp0, tp1)[ntp % 2][:, :],
                                      xfb[:, (i % 2) * D + c * 128:
                                          (i % 2) * D + c * 128 + 128],
                                      idsb[:, :])
                    ins.then_inc(s_tp, 1)
                    ntp += 1
                t.wait_ge(s_tpe, ntp)
                t.wait_ge(s_zue, i + 1)
                if i >= 1:
                    t.wait_ge(s_rr, i)
                for c in range(8):
                    ins = t.matmul(smallp[:, 0:E],
                                   xffm[:, c * 128:(c + 1) * 128],
                                   wgt[:, c * E:(c + 1) * E],
                                   start=(c == 0), stop=(c == 7))
                ins.then_inc(s_gt, 1)
                t.wait_ge(s_rt, i + 1)
                t.wait_ge(s_rr, i)
                t.matmul(smallp[:, 32:32 + E], ussb[:, :], kg32[:, :],
                         start=True, stop=False)
                t.matmul(smallp[:, 32:32 + E], onr[:, :], runr[:, :],
                         start=False, stop=True)
                ins2 = t.matmul(smallp[0:1, 64:64 + E], on1[:, :],
                                kg32[:, :], start=True, stop=True)
                ins2.then_inc(s_ps, 1)

        @block.vector
        def _(v):
            v.memset(stat[:, 5:6], EPS_LN)
            v.memset(runr[:, :], 0.0)
            v.wait_ge(s_cs, 16 * 11)
            v.tensor_tensor(ssb[:, :], sp3s[:, 0:D], sp3s[:, D:2 * D],
                            ALU.add)
            ins = v.tensor_tensor(ssb[:, :], ssb[:, :], sp3s[:, 2 * D:3 * D],
                                  ALU.add)
            vsync.after(v, ins)
            v.tensor_copy(sbf[:, :], ssb[:, :])
            v.tensor_tensor(zrow[:, :], zp3s[:, 0:D], zp3s[:, D:2 * D],
                            ALU.add)
            ins = v.tensor_tensor(zrow[:, :], zrow[:, :],
                                  zp3s[:, 2 * D:3 * D], ALU.add)
            vsync.after(v, ins)
            v.tensor_copy(zrbf[:, :], zrow[:, :]).then_inc(s_sbf, 1)
            ntp = 0
            for i in range(NT):
                for b in range(16):
                    v.wait_ge(s_tp, ntp + 1)
                    tpp = (tp0, tp1)[ntp % 2]
                    de, do = ((qte, qto) if b < 8 else (kte, kto))
                    v.tensor_copy(de[:, (b % 8) * 128:(b % 8) * 128 + 128],
                                  tpp[0:64, :])
                    v.tensor_copy(do[:, (b % 8) * 128:(b % 8) * 128 + 128],
                                  tpp[64:128, :]).then_inc(s_tpe, 1)
                    ntp += 1
                for h in range(H):
                    v.wait_ge(s_at, 16 * i + h + 1)
                    v.tensor_tensor(atm[:, h * 128:(h + 1) * 128],
                                    (pb1, pb2)[h % 2][:, 0:128],
                                    trusb[:, :], ALU.mult).then_inc(s_ate, 1)
                v.wait_ge(s_kc, i + 1)
                ring_wait(v, s_qb, i)
                v.tensor_tensor(prodb[:, 0:512],
                                qbt[:, (i % 2) * D:(i % 2) * D + 512],
                                pk1[:, :], ALU.mult)
                ins = v.tensor_tensor(prodb[:, 512:1024],
                                      qbt[:, (i % 2) * D + 512:
                                          (i % 2 + 1) * D],
                                      pk2[:, :], ALU.mult)
                vsync.after(v, ins)
                ins = v.reduce_sum(dent[:, 0:16],
                                   prodb[:, :].rearrange("p (h d) -> p h d",
                                                         d=DH),
                                   axis=AX.X)
                ins.then_inc(s_dche, 1)
                v.wait_ge(s_dche, i + 1)
                ins = v.tensor_scalar(dent[:, 0:16], dent[:, 0:16], EPS_AT,
                                      None, ALU.add)
                vsync.after(v, ins)
                ins = v.reciprocal(dent[:, 16:32], dent[:, 0:16])
                vsync.after(v, ins)
                for h in range(H):
                    pY = (pb4, pb5)[h % 2][:, 0:64]
                    pS = (pb4, pb5)[h % 2][0:64, 64:128]
                    v.wait_ge(s_num, 16 * i + h + 1)
                    v.tensor_scalar(ysb[:, h * 64:(h + 1) * 64], pY,
                                    dent[:, 16 + h:17 + h], None,
                                    ALU.mult).then_inc(s_ye, 1)
                    v.wait_ge(s_su1, 16 * i + h + 1)
                    v.tensor_tensor(ssb[:, h * 64:(h + 1) * 64],
                                    ssb[:, h * 64:(h + 1) * 64], pS,
                                    ALU.add).then_inc(s_se, 1)
                v.wait_ge(s_zu, i + 1)
                v.tensor_tensor(zrow[:, 0:512], zrow[:, 0:512],
                                pk1[0:1, :], ALU.add)
                ins = v.tensor_tensor(zrow[:, 512:1024], zrow[:, 512:1024],
                                      pk2[0:1, :], ALU.add)
                ins.then_inc(s_zue, 1)
                v.wait_ge(s_zue, i + 1)
                v.tensor_copy(zrbf[:, :], zrow[:, :])
                ins = v.tensor_copy(sbf[:, :], ssb[:, :])
                ins.then_inc(s_sbf, 1)
                for c in range(8):
                    v.wait_ge(s_tp, ntp + 1)
                    if i >= 1 and c == 0:
                        v.wait_ge(s_proj, i)
                    v.tensor_copy(ytsb[:, c * 128:(c + 1) * 128],
                                  (tp0, tp1)[ntp % 2][:, :]).then_inc(s_tpe, 1)
                    ntp += 1
                v.wait_ge(s_proj, i + 1)
                ring_wait(v, s_x2, i)
                ring_wait(v, s_sx3, i - 2)
                xo = x3t[:, (i % 2) * D:(i % 2 + 1) * D]
                x2s = x2t[:, (i % 2) * D:(i % 2 + 1) * D]
                for hh in range(2):
                    sl = slice(hh * 512, (hh + 1) * 512)
                    ins = v.tensor_tensor(xo[:, sl], (pb1, pb2)[hh][:, :],
                                          x2s[:, sl], ALU.add)
                ins.then_inc(s_x3, 1)
                v.engine_nop().then_inc(s_x3e, 1)
                v.wait_ge(s_x3, i + 1)
                ins = v.reduce_sum(stat[:, 0:1], xo, axis=AX.X)
                vsync.after(v, ins)
                ins = v.tensor_scalar(stat[:, 1:2], stat[:, 0:1], -RLN, None,
                                      ALU.mult)
                vsync.after(v, ins)
                v.tensor_scalar(xm2[:, :], xo, stat[:, 1:2], None,
                                ALU.add).then_inc(s_ln, 1)
                v.wait_ge(s_sc, 3 * i + 1)
                v.reduce_sum(stat[:, 2:3], sq2[:, :], axis=AX.X).then_inc(
                    s_ln, 1)
                v.wait_ge(s_sc, 3 * i + 2)
                ring_wait(v, s_sxf, i - 2)
                v.tensor_scalar(xfb[:, (i % 2) * D:(i % 2 + 1) * D],
                                xm2[:, :], stat[:, 4:5], None,
                                ALU.mult).then_inc(s_xfb, 1)
                for c in range(8):
                    v.wait_ge(s_tp, ntp + 1)
                    v.tensor_copy(xffm[:, c * 128:(c + 1) * 128],
                                  (tp0, tp1)[ntp % 2][:, :]).then_inc(s_tpe, 1)
                    ntp += 1
                v.wait_ge(s_gt, i + 1)
                ins = v.tensor_copy(lg[:, :], smallp[:, 0:E])
                vsync.after(v, ins)
                v.reduce_max(stat[:, 6:7], lg[:, :], axis=AX.X,
                             negate=True).then_inc(s_ln, 1)
                v.wait_ge(s_sc, 3 * i + 3)
                ins = v.reduce_sum(stat[:, 7:8], sc_t[:, :], axis=AX.X)
                vsync.after(v, ins)
                ins = v.reciprocal(stat[:, 6:7], stat[:, 7:8])
                vsync.after(v, ins)
                ins = v.tensor_scalar(sc_t[:, :], sc_t[:, :], stat[:, 6:7],
                                      None, ALU.mult)
                vsync.after(v, ins)
                ins = v.reduce_max(gs[:, :],
                                   sc_t[:, :].rearrange("p (g k) -> p g k",
                                                        k=4),
                                   axis=AX.X)
                vsync.after(v, ins)
                ins = v.max(g8[:, :], gs[:, :])
                vsync.after(v, ins)
                ins = v.tensor_scalar(kg[:, :], gs[:, :], g8[:, LG - 1:LG],
                                      None, ALU.is_ge)
                vsync.after(v, ins)
                for jj in range(4):
                    ins = v.tensor_copy(
                        kg32[:, :].rearrange("p (g k) -> p g k", k=4)[:, :, jj],
                        kg[:, :])
                vsync.after(v, ins)
                ins = v.tensor_scalar(msk[:, :], kg32[:, :], -1.0, None,
                                      ALU.add)
                vsync.after(v, ins)
                ins = v.scalar_tensor_tensor(msk[:, :], msk[:, :], 1e8,
                                             sc_t[:, :], ALU.mult, ALU.add)
                vsync.after(v, ins)
                ins = v.max(m8[:, :], msk[:, :])
                vsync.after(v, ins)
                ins = v.max_index(eidx[:, :], m8[:, :], msk[:, :])
                vsync.after(v, ins)
                ins = v.tensor_copy(eidf[:, :], eidx[:, :])
                vsync.after(v, ins)
                v.tensor_scalar(kg32[:, :], msk[:, :], m8[:, TOPK - 1:TOPK],
                                None, ALU.is_ge).then_inc(s_rt, 1)
                ins = v.reduce_sum(rsm[:, 2:3], m8[:, 0:TOPK], axis=AX.X)
                vsync.after(v, ins)
                ins = v.tensor_scalar(rsm[:, 2:3], rsm[:, 2:3], 1e-9, None,
                                      ALU.add)
                vsync.after(v, ins)
                ins = v.reciprocal(rsm[:, 3:4], rsm[:, 2:3])
                vsync.after(v, ins)
                ring_wait(v, s_sro, i - 2)
                v.tensor_scalar(routt[:, (i % 2) * 12 + 8:(i % 2) * 12 + 12],
                                m8[:, 0:TOPK], rsm[:, 3:4], None, ALU.mult)
                v.tensor_copy(routt[:, (i % 2) * 12:(i % 2) * 12 + TOPK],
                              eidf[:, 0:TOPK])
                for j in range(TOPK):
                    v.tensor_scalar(selJ[:, j * E:(j + 1) * E], iosb[:, :],
                                    eidf[:, j:j + 1], None, ALU.is_equal)
                v.wait_ge(s_ps, i + 1)
                ins = v.tensor_copy(post[:, :], smallp[:, 32:32 + E])
                vsync.after(v, ins)
                for j in range(TOPK):
                    ins = v.tensor_tensor(tmp32[:, :],
                                          selJ[:, j * E:(j + 1) * E],
                                          post[:, :], ALU.mult)
                    vsync.after(v, ins)
                    ins = v.reduce_sum(routt[:, (i % 2) * 12 + 4 + j:
                                             (i % 2) * 12 + 4 + j + 1],
                                       tmp32[:, :], axis=AX.X)
                    vsync.after(v, ins)
                v.engine_nop().then_inc(s_ro, 1)
                v.tensor_tensor(runr[:, :], runr[:, :], smallp[0:1, 64:64 + E],
                                ALU.add).then_inc(s_rr, 1)

        @block.scalar
        def _(a):
            for i in range(NT):
                a.wait_ge(s_ln, 3 * i + 1)
                a.activation(sq2[:, :], xm2[:, :], AF.Square).then_inc(s_sc, 1)
                a.wait_ge(s_ln, 3 * i + 2)
                a.activation(stat[:, 3:4], stat[:, 2:3], AF.Ln,
                             bias=stat[:, 5:6], scale=RLN)
                asns(a)
                a.activation(stat[:, 4:5], stat[:, 3:4], AF.Exp,
                             scale=-0.5).then_inc(s_sc, 1)
                a.wait_ge(s_ln, 3 * i + 3)
                a.activation(sc_t[:, :], lg[:, :], AF.Exp,
                             bias=stat[:, 6:7]).then_inc(s_sc, 1)
                a.wait_ge(s_x3, i + 1)
                sm, per = ring_inc(s_sx3, i)
                a.dma_start(out=x3_o[i * 128:(i + 1) * 128, :],
                            in_=x3t[:, (i % 2) * D:(i % 2 + 1) * D]
                            ).then_inc(sm, per)
                a.wait_ge(s_xfb, i + 1)
                sm, per = ring_inc(s_sxf, i)
                a.dma_start(out=xf2_o[i * 128:(i + 1) * 128, :],
                            in_=xfb[:, (i % 2) * D:(i % 2 + 1) * D]
                            ).then_inc(sm, per)
                a.wait_ge(s_ro, i + 1)
                sm, per = ring_inc(s_sro, i)
                a.dma_start(out=ro2_o[i * 128:(i + 1) * 128, :],
                            in_=routt[:, (i % 2) * 12:(i % 2 + 1) * 12]
                            ).then_inc(sm, per)
    return nc



_BUILT = {}


def _programs():
    if not _BUILT:
        _BUILT["l1"] = build_l1()
        _BUILT["l2"] = build_l2()
        _BUILT["l3"] = build_l3()
        _BUILT["l4"] = build_l4()
        _BUILT["l5"] = build_l3(second=True)
    return _BUILT


def kernel(**inputs):
    P = _programs()
    cores = list(range(8))
    x0_full = inputs["L0_attn_a_x"]

    ones_bf = np.ones((128, 1), dtype=ml_dtypes.bfloat16)
    ident = np.eye(128, dtype=ml_dtypes.bfloat16)
    us = np.triu(np.ones((128, 128), np.float32), 1)
    tru = np.triu(np.ones((128, 128), np.float32))
    trub = np.triu(np.ones((128, 128))).astype(ml_dtypes.bfloat16)
    io32 = np.tile(np.arange(32, dtype=np.float32), (128, 1))
    io28 = np.arange(OV_N, dtype=np.float32).reshape(1, OV_N)
    on128 = np.ones((128, 1), np.float32)
    onr = np.ones((1, 128), np.float32)
    onrb = np.ones((1, 128), ml_dtypes.bfloat16)
    on4 = np.ones((4, 128), np.float32)
    colbase = np.zeros((128, 24), np.float32)
    for cm in range(24):
        j, k = cm // 8, cm % 8
        colbase[:, cm] = j * 32 * 1024 + k * 128 + np.arange(128)
    bo_bc = np.tile(inputs["L0_attn_a_bo"].reshape(1, -1),
                    (128, 1)).astype(np.float32, copy=False)

    xsl = [np.ascontiguousarray(x0_full[c // 4,
                                        (c % 4) * TOK:(c % 4 + 1) * TOK])
           for c in cores]
    wcat = np.concatenate([inputs["L0_ffn_w1"].reshape(E * D, D),
                           inputs["L0_ffn_w3"].reshape(E * D, D),
                           inputs["L0_ffn_w2"].reshape(E * D, D)], axis=0)

    im1 = [{"x0": xsl[c], "wq": inputs["L0_attn_a_wq"],
            "wk": inputs["L0_attn_a_wk"], "wv": inputs["L0_attn_a_wv"],
            "c_ones_bf": ones_bf} for c in cores]
    r1 = run_bass_kernel_spmd(P["l1"], im1, core_ids=cores)

    im2 = []
    for c in cores:
        grp = range((c // 4) * 4, (c // 4) * 4 + 4)
        im2.append({
            "x0": xsl[c], "qf": r1.results[c]["qf"],
            "kv4": np.ascontiguousarray(np.concatenate(
                [r1.results[j]["kvp"] for j in grp], axis=1)),
            "ks4": np.ascontiguousarray(np.concatenate(
                [r1.results[j]["ksp"] for j in grp], axis=0)),
            "wo": inputs["L0_attn_a_wo"], "bo_bc": bo_bc,
            "wg": inputs["L0_ffn_wg"],
            "wcat": np.ascontiguousarray(wcat[c * 12288:(c + 1) * 12288]),
            "c_id": ident, "c_us": us,
            "c_io32": io32, "c_on128": on128, "c_onr": onr, "c_on4": on4,
        })
    r2 = run_bass_kernel_spmd(P["l2"], im2, core_ids=cores)

    counts_all = np.ascontiguousarray(np.concatenate(
        [r2.results[c]["counts"] for c in cores], axis=0))
    wbf_full = np.ascontiguousarray(np.concatenate(
        [r2.results[c]["wbfp"] for c in cores], axis=0))
    im3 = []
    for c in cores:
        pm = np.zeros((8, 1), np.float32)
        pm[:c, 0] = 1.0
        im3.append({
            "x1": r2.results[c]["x1"], "xf": r2.results[c]["xf"],
            "rout": r2.results[c]["rout"], "counts_all": counts_all,
            "mycounts": np.ascontiguousarray(counts_all[c:c + 1]),
            "prefmask": pm, "wbf": wbf_full,
            "sw1": inputs["L0_ffn_sw1"], "sw2": inputs["L0_ffn_sw2"],
            "sw3": inputs["L0_ffn_sw3"],
            "w_attn": inputs["L0_attn_b_w_attn"],
            "c_io32": io32, "c_io28": io28, "c_onr": onr,
            "c_colbase": colbase, "c_ones_bf": ones_bf,
        })
    r3 = run_bass_kernel_spmd(P["l3"], im3, core_ids=cores)

    z64 = np.zeros((64, D), np.float32)
    z1 = np.zeros((1, D), np.float32)
    im4 = []
    for c in cores:
        b, q = c // 4, c % 4
        preds = [b * 4 + qq for qq in range(q)]
        sps = [r3.results[j]["Sp"] for j in preds] + [z64] * (3 - len(preds))
        zps = [r3.results[j]["zp"] for j in preds] + [z1] * (3 - len(preds))
        im4.append({
            "x2": r3.results[c]["x2"], "qb": r3.results[c]["qb"],
            "kb": r3.results[c]["kb"], "vb": r3.results[c]["vb"],
            "sp3": np.ascontiguousarray(np.concatenate(sps, axis=1)),
            "zp3": np.ascontiguousarray(np.concatenate(zps, axis=1)),
            "wp": inputs["L0_attn_b_w_proj"], "wg": inputs["L0_ffn_wg"],
            "c_id": ident, "c_tru": tru, "c_trub": trub, "c_onrb": onrb,
            "c_us": us, "c_io32": io32, "c_on128": on128, "c_onr": onr,
            "c_onb": ones_bf,
        })
    r4 = run_bass_kernel_spmd(P["l4"], im4, core_ids=cores)

    counts2 = np.ascontiguousarray(np.concatenate(
        [r4.results[c]["counts2"] for c in cores], axis=0))
    im5 = []
    for c in cores:
        pm = np.zeros((8, 1), np.float32)
        pm[:c, 0] = 1.0
        im5.append({
            "x1": r4.results[c]["x3"], "xf": r4.results[c]["xf2"],
            "rout": r4.results[c]["rout2"], "counts_all": counts2,
            "mycounts": np.ascontiguousarray(counts2[c:c + 1]),
            "prefmask": pm, "wbf": wbf_full,
            "sw1": inputs["L0_ffn_sw1"], "sw2": inputs["L0_ffn_sw2"],
            "sw3": inputs["L0_ffn_sw3"],
            "c_io32": io32, "c_io28": io28, "c_onr": onr,
            "c_colbase": colbase, "c_ones_bf": ones_bf,
        })
    r5 = run_bass_kernel_spmd(P["l5"], im5, core_ids=cores)

    out = np.empty((2, 4 * TOK, D), np.float32)
    for c in cores:
        out[c // 4, (c % 4) * TOK:(c % 4 + 1) * TOK] = r5.results[c]["x2"]
    return out
